# version 32
# baseline (speedup 1.0000x reference)
import contextlib
import numpy as np
import concourse.bass as bass
import concourse.mybir as mybir
from concourse.bass_utils import run_bass_kernel_spmd

F32 = mybir.dt.float32
BF16 = mybir.dt.bfloat16
I32 = mybir.dt.int32
AF = mybir.ActivationFunctionType
ALU = mybir.AluOpType

P = 128
W_WARM = 128
T_MAIN = 2048
T_TOT = W_WARM + T_MAIN
CHUNKS = [(0, 128), (128, 352), (480, 352), (832, 352), (1184, 352), (1536, 352), (1888, 288)]
BW = 352
D_MODEL = 1024
D_INNER = 2048
NFT = D_MODEL // P
NDT = D_INNER // P
NS = 16
EPS = 1e-5
N_CORES = 8

_DBG = False
_PARANOID = False
ENGINES = ("sync", "tensor", "vector", "scalar", "gpsimd")
WQ = ("sync", "scalar")


def subchunks(w):
    if w == 128:
        return [128]
    if w == 352:
        return [88, 88, 88, 88]
    if w == 288:
        return [96, 96, 96]
    raise ValueError(w)


class Sched:
    NQL = 8

    def __init__(self, nc, es):
        self.sem = {e: es.enter_context(nc.semaphore("s_" + e)) for e in ENGINES}
        self.qsem = {e: [es.enter_context(nc.semaphore(f"q_{e}{j}")) for j in range(self.NQL)]
                     for e in ("sync", "scalar", "gpsimd")}
        self.n = {e: 0 for e in ENGINES}
        self.q = {e: 0 for e in ENGINES}
        self.ops = {e: [] for e in ENGINES}
        self.last_wait = {e: {} for e in ENGINES}
        self.small_toks = set()
        self._wq_i = 0

    SELF_MARGIN = 8

    def _resolve(self, eng, deps):
        small_toks = self.small_toks
        need = {}
        for d in deps:
            if d is None:
                continue
            kind, e, k = d
            if kind == "c":
                if e == eng:
                    if k <= self.n[eng] - self.SELF_MARGIN or (e not in ("vector", "gpsimd") and (e, k) not in small_toks):
                        continue
                    key, sem, val = ("s", e), self.sem[e], k
                else:
                    key, sem, val = ("s", e), self.sem[e], k
            else:
                lane = (k - 1) % self.NQL
                key = ("q", e, lane)
                sem = self.qsem[e][lane]
                val = 16 * ((k - 1) // self.NQL + 1)
            cur = need.get(key)
            if cur is None or val > cur[1]:
                need[key] = (sem, val)
        waits = []
        for key, (sem, val) in need.items():
            if self.last_wait[eng].get(key, -1) >= val:
                continue
            self.last_wait[eng][key] = val
            waits.append((sem, val))
        return waits

    mark_small = True

    def do(self, eng, fn, deps=(), small=None):
        waits = self._resolve(eng, deps)
        self.ops[eng].append(("c", fn, waits))
        self.n[eng] += 1
        tok = ("c", eng, self.n[eng])
        if small is None:
            small = self.mark_small
        if small:
            self.small_toks.add((eng, self.n[eng]))
        return tok

    def dma(self, eng, out, in_, deps=()):
        waits = self._resolve(eng, deps)
        lane = self.q[eng] % self.NQL
        self.ops[eng].append(("d", (out, in_, lane), waits))
        self.q[eng] += 1
        return ("d", eng, self.q[eng])

    def wdma(self, out, in_, deps=()):
        e = WQ[self._wq_i % len(WQ)]
        self._wq_i += 1
        return self.dma(e, out, in_, deps)

    def barrier(self, engines=("tensor", "vector", "scalar", "gpsimd")):
        snap = tuple(("c", e, self.n[e]) for e in engines) + tuple(
            ("d", e, k) for e in ("sync", "scalar", "gpsimd") for k in range(max(1, self.q[e] - self.NQL + 1), self.q[e] + 1))
        return tuple(self.do(e, lambda x: x.nop(), snap) for e in engines)

    def minibar(self):
        if _PARANOID:
            self.barrier(("tensor", "vector", "scalar", "gpsimd"))

    def replay(self, blk):
        for e in ENGINES:
            mylist, mysem = self.ops[e], self.sem[e]
            myq = self.qsem.get(e)

            def make(section_list, sem, qs):
                def body(engine):
                    for kind, payload, waits in section_list:
                        for s, v in waits:
                            engine.wait_ge(s, v)
                        if kind == "c":
                            payload(engine).then_inc(sem, 1)
                        else:
                            out, in_, lane = payload
                            engine.dma_start(out=out, in_=in_).then_inc(qs[lane], 16)
                return body
            getattr(blk, e)(make(mylist, mysem, myq))


class Rot:
    def __init__(self, bufs):
        self.bufs = bufs
        self.cons = [[] for _ in bufs]
        self.i = 0

    def get(self):
        j = self.i % len(self.bufs)
        self.i += 1
        deps = tuple(self.cons[j])
        self.cons[j] = []
        return j, self.bufs[j], deps

    def used(self, j, tok):
        self.cons[j].append(tok)


def build_nc():
    nc = bass.Bass("TRN2", target_bir_lowering=False)
    es = contextlib.ExitStack()

    DT = nc.dram_tensor
    d_x = DT("x_in", [D_MODEL, T_TOT], F32, kind="ExternalInput").ap()
    d_warm = DT("warm", [1, 1], F32, kind="ExternalInput").ap()
    d_llr = DT("log_lam_real", [1, 64], F32, kind="ExternalInput").ap()
    d_lim = DT("lam_imag", [1, 64], F32, kind="ExternalInput").ap()
    d_brT = DT("b_realT", [D_MODEL, 64], F32, kind="ExternalInput").ap()
    d_biT = DT("b_imagT", [D_MODEL, 64], F32, kind="ExternalInput").ap()
    d_crT = DT("c_realT", [64, D_MODEL], F32, kind="ExternalInput").ap()
    d_ciT = DT("c_imagT", [64, D_MODEL], F32, kind="ExternalInput").ap()
    d_aD = DT("a_D", [1, D_MODEL], F32, kind="ExternalInput").ap()
    d_ldt = DT("log_dt", [1, 1], F32, kind="ExternalInput").ap()
    d_woT = DT("w_outT", [D_MODEL, D_MODEL], F32, kind="ExternalInput").ap()
    d_bout = DT("b_out", [P, NFT], F32, kind="ExternalInput").ap()
    d_inw = DT("in_proj_w", [D_MODEL, 2 * D_INNER], F32, kind="ExternalInput").ap()
    d_cw = DT("conv_w", [P, 64], F32, kind="ExternalInput").ap()
    d_cb = DT("conv_b", [P, NDT], F32, kind="ExternalInput").ap()
    d_xpw = DT("x_proj_w", [P, NDT * 33], F32, kind="ExternalInput").ap()
    d_dtw = DT("dt_w", [P, NDT], F32, kind="ExternalInput").ap()
    d_dtb = DT("dt_b", [P, NDT], F32, kind="ExternalInput").ap()
    d_alog = DT("a_log", [P, NDT * NS], F32, kind="ExternalInput").ap()
    d_mD = DT("m_D", [P, NDT], F32, kind="ExternalInput").ap()
    d_opw = DT("out_proj_w", [D_INNER, D_MODEL], F32, kind="ExternalInput").ap()
    d_g1r = DT("g1r", [32, 128], F32, kind="ExternalInput").ap()
    d_g2r = DT("g2r", [64, 16], F32, kind="ExternalInput").ap()
    d_g1i = DT("g1i", [32, 128], F32, kind="ExternalInput").ap()
    d_g2i = DT("g2i", [64, 16], F32, kind="ExternalInput").ap()
    d_fb = DT("ffn_bias", [P, NFT], F32, kind="ExternalInput").ap()
    d_out = DT("out", [D_MODEL, T_MAIN], F32, kind="ExternalOutput").ap()
    if _DBG:
        d_dbg = [DT(f"dbg{j}", [D_MODEL, T_TOT], F32, kind="ExternalOutput").ap() for j in range(3)]
        d_dbgs = DT("dbgs", [64, 2 * (BW + 1)], F32, kind="ExternalOutput").ap()
        d_dbgb = DT("dbgb", [64, 2 * BW], F32, kind="ExternalOutput").ap()
        d_dbgy = DT("dbgy", [P, NFT * BW], BF16, kind="ExternalOutput").ap()
        d_dbglp = DT("dbglp", [64, 258], F32, kind="ExternalOutput").ap()
        d_dbgln = DT("dbgln", [64, 258], F32, kind="ExternalOutput").ap()
        d_dbglsc = DT("dbglsc", [64, 16], F32, kind="ExternalOutput").ap()
        d_dbglrow = DT("dbglrow", [1, 768], F32, kind="ExternalOutput").ap()
        d_dbgfw = DT("dbgfw", [P, 3 * 2048], BF16, kind="ExternalOutput").ap()
        d_dbgg1t = DT("dbgg1t", [P, 32], F32, kind="ExternalOutput").ap()
        d_dbgxc = DT("dbgxc", [P, NDT * BW], BF16, kind="ExternalOutput").ap()
        d_dbgsz = DT("dbgsz", [P, NDT * BW], BF16, kind="ExternalOutput").ap()
        d_dbgy2 = DT("dbgy2", [P, NDT * BW], BF16, kind="ExternalOutput").ap()
        d_dbgrw = DT("dbgrw", [33, BW], BF16, kind="ExternalOutput").ap()
        d_dbgst = DT("dbgst", [P, NDT * NS], BF16, kind="ExternalOutput").ap()
        d_dbga = DT("dbga", [P, NDT * NS], F32, kind="ExternalOutput").ap()
        d_dbgdt = DT("dbgdt", [P, NDT], F32, kind="ExternalOutput").ap()

    d_winp = DT("scr_inproj", [NFT, 32, P, P], BF16).ap()
    d_wop = DT("scr_outproj", [NDT, NFT, P, P], BF16).ap()
    d_wwo = DT("scr_wout", [NFT, NFT, P, P], BF16).ap()
    d_wcv = DT("scr_conv", [NDT, 4, P, P], BF16).ap()
    d_rows = DT("scr_rows", [1, 512], F32).ap()
    d_rows16 = DT("scr_rows16", [33, 512], BF16).ap()

    A = nc.alloc_sbuf_tensor
    x_sb = [A(f"x{i}", [P, T_TOT], F32) for i in range(NFT)]

    def mk(name, ntiles, dtype=BF16, stride=BW):
        t = A(name, [P, ntiles * stride], dtype)
        return t, stride

    xn_t, _ = mk("xn", NFT)
    xn = lambda i: xn_t[:, i * BW:(i + 1) * BW]
    xsq = A("xsq", [P, BW], BF16)
    m_bc = A("m_bc", [P, BW], BF16)
    r_bc = A("r_bc", [P, BW], BF16)
    dtraw_bc = A("dtraw_bc", [P, BW], BF16)

    bst_t = A("bst", [P, NFT * P], BF16)
    bst = lambda f: bst_t[:, f * P:(f + 1) * P]
    cctR = A("cctR", [64, D_MODEL], BF16)
    cctIN = A("cctIN", [64, D_MODEL], BF16)
    addiag = A("addiag", [P, NFT * P], BF16)
    bu = A("bu", [64, 2 * BW], F32)
    s_sb = A("s_sb", [64, 2 * (BW + 1)], F32)
    sb16 = A("sb16", [64, 2 * BW], BF16)
    ya_t, _ = mk("ya", NFT)
    ya = lambda i: ya_t[:, i * BW:(i + 1) * BW]
    lamP = A("lamP", [64, 258], F32)
    lamN = A("lamN", [64, 258], F32)
    lam_sc = A("lam_sc", [64, 16], F32)
    pow_scr = A("pow_scr", [64, 256], F32)
    onesf = A("onesf", [P, 128], F32)

    fw_t = A("fw", [P, 3 * 2048], BF16)
    fwr = lambda k: fw_t[:, k * 512:(k + 1) * 512]
    fwi = lambda k: fw_t[:, 2048 + k * 512:2048 + (k + 1) * 512]
    fwin = lambda k: fw_t[:, 4096 + k * 512:4096 + (k + 1) * 512]

    xpw_t = A("xpw", [P, NDT * 33], BF16)
    xpw = lambda d: xpw_t[:, d * 33:(d + 1) * 33]
    xi_ext = [A(f"xiext{i}", [P, BW + 3], BF16) for i in range(2)]
    tails = A("tails", [P, 3 * NDT], BF16)
    xc_t, _ = mk("xc", NDT)
    xc = lambda d: xc_t[:, d * BW:(d + 1) * BW]
    sz_t, _ = mk("sz", NDT)
    sz = lambda d: sz_t[:, d * BW:(d + 1) * BW]
    y2_t, _ = mk("y2", NDT)
    y2 = lambda d: y2_t[:, d * BW:(d + 1) * BW]
    bbc_t, _ = mk("bbc", NS)
    bbc = lambda s: bbc_t[:, s * BW:(s + 1) * BW]
    cbc_t, _ = mk("cbc", NS)
    cbc = lambda s: cbc_t[:, s * BW:(s + 1) * BW]
    hpool = A("hpool", [P, 4 * BW], BF16)
    states = A("states", [P, NDT * NS], BF16)
    dt_b2 = [A(f"dtt{i}", [P, BW], BF16) for i in range(2)]
    dtl_t = A("dtl_t", [P, BW], BF16)
    dtx_b2 = [A(f"dtx{i}", [P, BW], BF16) for i in range(2)]
    da_b = [A(f"dab{i}", [P, BW], BF16) for i in range(3)]
    u_b = [A(f"ub{i}", [P, BW], BF16) for i in range(2)]
    g_b = [A(f"gb{i}", [P, BW], BF16) for i in range(3)]
    rows16 = A("rows16", [33, BW], BF16)
    flat_db = [A(f"flatdb{i}", [1, BW], BF16) for i in range(2)]
    dtw_sb = A("dtw_sb", [P, NDT], F32)
    dtb_sb = A("dtb_sb", [P, NDT], F32)
    a_sb = A("a_sb", [P, NDT * NS], F32)
    mD_sb = A("mD_sb", [P, NDT], F32)
    cb_sb = A("cb_sb", [P, NDT], F32)
    bout_sb = A("bout_sb", [P, NFT], F32)
    fb_sb = A("fb_sb", [P, NFT], F32)
    wm_col = A("wm_col", [P, 1], F32)

    st_inp = Rot([A(f"stinp{i}", [P, P], BF16) for i in range(4)])
    st_op = Rot([A(f"stop{i}", [P, P], BF16) for i in range(4)])
    st_wo = Rot([A(f"stwo{i}", [P, P], BF16) for i in range(3)])
    st_cv = Rot([A(f"stcv{i}", [P, P], BF16) for i in range(4)])
    cast_st = A("cast_st", [P, 512], F32)
    cast_st2 = A("cast_st2", [P, 512], BF16)

    consts = A("consts", [P, 12], F32)
    ones_c = A("ones_c", [1, P], BF16)
    onescol = A("onescol", [P, 1], BF16)
    ident = A("ident", [P, P], BF16)
    diagmask = A("diagmask", [P, P], I32)
    lrow = A("lrow", [1, 768], F32)
    dcol = A("dcol", [P, 2], F32)
    lrow_i = A("lrow_i", [1, 64], I32)
    stat_r = A("stat_r", [1, 2 * BW], F32)
    rowsc = A("rowsc", [1, BW], F32)
    rowb16 = A("rowb16", [1, BW], BF16)
    rowb16b = A("rowb16b", [1, BW], BF16)
    g1_sb = A("g1_sb", [32, 128], F32)
    g2_sb = A("g2_sb", [64, 16], F32)
    g1t_sb = A("g1t_sb", [P, 32], F32)
    selA = A("selA", [8, P], BF16)
    selB = A("selB", [16, P], BF16)
    gsl = A("gsl", [16, 32], F32)
    gsl2 = A("gsl2", [16, 16], F32)
    gslb = A("gslb", [16, 32], BF16)
    gslb2 = A("gslb2", [16, 16], BF16)
    a1ev = A("a1ev", [P, 32], F32)
    a2ev = A("a2ev", [P, 16], F32)
    ftmp = A("ftmp", [P, 512], F32)
    facc = A("facc", [P, 512], F32)
    iota1 = ftmp.ap()[:, 0:P].bitcast(I32)
    iota2 = facc.ap()[:, 0:P].bitcast(I32)

    PT = nc.alloc_psum_tensor
    mmA = PT("mmA", [P, 512], F32)
    mmB = PT("mmB", [P, 512], F32)
    yA = PT("yA", [P, 512], F32)
    yB = PT("yB", [P, 512], F32)
    psS = PT("psS", [P, 512], F32)
    psT = PT("psT", [P, 512], F32)
    psU = PT("psU", [P, 512], F32)
    psV = PT("psV", [P, 512], F32)
    ps_cons = {id(p): [] for p in (mmA, mmB, yA, yB, psS, psT, psU, psV)}

    S = Sched(nc, es)

    def TS(out, in0, s1, op):
        return lambda e: e.tensor_scalar(out, in0, s1, None, op)

    def TS2(out, in0, s1, s2, op0, op1):
        return lambda e: e.tensor_scalar(out, in0, s1, s2, op0, op1)

    def TT(out, a, b, op):
        return lambda e: e.tensor_tensor(out, a, b, op)

    def STT(out, in0, sc, in1, op0, op1):
        return lambda e: e.scalar_tensor_tensor(out, in0, sc, in1, op0, op1)

    def CP(out, in_):
        return lambda e: e.tensor_copy(out, in_)

    def ACT(out, in_, f, bias=0.0, scale=1.0):
        return lambda e: e.activation(out, in_, f, bias=bias, scale=scale)

    def MM(out, lhsT, rhs, start, stop):
        return lambda e: e.matmul(out, lhsT, rhs, start=start, stop=stop)

    def LDW(w):
        return lambda e: e.ldweights(w)

    def MSET(ap, v):
        return lambda e: e.memset(ap, v)

    def SCAN(out, d0, d1, init):
        return lambda e: e.tensor_tensor_scan(out, d0, d1, init, ALU.mult, ALU.add)

    def ps_use(ps, deps=()):
        d = tuple(ps_cons[id(ps)]) + tuple(deps)
        ps_cons[id(ps)] = []
        return d

    def ps_done(ps, tok):
        ps_cons[id(ps)].append(tok)

    cvals = [-1.0, EPS, 1.0 / D_MODEL, np.pi / 2, 2 * np.pi, np.pi, -0.5, -6.0, 6.0, -2.0, 1.0 / (2 * np.pi), 0.0]
    tk_c = None
    for j, v in enumerate(cvals):
        tk_c = S.do("vector", MSET(consts[:, j:j + 1], float(v)))
    C_NEG1, C_EPS, C_INV, C_HPI, C_2PI, C_PI, C_NHALF, C_N6, C_P6, C_N2, C_I2PI = range(11)
    tk_ones = S.do("vector", MSET(ones_c[:, :], 1.0))
    tk_oc = S.do("vector", MSET(onescol[:, :], 1.0))
    tk_of = S.do("vector", MSET(onesf[:, :], 1.0))

    tk_x = tuple(S.dma("gpsimd", x_sb[i][:, :], d_x[i * P:(i + 1) * P, :]) for i in range(NFT))

    t_i1 = S.do("gpsimd", lambda e: e.iota(iota1, [[1, P]], channel_multiplier=0))
    t_i2 = S.do("gpsimd", lambda e: e.iota(iota2, [[0, P]], channel_multiplier=1))
    t_msk = S.do("vector", TT(diagmask[:, :], iota1, iota2, ALU.is_equal), (t_i1, t_i2))
    t_id0 = S.do("vector", MSET(ident[:, :], 0.0))
    t_id = S.do("vector", lambda e: e.copy_predicated(
        ident[:, :], diagmask[:, :], onescol[:, 0:1].broadcast_to([P, P])), (t_msk, t_id0, tk_oc))

    tk_wm = S.dma("sync", wm_col[:, :], d_warm.partition_broadcast(P))
    tk_dtw = S.dma("sync", dtw_sb[:, :], d_dtw)
    tk_dtb = S.dma("sync", dtb_sb[:, :], d_dtb)
    tk_mD = S.dma("sync", mD_sb[:, :], d_mD)
    tk_cb = S.dma("sync", cb_sb[:, :], d_cb)
    tk_bo = S.dma("sync", bout_sb[:, :], d_bout)
    tk_fb = S.dma("sync", fb_sb[:, :], d_fb)
    tk_alog = S.dma("sync", facc[:, 0:256], d_alog)
    t_a = S.do("vector", TS(facc[:, 256:512], facc[:, 0:256], consts[:, C_P6:C_P6 + 1], ALU.min), (tk_alog, tk_c))
    t_a = S.do("vector", TS(a_sb[:, :], facc[:, 256:512], consts[:, C_N6:C_N6 + 1], ALU.max), (t_a,))
    t_a = S.do("scalar", ACT(facc[:, 0:256], a_sb[:, :], AF.Exp), (t_a,))
    tk_a = S.do("vector", TS(a_sb[:, :], facc[:, 0:256], consts[:, C_NEG1:C_NEG1 + 1], ALU.mult), (t_a,))

    tk_xpl = S.dma("sync", cast_st[:, 0:512], d_xpw[:, 0:512])
    tk_xp0 = S.do("vector", CP(xpw_t[:, 0:512], cast_st[:, 0:512]), (tk_xpl,))
    tk_xpl2 = S.dma("sync", cast_st[:, 0:16], d_xpw[:, 512:528], deps=(tk_xp0,))
    tk_xp = S.do("vector", CP(xpw_t[:, 512:528], cast_st[:, 0:16]), (tk_xpl2, tk_xp0))

    tl1 = S.dma("sync", lrow[0:1, 0:64], d_llr)
    tl2 = S.dma("sync", lrow[0:1, 64:128], d_lim)
    tl3 = S.dma("sync", lrow[0:1, 128:129], d_ldt)
    p_dt = S.do("scalar", ACT(lrow[0:1, 129:130], lrow[0:1, 128:129], AF.Exp), (tl3,))
    t_dt = p_dt
    p_e1 = S.do("scalar", ACT(lrow[0:1, 192:256], lrow[0:1, 0:64], AF.Exp), (tl1,))
    p_m1 = S.do("vector", TS(lrow[0:1, 256:320], lrow[0:1, 192:256], lrow[0:1, 129:130], ALU.mult), (p_e1, p_dt))
    p_lrdt = S.do("vector", TS(lrow[0:1, 320:384], lrow[0:1, 256:320], consts[0:1, C_NEG1:C_NEG1 + 1], ALU.mult), (p_m1, tk_c))
    p_mag = S.do("scalar", ACT(lrow[0:1, 384:448], lrow[0:1, 320:384], AF.Exp), (p_lrdt,))
    p_th0 = S.do("vector", TS(lrow[0:1, 448:512], lrow[0:1, 64:128], lrow[0:1, 129:130], ALU.mult), (tl2, p_dt))
    p_q = S.do("vector", TS(lrow[0:1, 512:576], lrow[0:1, 448:512], consts[0:1, C_I2PI:C_I2PI + 1], ALU.mult), (p_th0,))
    p_qi = S.do("vector", CP(lrow_i[0:1, :], lrow[0:1, 512:576]), (p_q,))
    p_qf = S.do("vector", CP(lrow[0:1, 576:640], lrow_i[0:1, :]), (p_qi,))
    p_q2 = S.do("vector", TS(lrow[0:1, 640:704], lrow[0:1, 576:640], consts[0:1, C_2PI:C_2PI + 1], ALU.mult), (p_qf,))
    p_th = S.do("vector", TT(lrow[0:1, 704:768], lrow[0:1, 448:512], lrow[0:1, 640:704], ALU.subtract), (p_q2, p_th0))
    p_gt = S.do("vector", lambda e: e.tensor_scalar(stat_r[0:1, 448:512], lrow[0:1, 704:768],
                                                    consts[0:1, C_PI:C_PI + 1], None, ALU.is_gt), (p_th,))
    p_g2 = S.do("vector", TS(stat_r[0:1, 512:576], stat_r[0:1, 448:512], consts[0:1, C_2PI:C_2PI + 1], ALU.mult), (p_gt,))
    p_thr = S.do("vector", TT(lrow[0:1, 0:64], lrow[0:1, 704:768], stat_r[0:1, 512:576], ALU.subtract), (p_g2, p_th, p_e1))
    p_sn = S.do("scalar", ACT(lrow[0:1, 64:128], lrow[0:1, 0:64], AF.Sin), (p_thr, p_th0))
    p_ab = S.do("scalar", ACT(stat_r[0:1, 0:64], lrow[0:1, 0:64], AF.Abs), (p_thr,))
    p_pm = S.do("vector", TS2(stat_r[0:1, 64:128], stat_r[0:1, 0:64], consts[0:1, C_NEG1:C_NEG1 + 1],
                              consts[0:1, C_HPI:C_HPI + 1], ALU.mult, ALU.add), (p_ab,))
    p_cs = S.do("scalar", ACT(stat_r[0:1, 128:192], stat_r[0:1, 64:128], AF.Sin), (p_pm,))
    p_l1r = S.do("vector", TT(rowsc[0:1, 0:64], stat_r[0:1, 128:192], lrow[0:1, 384:448], ALU.mult), (p_cs, p_mag))
    p_l1i = S.do("vector", TT(rowsc[0:1, 64:128], lrow[0:1, 64:128], lrow[0:1, 384:448], ALU.mult), (p_sn, p_mag))
    p_im2 = S.do("scalar", lambda e: e.activation(stat_r[0:1, 192:256], lrow[0:1, 320:384], AF.Exp,
                                                  scale=consts[0:1, C_N2:C_N2 + 1]), (p_lrdt,))
    p_lm1r = S.do("vector", TT(stat_r[0:1, 256:320], rowsc[0:1, 0:64], stat_r[0:1, 192:256], ALU.mult), (p_im2, p_l1r))
    p_m3 = S.do("vector", TT(stat_r[0:1, 320:384], rowsc[0:1, 64:128], stat_r[0:1, 192:256], ALU.mult), (p_im2, p_l1i))
    p_lm1i = S.do("vector", TS(stat_r[0:1, 384:448], stat_r[0:1, 320:384], consts[0:1, C_NEG1:C_NEG1 + 1], ALU.mult), (p_m3,))
    tkr1 = S.dma("gpsimd", d_rows[0:1, 0:64], rowsc[0:1, 0:64], deps=(p_l1r,))
    tkr2 = S.dma("gpsimd", d_rows[0:1, 64:128], rowsc[0:1, 64:128], deps=(p_l1i,))
    tkr3 = S.dma("gpsimd", d_rows[0:1, 128:192], stat_r[0:1, 256:320], deps=(p_lm1r,))
    tkr4 = S.dma("gpsimd", d_rows[0:1, 192:256], stat_r[0:1, 384:448], deps=(p_lm1i,))
    tc1 = S.dma("sync", lam_sc[0:64, 0:1], d_rows[0:1, 0:64].transpose([1, 0]), deps=(tkr1,))
    tc2 = S.dma("sync", lam_sc[0:64, 1:2], d_rows[0:1, 64:128].transpose([1, 0]), deps=(tkr2,))
    tc3 = S.dma("sync", lam_sc[0:64, 2:3], d_rows[0:1, 128:192].transpose([1, 0]), deps=(tkr3,))
    tc4 = S.dma("sync", lam_sc[0:64, 3:4], d_rows[0:1, 192:256].transpose([1, 0]), deps=(tkr4,))

    def cplx_scale(dst_r, dst_i, src_r, src_i, sc_r, sc_i, deps):
        n = dst_r.shape[-1]
        t1 = S.do("vector", TS(pow_scr[0:64, 0:n], src_i, sc_i, ALU.mult), deps)
        t2 = S.do("vector", TS(dst_r, src_r, sc_r, ALU.mult), (t1,))
        t3 = S.do("vector", TT(dst_r, dst_r, pow_scr[0:64, 0:n], ALU.subtract), (t2, t1))
        t4 = S.do("vector", TS(pow_scr[0:64, 0:n], src_r, sc_i, ALU.mult), (t3,))
        t5 = S.do("vector", TS(dst_i, src_i, sc_r, ALU.mult), (t4,))
        t6 = S.do("vector", TT(dst_i, dst_i, pow_scr[0:64, 0:n], ALU.add), (t5, t4))
        return t6

    IM = 129

    def build_pow(tab, sc0, deps):
        t0 = S.do("vector", MSET(tab[0:64, 0:1], 1.0), deps)
        t1 = S.do("vector", MSET(tab[0:64, IM:IM + 1], 0.0), deps)
        t2 = S.do("vector", CP(lam_sc[0:64, 4:6], lam_sc[0:64, sc0:sc0 + 2]), deps)
        last = S.do("vector", lambda e: e.nop(), (t0, t1, t2))
        width = 1
        while width < 129:
            n = min(width, 129 - width)
            last = cplx_scale(tab[0:64, width:width + n], tab[0:64, IM + width:IM + width + n],
                              tab[0:64, 0:n], tab[0:64, IM:IM + n],
                              lam_sc[0:64, 4:5], lam_sc[0:64, 5:6], (last,))
            if width * 2 < 129:
                last = cplx_scale(lam_sc[0:64, 6:7], lam_sc[0:64, 7:8],
                                  lam_sc[0:64, 4:5], lam_sc[0:64, 5:6],
                                  lam_sc[0:64, 4:5], lam_sc[0:64, 5:6], (last,))
                last = S.do("vector", CP(lam_sc[0:64, 4:6], lam_sc[0:64, 6:8]), (last,))
            width *= 2
        return last

    tk_lp = build_pow(lamP, 0, (tc1, tc2))
    tk_ln = build_pow(lamN, 2, (tc3, tc4, tk_lp))
    if _DBG:
        S.dma("gpsimd", d_dbglp, lamP[0:64, :], deps=(tk_lp, tk_ln))
        S.dma("gpsimd", d_dbgln, lamN[0:64, :], deps=(tk_ln,))
        S.dma("gpsimd", d_dbglsc, lam_sc[0:64, :], deps=(tk_ln,))
        S.dma("gpsimd", d_dbglrow, lrow[0:1, :], deps=(tk_ln,))

    tk_dc0 = S.dma("gpsimd", d_rows[0:1, 256:257], lrow[0:1, 129:130], deps=(t_dt,))
    tk_dc = S.dma("sync", dcol[:, 0:1], d_rows[0:1, 256:257].partition_broadcast(P), deps=(tk_dc0,))
    tk_bst = []
    prev = tk_xp
    for f in range(NFT):
        t1 = S.dma("sync", cast_st[:, 0:64], d_brT[f * P:(f + 1) * P, :], deps=(prev,))
        t2 = S.dma("sync", cast_st[:, 64:128], d_biT[f * P:(f + 1) * P, :], deps=(t1,))
        t3 = S.do("vector", TS(cast_st[:, 0:128], cast_st[:, 0:128], dcol[:, 0:1], ALU.mult), (t1, t2, tk_dc))
        prev = S.do("vector", CP(bst(f), cast_st[:, 0:128]), (t3,))
        tk_bst.append(prev)
    tk_cct = prev
    for half in range(2):
        t1 = S.dma("sync", cast_st[0:64, 0:512], d_crT[:, half * 512:(half + 1) * 512], deps=(tk_cct,))
        tk_cct = S.do("vector", CP(cctR[:, half * 512:(half + 1) * 512], cast_st[0:64, 0:512]), (t1,))
        t2 = S.dma("sync", cast_st[0:64, 0:512], d_ciT[:, half * 512:(half + 1) * 512], deps=(tk_cct,))
        t3 = S.do("vector", TS(cast_st[0:64, 0:512], cast_st[0:64, 0:512], consts[0:64, C_NEG1:C_NEG1 + 1], ALU.mult), (t2,))
        tk_cct = S.do("vector", CP(cctIN[:, half * 512:(half + 1) * 512], cast_st[0:64, 0:512]), (t3,))
    tk_ad = []
    prevz = None
    tk_adl = None
    for f in range(NFT):
        if f % 4 == 0:
            tk_adl = S.dma("sync", cast_st[0:1, 0:512], d_aD[0:1, (f // 4) * 512:(f // 4 + 1) * 512],
                           deps=(tk_cct,) + ((prevz,) if prevz else ()))
        zz = S.wdma(d_rows[0:1, 0:128], cast_st[0:1, (f % 4) * P:(f % 4 + 1) * P],
                   deps=(tk_adl,) + ((prevz,) if prevz else ()))
        zc = S.dma("sync", dcol[:, 1:2], d_rows[0:1, 0:128].transpose([1, 0]), deps=(zz,))
        z1 = S.do("vector", MSET(ftmp[:, 0:128], 0.0), (zc,))
        z2 = S.do("vector", lambda e: e.copy_predicated(
            ftmp[:, 0:128], diagmask[:, :], dcol[:, 1:2].broadcast_to([P, P])), (z1, t_msk))
        tok = S.do("vector", CP(addiag[:, f * P:(f + 1) * P], ftmp[:, 0:128]), (z2,))
        tk_ad.append(tok)
        prevz = tok

    t_sa1 = S.do("gpsimd", lambda e: e.iota(iota1[0:8, :], [[0, P]], channel_multiplier=1), (t_id, tk_ad[-1]))
    t_sa2 = S.do("gpsimd", lambda e: e.iota(iota2[0:8, :], [[1, 8], [0, 16]], channel_multiplier=0), (tk_ad[-1],))
    t_sa3 = S.do("vector", TT(iota1[0:8, :], iota1[0:8, :], iota2[0:8, :], ALU.is_equal), (t_sa1, t_sa2))
    t_z = S.do("vector", MSET(selA[:, :], 0.0))
    t_sa = S.do("vector", lambda e: e.copy_predicated(selA[:, :], iota1[0:8, :],
                                                      onescol[0:8, 0:1].broadcast_to([8, P])), (t_sa3, t_z, tk_oc))
    t_sb1 = S.do("gpsimd", lambda e: e.iota(iota1[0:16, :], [[0, P]], channel_multiplier=1), (t_sa,))
    t_sb2 = S.do("gpsimd", lambda e: e.iota(iota2[0:16, :], [[0, 8], [1, 16]], channel_multiplier=0), (t_sa,))
    t_sb3 = S.do("vector", TT(iota1[0:16, :], iota1[0:16, :], iota2[0:16, :], ALU.is_equal), (t_sb1, t_sb2))
    t_z = S.do("vector", MSET(selB[:, :], 0.0))
    t_sb = S.do("vector", lambda e: e.copy_predicated(selB[:, :], iota1[0:16, :],
                                                      onescol[0:16, 0:1].broadcast_to([16, P])), (t_sb3, t_z, tk_oc))

    def build_ffn_w(dst_tiles, dg1, dg2, deps):
        l1 = S.dma("sync", g1_sb[:, :], dg1, deps=deps)
        l2 = S.dma("sync", g2_sb[:, :], dg2, deps=deps)
        tt = []
        for b in range(4):
            tt.append(S.do("vector", lambda e, b_=b: e.transpose(
                g1t_sb[b_ * 32:(b_ + 1) * 32, 0:32], g1_sb[0:32, b_ * 32:(b_ + 1) * 32]), (l1,)))
        last = None
        w2 = w4 = m1 = m2 = None
        for tile in range(4):
            accum_last = None
            for c in range(4):
                base = c * 32 + tile * 8
                c1 = S.wdma(gsl[0:8, :], g1t_sb[base:base + 8, 0:32], deps=tuple(tt) + ((m1,) if m1 else ()))
                c2 = S.wdma(gsl2[0:16, :], g2_sb[c * 16:(c + 1) * 16, :], deps=(l2,) + ((m2,) if m2 else ()))
                m1 = S.do("vector", CP(gslb[0:8, :], gsl[0:8, :]), (c1,) + ((w2,) if w2 else ()))
                m2 = S.do("vector", CP(gslb2[0:16, :], gsl2[0:16, :]), (c2,) + ((w4,) if w4 else ()))
                dA_ = ps_use(mmA, (t_sa, m1))
                w1 = S.do("tensor", LDW(selA[0:8, :]), dA_)
                w2 = S.do("tensor", MM(mmA[:, 0:32], selA[0:8, :], gslb[0:8, 0:32], True, True), (w1,))
                e1 = S.do("vector", CP(a1ev[:, 0:32], mmA[:, 0:32]), (w2,))
                ps_done(mmA, e1)
                dB_ = ps_use(mmB, (t_sb, m2))
                w3 = S.do("tensor", LDW(selB[0:16, :]), dB_ + (e1,))
                w4 = S.do("tensor", MM(mmB[:, 0:16], selB[0:16, :], gslb2[0:16, 0:16], True, True), (w3,))
                e2 = S.do("vector", CP(a2ev[:, 0:16], mmB[:, 0:16]), (w4,))
                ps_done(mmB, e2)
                a1r = a1ev[:, 0:32].unsqueeze(2).broadcast_to([P, 32, 16])
                a2r = a2ev[:, 0:16].unsqueeze(1).broadcast_to([P, 32, 16])
                if c == 0:
                    accum_last = S.do("vector", TT(facc[:, 0:512].rearrange("p (b e) -> p b e", e=16),
                                                   a1r, a2r, ALU.mult), (e1, e2, tk_a))
                else:
                    t_ = S.do("vector", TT(ftmp[:, 0:512].rearrange("p (b e) -> p b e", e=16),
                                           a1r, a2r, ALU.mult), (e1, e2))
                    accum_last = S.do("vector", TT(facc[:, 0:512], facc[:, 0:512], ftmp[:, 0:512], ALU.add), (t_,))
            last = S.do("vector", CP(dst_tiles[tile], facc[:, 0:512]), (accum_last,))
        return last

    tk_fr = build_ffn_w([fwr(i) for i in range(4)], d_g1r, d_g2r, (t_sb,))
    tk_fi = build_ffn_w([fwi(i) for i in range(4)], d_g1i, d_g2i, (tk_fr,))
    tk_fn = tk_fi
    for i in range(4):
        tk_fn = S.do("vector", TS(fwin(i), fwi(i), consts[:, C_NEG1:C_NEG1 + 1], ALU.mult), (tk_fn,))
    if _DBG:
        S.dma("gpsimd", d_dbgfw, fw_t[:, :], deps=(tk_fn,))
        S.dma("gpsimd", d_dbgg1t, g1t_sb[:, :], deps=(tk_fn,))

    cwinF = bbc_t.ap().bitcast(F32)
    cst_in = [cwinF[:, i * 512:(i + 1) * 512] for i in range(4)]
    cst_out = [cbc_t[:, i * 512:(i + 1) * 512] for i in range(4)]
    cslot = {"cp": [None] * 4, "outs": [()] * 4, "i": 0}

    def cast_weight(dsrc, ddst, n_k, n_m, row_len, deps=()):
        toks = []
        sub = row_len // 512
        for k in range(n_k):
            for s_ in range(sub):
                sl = cslot["i"] % 4
                cslot["i"] += 1
                q = ("sync", "scalar")[sl % 2]
                l1 = S.dma(q, cst_in[sl], dsrc[k * P:(k + 1) * P, s_ * 512:(s_ + 1) * 512],
                           deps=deps + ((cslot["cp"][sl],) if cslot["cp"][sl] else ()))
                c1 = S.do("vector", CP(cst_out[sl], cst_in[sl]), (l1,) + cslot["outs"][sl])
                cslot["cp"][sl] = c1
                cslot["outs"][sl] = ()
                for m in range(4):
                    gm = s_ * 4 + m
                    if gm >= n_m:
                        break
                    t_ = S.wdma(ddst[k, gm], cst_out[sl][:, m * P:(m + 1) * P], deps=(c1,))
                    toks.append(t_)
                    cslot["outs"][sl] += (t_,)
        return toks

    wcast_deps = (tk_cct, tk_bst[-1], tk_xp, tk_adl)
    ct1 = cast_weight(d_inw, d_winp, NFT, 32, 4096)
    ct2 = cast_weight(d_opw, d_wop, NDT, NFT, 1024)
    ct3 = cast_weight(d_woT, d_wwo, NFT, NFT, 1024)
    tk_cwl = S.dma("sync", cast_st[:, 0:64], d_cw, deps=wcast_deps)
    ct4 = []
    prevz = None
    for dtile in range(NDT):
        for tap in range(4):
            z1 = S.do("vector", MSET(ftmp[:, 0:128], 0.0), (tk_cwl,) + ((prevz,) if prevz else ()))
            z2 = S.do("vector", lambda e, d=dtile, tp=tap: e.copy_predicated(
                ftmp[:, 0:128], diagmask[:, :],
                cast_st[:, d * 4 + tp:d * 4 + tp + 1].broadcast_to([P, P])), (z1, t_msk))
            z3 = S.do("vector", CP(cast_st2[:, 0:128], ftmp[:, 0:128]), (z2,) + ((prevz,) if prevz else ()))
            prevz = S.wdma(d_wcv[dtile, tap], cast_st2[:, 0:128], deps=(z3,))
            ct4.append(prevz)
    CASTS = (ct1[-1], ct2[-1], ct3[-1], ct4[-1])

    S.barrier(("tensor", "vector", "scalar", "gpsimd", "sync"))
    S.mark_small = False

    ln_guard = {"xsq": None, "mr": None, "stat": None, "lr": None}

    def ln_stats(c0, w, extra=(), dstf=None):
        dstf = dstf or xn
        cp_toks, mm_toks = [], []
        lrg = (ln_guard["lr"],) if ln_guard["lr"] else ()
        g = tuple(t for t in (ln_guard["xsq"], ln_guard["mr"], ln_guard["stat"]) if t)
        dA_ = ps_use(psS, extra + g)
        dB_ = ps_use(psT, ())
        t_lw = S.do("tensor", LDW(onescol[:, 0:1]), dA_ + dB_ + (tk_oc,))
        for ft in range(NFT):
            cpt = S.do("vector", CP(dstf(ft)[:, 0:w], x_sb[ft][:, c0:c0 + w]), extra + g + tk_x)
            war = (mm_toks[-1],) if mm_toks else ()
            sqt = S.do("vector", TT(xsq[:, 0:w], dstf(ft)[:, 0:w], dstf(ft)[:, 0:w], ALU.mult), (cpt,) + war)
            m1 = S.do("tensor", MM(psS[0:1, 0:w], onescol[:, 0:1], dstf(ft)[:, 0:w], ft == 0, ft == NFT - 1), (cpt, t_lw))
            m2 = S.do("tensor", MM(psT[0:1, 0:w], onescol[:, 0:1], xsq[:, 0:w], ft == 0, ft == NFT - 1), (sqt, t_lw))
            cp_toks.append(cpt)
            mm_toks += [m1, m2]
        ln_guard["xsq"] = mm_toks[-1]
        _ms = S.mark_small
        S.mark_small = True
        r1 = S.do("vector", TS(stat_r[0:1, 0:w], psS[0:1, 0:w], consts[0:1, C_INV:C_INV + 1], ALU.mult), tuple(mm_toks))
        r2 = S.do("vector", TS(stat_r[0:1, BW:BW + w], psT[0:1, 0:w], consts[0:1, C_INV:C_INV + 1], ALU.mult), tuple(mm_toks))
        ps_done(psS, r1)
        ps_done(psT, r2)
        r3 = S.do("vector", TT(rowsc[0:1, 0:w], stat_r[0:1, 0:w], stat_r[0:1, 0:w], ALU.mult), (r1,))
        r4 = S.do("vector", TT(rowsc[0:1, 0:w], stat_r[0:1, BW:BW + w], rowsc[0:1, 0:w], ALU.subtract), (r2, r3))
        r5 = S.do("scalar", lambda e, w_=w: e.activation(lrow[0:1, 0:w_], rowsc[0:1, 0:w_], AF.Ln,
                                                         bias=consts[0:1, C_EPS:C_EPS + 1]), (r4,) + lrg)
        r6 = S.do("scalar", lambda e, w_=w: e.activation(rowsc[0:1, 0:w_], lrow[0:1, 0:w_], AF.Exp,
                                                         scale=consts[0:1, C_NHALF:C_NHALF + 1]), (r5,))
        ln_guard["lr"] = r6
        r7 = S.do("vector", CP(rowb16[0:1, 0:w], rowsc[0:1, 0:w]), (r6,))
        r8 = S.do("vector", CP(rowb16b[0:1, 0:w], stat_r[0:1, 0:w]), (r6, r1))
        ln_guard["stat"] = r8
        S.mark_small = _ms
        return (cp_toks, r7, r8, g)

    def ln_apply(c0, w, st, dstf=None):
        dstf = dstf or xn
        cp_toks, r7, r8, g = st
        dA_ = ps_use(psS, (r7, r8))
        dB_ = ps_use(psT, ())
        b1 = S.do("tensor", LDW(ones_c[0:1, :]), dA_ + dB_ + (tk_ones,))
        b2 = S.do("tensor", MM(psS[:, 0:w], ones_c[0:1, :], rowb16b[0:1, 0:w], True, True), (b1,))
        b3 = S.do("tensor", MM(psT[:, 0:w], ones_c[0:1, :], rowb16[0:1, 0:w], True, True), (b2,))
        e1 = S.do("scalar", ACT(m_bc[:, 0:w], psS[:, 0:w], AF.Copy), (b2,) + g)
        e2 = S.do("scalar", ACT(r_bc[:, 0:w], psT[:, 0:w], AF.Copy), (b3,) + g)
        ps_done(psS, e1)
        ps_done(psT, e2)
        out = []
        for ft in range(NFT):
            s1 = S.do("vector", TT(dstf(ft)[:, 0:w], dstf(ft)[:, 0:w], m_bc[:, 0:w], ALU.subtract), (e1, cp_toks[ft]))
            s2 = S.do("vector", TT(dstf(ft)[:, 0:w], dstf(ft)[:, 0:w], r_bc[:, 0:w], ALU.mult), (s1, e2))
            out.append(s2)
        ln_guard["mr"] = out[-1]
        S.minibar()
        return out

    def layernorm(c0, w, extra=(), dstf=None):
        st = ln_stats(c0, w, extra, dstf)
        return ln_apply(c0, w, st, dstf)

    SI = BW + 1
    t_s0 = S.do("vector", MSET(s_sb[0:64, 0:1], 0.0))
    t_s0b = S.do("vector", MSET(s_sb[0:64, SI:SI + 1], 0.0))
    xcv = lambda i: xc_t[:, i * BW:(i + 1) * BW]
    sza = lambda i: sz_t[:, i * BW:(i + 1) * BW]
    xn_gA = {0: (), 1: ()}
    ya_gA = {0: (), 1: ()}
    sb16_cons = ()
    NCHa = len(CHUNKS)
    lnA = {0: layernorm(CHUNKS[0][0], CHUNKS[0][1], (), dstf=xn)}
    wo_grp = {}

    def yab(cc):
        return ya if cc % 2 == 0 else sza

    def dsta(cc):
        return xn if cc % 2 == 0 else xcv

    def emit_wout(cc):
        cwo = CHUNKS[cc][1]
        yv = yab(cc)
        last = None
        for do_ in range(NFT):
            ps = yA if do_ % 2 == 0 else yB
            dps = ps_use(ps, ())
            ldt_ = None
            mlast = None
            for k in range(NFT):
                j, buf, wdeps = st_wo.get()
                tw = S.wdma(buf[:, :], d_wwo[k, do_], wdeps + (CASTS[2],))
                ldt_ = S.do("tensor", LDW(buf[:, :]), dps + (tw,) + ((ldt_,) if ldt_ else ()))
                mlast = S.do("tensor", MM(ps[:, 0:cwo], buf[:, :], yv(k)[:, 0:cwo], k == 0, k == NFT - 1),
                             (ldt_, ya_tk[(cc, k)]))
                st_wo.used(j, mlast)
            wo_grp[(cc, do_)] = mlast
            last = mlast
        ya_gA[cc % 2] = (last,)

    def emit_evr_a(cc, o):
        cwo = CHUNKS[cc][0:2]
        p0a, pwa = cwo
        ps = yA if o % 2 == 0 else yB
        evr = S.do("vector", STT(x_sb[o][:, p0a:p0a + pwa], ps[:, 0:pwa], bout_sb[:, o:o + 1],
                                 x_sb[o][:, p0a:p0a + pwa], ALU.add, ALU.add), (wo_grp[(cc, o)], tk_bo))
        ps_done(ps, evr)

    ya_tk = {}
    for ci, (c0, w) in enumerate(CHUNKS):
        ln = lnA[ci]
        dstf = dsta(ci)
        dA_ = ps_use(mmA, tuple(ln))
        dB_ = ps_use(mmB, tuple(ln))
        ldt_ = None
        mmr = mmi = None
        for f in range(NFT):
            ldt_ = S.do("tensor", LDW(bst(f)[:, 0:64]), dA_ + dB_ + (tk_bst[f],) + ((ldt_,) if ldt_ else ()))
            mmr = S.do("tensor", MM(mmA[0:64, 0:w], bst(f)[:, 0:64], dstf(f)[:, 0:w], f == 0, f == NFT - 1), (ldt_, ln[f]))
            ldt_ = S.do("tensor", LDW(bst(f)[:, 64:128]), (mmr,))
            mmi = S.do("tensor", MM(mmB[0:64, 0:w], bst(f)[:, 64:128], dstf(f)[:, 0:w], f == 0, f == NFT - 1), (ldt_,))
        if ci >= 1:
            emit_wout(ci - 1)
        evrq = list(range(NFT)) if ci >= 1 else []
        ev_bu = S.do("vector", CP(bu[0:64, 0:w], mmA[0:64, 0:w]), (mmr, mmi, t_s0, t_s0b))
        ev_bu2 = S.do("vector", CP(bu[0:64, BW:BW + w], mmB[0:64, 0:w]), (ev_bu,))
        ps_done(mmA, ev_bu)
        ps_done(mmB, ev_bu2)
        S.minibar()
        off = 0
        prev = ev_bu2
        _ms = S.mark_small
        S.mark_small = True
        for cw in subchunks(w):
            aR, bR = 1 + off, 1 + off + cw
            aI, bI = SI + 1 + off, SI + 1 + off + cw
            buR = bu[0:64, off:off + cw]
            buI = bu[0:64, BW + off:BW + off + cw]
            t1 = S.do("vector", TT(pow_scr[0:64, 0:cw], buI, lamN[0:64, IM:IM + cw], ALU.mult), (prev, tk_ln) + sb16_cons)
            t2 = S.do("vector", TT(s_sb[0:64, aR:bR], buR, lamN[0:64, 0:cw], ALU.mult), (t1,))
            t3 = S.do("vector", TT(s_sb[0:64, aR:bR], s_sb[0:64, aR:bR], pow_scr[0:64, 0:cw], ALU.subtract), (t2,))
            t4 = S.do("vector", TT(pow_scr[0:64, 0:cw], buR, lamN[0:64, IM:IM + cw], ALU.mult), (t3,))
            t5 = S.do("vector", TT(s_sb[0:64, aI:bI], buI, lamN[0:64, 0:cw], ALU.mult), (t4,))
            t6 = S.do("vector", TT(s_sb[0:64, aI:bI], s_sb[0:64, aI:bI], pow_scr[0:64, 0:cw], ALU.add), (t5,))
            csR = S.do("vector", SCAN(s_sb[0:64, aR:bR], onesf[0:64, 0:cw], s_sb[0:64, aR:bR], 0.0), (t6, tk_of))
            csI = S.do("vector", SCAN(s_sb[0:64, aI:bI], onesf[0:64, 0:cw], s_sb[0:64, aI:bI], 0.0), (csR,))
            q1 = S.do("vector", TT(pow_scr[0:64, 0:cw], s_sb[0:64, aR:bR], lamP[0:64, IM:IM + cw], ALU.mult), (csI, tk_lp))
            q2 = S.do("vector", TT(pow_scr[0:64, 128:128 + cw], s_sb[0:64, aI:bI], lamP[0:64, IM:IM + cw], ALU.mult), (q1,))
            q3 = S.do("vector", TT(s_sb[0:64, aR:bR], s_sb[0:64, aR:bR], lamP[0:64, 0:cw], ALU.mult), (q2,))
            q4 = S.do("vector", TT(s_sb[0:64, aR:bR], s_sb[0:64, aR:bR], pow_scr[0:64, 128:128 + cw], ALU.subtract), (q3,))
            q5 = S.do("vector", TT(s_sb[0:64, aI:bI], s_sb[0:64, aI:bI], lamP[0:64, 0:cw], ALU.mult), (q4,))
            q6 = S.do("vector", TT(s_sb[0:64, aI:bI], s_sb[0:64, aI:bI], pow_scr[0:64, 0:cw], ALU.add), (q5,))
            k1 = S.do("vector", TS(pow_scr[0:64, 0:cw], lamP[0:64, IM + 1:IM + 1 + cw], s_sb[0:64, aI - 1:aI], ALU.mult), (q6,))
            k2 = S.do("vector", STT(s_sb[0:64, aR:bR], lamP[0:64, 1:1 + cw], s_sb[0:64, aR - 1:aR], s_sb[0:64, aR:bR], ALU.mult, ALU.add), (k1,))
            k3 = S.do("vector", TT(s_sb[0:64, aR:bR], s_sb[0:64, aR:bR], pow_scr[0:64, 0:cw], ALU.subtract), (k2,))
            k4 = S.do("vector", TS(pow_scr[0:64, 0:cw], lamP[0:64, IM + 1:IM + 1 + cw], s_sb[0:64, aR - 1:aR], ALU.mult), (k3,))
            k5 = S.do("vector", STT(s_sb[0:64, aI:bI], lamP[0:64, 1:1 + cw], s_sb[0:64, aI - 1:aI], s_sb[0:64, aI:bI], ALU.mult, ALU.add), (k4,))
            k6 = S.do("vector", TT(s_sb[0:64, aI:bI], s_sb[0:64, aI:bI], pow_scr[0:64, 0:cw], ALU.add), (k5,))
            prev = k6
            off += cw
            for _ in range(2):
                if evrq:
                    emit_evr_a(ci - 1, evrq.pop(0))
        S.mark_small = _ms
        while evrq:
            emit_evr_a(ci - 1, evrq.pop(0))
        if _DBG and ci == 1:
            S.dma("gpsimd", d_dbgs, s_sb[0:64, :], deps=(prev,))
            S.dma("gpsimd", d_dbgb, bu[0:64, :], deps=(prev,))
        S.minibar()
        t_st = S.do("vector", CP(s_sb[0:64, 0:1], s_sb[0:64, w:w + 1]), (prev,), small=True)
        t_st2 = S.do("vector", CP(s_sb[0:64, SI:SI + 1], s_sb[0:64, SI + w:SI + w + 1]), (prev,), small=True)
        t_sb16 = S.do("vector", CP(sb16[0:64, 0:w], s_sb[0:64, 1:1 + w]), (prev,) + sb16_cons)
        t_sb16b = S.do("vector", CP(sb16[0:64, BW:BW + w], s_sb[0:64, SI + 1:SI + 1 + w]), (t_sb16,))
        if ci + 1 < NCHa:
            lnA[ci + 1] = layernorm(CHUNKS[ci + 1][0], CHUNKS[ci + 1][1],
                                    xn_gA[(ci + 1) % 2], dstf=dsta(ci + 1))
        yv = yab(ci)
        m2l = None
        for f in range(NFT):
            dB_ = ps_use(mmB, (t_sb16b,))
            l1 = S.do("tensor", LDW(cctR[:, f * P:(f + 1) * P]), dB_ + (tk_cct,))
            m1 = S.do("tensor", MM(mmB[:, 0:w], cctR[:, f * P:(f + 1) * P], sb16[0:64, 0:w], True, False), (l1,))
            l2 = S.do("tensor", LDW(cctIN[:, f * P:(f + 1) * P]), (m1,))
            m1b = S.do("tensor", MM(mmB[:, 0:w], cctIN[:, f * P:(f + 1) * P], sb16[0:64, BW:BW + w], False, False), (l2,))
            l3 = S.do("tensor", LDW(addiag[:, f * P:(f + 1) * P]), (m1b, tk_ad[f]))
            m2 = S.do("tensor", MM(mmB[:, 0:w], addiag[:, f * P:(f + 1) * P], dstf(f)[:, 0:w], False, True), (l3, ln[f]))
            ev = S.do("scalar", ACT(yv(f)[:, 0:w], mmB[:, 0:w], AF.Copy), (m2,) + ya_gA[ci % 2])
            ps_done(mmB, ev)
            ya_tk[(ci, f)] = ev
            m2l = m2
        S.minibar()
        xn_gA[ci % 2] = (m2l,)
        sb16_cons = (m2l,)
        if _DBG and ci == 1:
            S.dma("gpsimd", d_dbgy, ya_t[:, :], deps=tuple(ya_tk[(ci, f)] for f in range(NFT)))
    emit_wout(NCHa - 1)
    for o in range(NFT):
        emit_evr_a(NCHa - 1, o)
    S.barrier()

    def dump_resid(j, deps):
        if not _DBG:
            return
        for i_ in range(NFT):
            S.dma("gpsimd", d_dbg[j][i_ * P:(i_ + 1) * P, :], x_sb[i_][:, :], deps=deps)

    dump_resid(0, S.barrier())

    def ffn_layer(chunks, write_out):
        out_toks = []
        guard = {0: (), 1: ()}
        NCF = len(chunks)
        dsts = [xn if i % 2 == 0 else ya for i in range(NCF)]
        lnt = {0: layernorm(chunks[0][0], chunks[0][1], (), dstf=dsts[0])}
        st_nx = {}
        for ci, (c0, w) in enumerate(chunks):
            ln = lnt[ci]
            dstf = dsts[ci]
            nxt = ci + 1 if ci + 1 < NCF else None
            last_mm = None
            for o in range(NFT):
                ps = yA if o % 2 == 0 else yB
                dps = ps_use(ps, tuple(ln))
                real_out = o < 4
                osub = o % 4
                ldt_ = None
                mlast = None
                for k in range(NFT):
                    if real_out:
                        wt = (fwr(k) if k < 4 else fwin(k - 4))
                    else:
                        wt = (fwi(k) if k < 4 else fwr(k - 4))
                    wsl = wt[:, osub * P:(osub + 1) * P]
                    ldt_ = S.do("tensor", LDW(wsl), dps + (tk_fn,) + ((ldt_,) if ldt_ else ()))
                    mlast = S.do("tensor", MM(ps[:, 0:w], wsl, dstf(k)[:, 0:w], k == 0, k == NFT - 1), (ldt_, ln[k]))
                evr = S.do("vector", STT(x_sb[o][:, c0:c0 + w], ps[:, 0:w], fb_sb[:, o:o + 1],
                                         x_sb[o][:, c0:c0 + w], ALU.add, ALU.add), (mlast, tk_fb))
                ps_done(ps, evr)
                last_mm = mlast
                if nxt is not None:
                    if o == 2:
                        st_nx[nxt] = ln_stats(chunks[nxt][0], chunks[nxt][1], guard[nxt % 2], dsts[nxt])
                    elif o == 5:
                        lnt[nxt] = ln_apply(chunks[nxt][0], chunks[nxt][1], st_nx[nxt], dsts[nxt])
                if write_out:
                    out_toks.append(S.dma("gpsimd", d_out[o * P:(o + 1) * P, c0 - W_WARM:c0 - W_WARM + w],
                                          x_sb[o][:, c0:c0 + w], deps=(evr,)))
            guard[ci % 2] = (last_mm,)
        return out_toks

    ffn_layer(CHUNKS, False)
    dump_resid(1, S.barrier())

    t_tl0 = S.do("vector", MSET(tails[:, :], 0.0))
    xie_g = [() for _ in range(2)]
    da_g = [None, None, None]
    g_g = [None, None, None]
    fdb = Rot(flat_db)
    NCH = len(CHUNKS)

    lnB = {}
    xc_toks = {}
    sz_toks = {}
    x1_toks = {}
    u17_toks = {}
    gate_toks = {}
    xproj_mm = {}
    opk_last = {}
    opgrp = {}
    inp_last = {}
    scan_end = {}
    BCx = {}
    PV = {"bo1": None}

    def start_inp_xi(cc, m):
        cw = CHUNKS[cc][1]
        psI = mmA if m % 2 == 0 else mmB
        dps = ps_use(psI, ())
        ldt_ = None
        mlast = None
        for k in range(NFT):
            j, buf, wdeps = st_inp.get()
            tw = S.wdma(buf[:, :], d_winp[k, m], wdeps + (CASTS[0],))
            ldt_ = S.do("tensor", LDW(buf[:, :]), dps + (tw,) + ((ldt_,) if ldt_ else ()))
            mlast = S.do("tensor", MM(psI[:, 0:cw], buf[:, :], xn(k)[:, 0:cw], k == 0, k == NFT - 1),
                         (ldt_, lnB[cc][k]))
            st_inp.used(j, mlast)
        inp_last[cc] = mlast
        return (m, mlast, psI)

    def emit_conv_part(cc, pend):
        m, mlast, psI, ev = pend
        cw = CHUNKS[cc][1]
        jx = m % 2
        xbuf = xi_ext[jx]
        dpc = ps_use(psI, (ev,))
        clast = None
        cl = None
        for tap in range(4):
            jc, cbuf, cdeps = st_cv.get()
            tw = S.wdma(cbuf[:, :], d_wcv[m, tap], cdeps + (CASTS[3],))
            cl = S.do("tensor", LDW(cbuf[:, :]), dpc + (tw,) + ((cl,) if cl else ()))
            clast = S.do("tensor", MM(psI[:, 0:cw], cbuf[:, :], xbuf[:, tap:tap + cw], tap == 0, tap == 3),
                         (cl, ev))
            st_cv.used(jc, clast)
        xie_g[jx] = (clast,)
        return clast

    def emit_evc_z(cc, pend, clast):
        m, mlast, psI, ev = pend
        cw = CHUNKS[cc][1]
        if cc == 0:
            prv_rd = ()
        else:
            prv_rd = (u17_toks.get((cc - 1, m), x1_toks[(cc - 1, m)]), xproj_mm[cc - 1])
        evc = S.do("scalar", lambda e, m_=m, w_=cw, p_=psI: e.activation(
            xc(m_)[:, 0:w_], p_[:, 0:w_], AF.Silu, bias=cb_sb[:, m_:m_ + 1]),
            (clast, tk_cb) + prv_rd)
        ps_done(psI, evc)
        xc_toks[(cc, m)] = evc
        dps = ps_use(psI, ())
        ldt_ = None
        zlast = None
        for k in range(NFT):
            j, buf, wdeps = st_inp.get()
            tw = S.wdma(buf[:, :], d_winp[k, 16 + m], wdeps + (CASTS[0],))
            ldt_ = S.do("tensor", LDW(buf[:, :]), dps + (tw,) + ((ldt_,) if ldt_ else ()))
            zlast = S.do("tensor", MM(psI[:, 0:cw], buf[:, :], xn(k)[:, 0:cw], k == 0, k == NFT - 1),
                         (ldt_, lnB[cc][k]))
            st_inp.used(j, zlast)
        inp_last[cc] = zlast
        return zlast

    def emit_evz(cc, pend, zlast):
        m, mlast, psI, ev = pend
        cw = CHUNKS[cc][1]
        pz = (gate_toks[(cc - 1, m)],) if (cc - 1, m) in gate_toks else ()
        evz = S.do("scalar", lambda e, m_=m, w_=cw, p_=psI: e.activation(
            sz(m_)[:, 0:w_], p_[:, 0:w_], AF.Silu), (zlast,) + pz)
        ps_done(psI, evz)
        sz_toks[(cc, m)] = evz

    def emit_xi_evac(cc, pend0):
        m, mlast, psI = pend0
        cw = CHUNKS[cc][1]
        jx = m % 2
        xbuf = xi_ext[jx]
        tl = S.do("scalar", ACT(xbuf[:, 0:3], tails[:, m * 3:(m + 1) * 3], AF.Copy),
                  (t_tl0, mlast) + xie_g[jx], small=True)
        ev = S.do("scalar", ACT(xbuf[:, 3:3 + cw], psI[:, 0:cw], AF.Copy), (tl,))
        ps_done(psI, ev)
        S.do("scalar", ACT(tails[:, m * 3:(m + 1) * 3], xbuf[:, cw:cw + 3], AF.Copy),
             (ev,), small=True)
        return (m, mlast, psI, ev)

    def finish_inp_pair(cc, pend4, with_z=True):
        clast = emit_conv_part(cc, pend4)
        zlast = emit_evc_z(cc, pend4, clast) if with_z else None
        if not with_z:
            m, mlast, psI, ev = pend4
            cw = CHUNKS[cc][1]
            evc = S.do("scalar", lambda e, m_=m, w_=cw, p_=psI: e.activation(
                xc(m_)[:, 0:w_], p_[:, 0:w_], AF.Silu, bias=cb_sb[:, m_:m_ + 1]),
                (clast, tk_cb))
            ps_done(psI, evc)
            xc_toks[(cc, m)] = evc
            return
        emit_evz(cc, pend4, zlast)

    xpst = {"ld": None}

    def emit_xproj_step(cc, k):
        cw2 = CHUNKS[cc][1]
        if k == 0:
            dps = ps_use(psS, ())
            xpst["ld"] = None
        else:
            dps = ()
        lt = S.do("tensor", LDW(xpw(k)), dps + (tk_xp,) + ((xpst["ld"],) if xpst["ld"] else ()))
        xpst["ld"] = S.do("tensor", MM(psS[0:33, 0:cw2], xpw(k), xc(k)[:, 0:cw2], k == 0, k == NDT - 1),
                          (lt, xc_toks[(cc, k)]))
        if k == NDT - 1:
            xproj_mm[cc] = xpst["ld"]

    def emit_xproj_tail(cc):
        cw2 = CHUNKS[cc][1]
        ev_r16 = S.do("vector", CP(rows16[:, 0:cw2], psS[0:33, 0:cw2]),
                      (xproj_mm[cc],) + ((PV["bo1"],) if PV["bo1"] else ()))
        ps_done(psS, ev_r16)
        PV["bo1"] = S.dma("gpsimd", d_rows16[0:33, 0:cw2], rows16[:, 0:cw2], deps=(ev_r16,))
        return PV["bo1"]

    def emit_bcast(cc, bo1):
        cw = CHUNKS[cc][1]
        bcc = (scan_end[cc - 1],) if cc >= 1 else ()
        dps = ps_use(mmB, bcc)
        lb = S.do("tensor", LDW(ones_c[0:1, :]), dps + (tk_ones,))
        n_rows = 17 if cc == 0 else 33
        bc_evs = []
        mprev = None
        for ri in range(n_rows):
            jf, fbuf, fdeps = fdb.get()
            td = S.dma("gpsimd", fbuf[0:1, 0:cw], d_rows16[ri:ri + 1, 0:cw], deps=(bo1,) + fdeps)
            mprev = S.do("tensor", MM(mmB[:, 0:cw], ones_c[0:1, :], fbuf[0:1, 0:cw], True, True),
                         ((lb,) if mprev is None else (mprev,)) + (td,) + ((bc_evs[-1],) if bc_evs else ()))
            fdb.used(jf, mprev)
            if ri == 0:
                dst = dtraw_bc[:, 0:cw]
            elif ri <= 16:
                dst = bbc(ri - 1)[:, 0:cw]
            else:
                dst = cbc(ri - 17)[:, 0:cw]
            bc_evs.append(S.do("scalar", ACT(dst, mmB[:, 0:cw], AF.Copy), (mprev,) + bcc))
        ps_done(mmB, bc_evs[-1])
        BCx[cc] = bc_evs[-1]

    def emit_out_proj(cc):
        c0p, cw = CHUNKS[cc]

        def one_group(o):
            psO = psU if o % 2 == 0 else psV
            dps = ps_use(psO, ())
            ldt_ = None
            mlast = None
            for k in range(NDT):
                j, buf, wdeps = st_op.get()
                tw = S.wdma(buf[:, :], d_wop[k, o], wdeps + (CASTS[1],))
                ldt_ = S.do("tensor", LDW(buf[:, :]), dps + (tw,) + ((ldt_,) if ldt_ else ()))
                mlast = S.do("tensor", MM(psO[:, 0:cw], buf[:, :], y2(k)[:, 0:cw], k == 0, k == NDT - 1),
                             (ldt_, gate_toks[(cc, k)]))
                st_op.used(j, mlast)
                if o == NFT - 1:
                    opk_last[(cc, k)] = mlast
            opgrp[(cc, o)] = mlast

        def one_evr(o):
            psO = psU if o % 2 == 0 else psV
            evr = S.do("vector", TT(x_sb[o][:, c0p:c0p + cw], psO[:, 0:cw], x_sb[o][:, c0p:c0p + cw], ALU.add),
                       (opgrp[(cc, o)],))
            ps_done(psO, evr)

        one_group(0)
        one_group(1)
        for o in range(2, NFT):
            one_evr(o - 2)
            one_group(o)
        one_evr(NFT - 2)
        one_evr(NFT - 1)

    def emit_scan(c):
        c0, cw = CHUNKS[c]
        warm = (c == 0)
        nxt = c + 1 if c + 1 < NCH else None
        BC = BCx[c]
        if c - 1 >= 1:
            emit_out_proj(c - 1)
        dt_guard = [None, None]
        d4_prev = [None]
        scan_last_v = None
        pend = [None]
        u_g = [None, None]
        uu_last = [None, None]
        chain = {}

        def emit_dt_chain(dtile):
            idt = dtile % 2
            d1 = S.do("vector", TS2(dt_b2[idt][:, 0:cw], dtraw_bc[:, 0:cw], dtw_sb[:, dtile:dtile + 1],
                                    dtb_sb[:, dtile:dtile + 1], ALU.mult, ALU.add),
                      (BC, tk_dtw, tk_dtb) + ((dt_guard[idt],) if dt_guard[idt] else ()))
            d2 = S.do("scalar", lambda e, i=idt, w_=cw: e.activation(
                dtl_t[:, 0:w_], dt_b2[i][:, 0:w_], AF.Exp, scale=consts[:, C_NEG1:C_NEG1 + 1]),
                (d1,) + ((d4_prev[0],) if d4_prev[0] else ()))
            d3 = S.do("scalar", lambda e, w_=cw: e.activation(
                xsq[:, 0:w_], dtl_t[:, 0:w_], AF.Ln, bias=1.0),
                (d2,) + ((ln_guard["xsq"],) if ln_guard["xsq"] else ()) + ((d4_prev[0],) if d4_prev[0] else ()))
            chain[dtile] = (d1, d2, d3)

        emit_dt_chain(0)
        for dtile in range(NDT):
            idt = dtile % 2
            d1, d2, d3 = chain[dtile]
            d4 = S.do("vector", TT(dt_b2[idt][:, 0:cw], dt_b2[idt][:, 0:cw], xsq[:, 0:cw], ALU.add), (d3, d1))
            d4_prev[0] = d4
            clast_p = emit_conv_part(nxt, pend[0]) if (nxt is not None and pend[0] is not None) else None
            p0_new = start_inp_xi(nxt, dtile) if nxt is not None else None
            x1 = S.do("vector", TT(dtx_b2[idt][:, 0:cw], dt_b2[idt][:, 0:cw], xc(dtile)[:, 0:cw], ALU.mult),
                      (d4, xc_toks[(c, dtile)]) + ((uu_last[idt],) if uu_last[idt] else ()))
            x1_toks[(c, dtile)] = x1
            if warm:
                x1 = S.do("vector", TS(dtx_b2[idt][:, 0:cw], dtx_b2[idt][:, 0:cw], wm_col[:, 0:1], ALU.mult),
                          (x1, tk_wm))
                psy = None
                dps = ()
            else:
                psy = yA if dtile % 2 == 0 else yB
                dps = ps_use(psy, ())
                t_idl = S.do("tensor", LDW(ident[:, :]), (t_id, BC))
            nmm = 0
            mmtok = None
            zlast_p = None
            pend_new = None
            for s in range(NS):
                if s == 2 and clast_p is not None:
                    zlast_p = emit_evc_z(nxt, pend[0], clast_p)
                    emit_xproj_step(nxt, pend[0][0])
                if s == 8 and p0_new is not None:
                    pend_new = emit_xi_evac(nxt, p0_new)
                js = s % 3
                da = S.do("scalar", lambda e, i=idt, s_=s, d_=dtile, buf=da_b[js], w_=cw: e.activation(
                    buf[:, 0:w_], dt_b2[i][:, 0:w_], AF.Exp,
                    scale=a_sb[:, d_ * NS + s_:d_ * NS + s_ + 1]),
                    (d4, tk_a) + ((da_g[js],) if da_g[js] else ()))
                ju = s % 2
                uu = S.do("vector", TT(u_b[ju][:, 0:cw], dtx_b2[idt][:, 0:cw], bbc(s)[:, 0:cw], ALU.mult),
                          (x1, BC) + ((u_g[ju],) if u_g[ju] else ()))
                uu_last[idt] = uu
                hs = hpool[:, (s % 4) * BW:(s % 4) * BW + cw]
                init = 0.0 if c == 0 else states[:, dtile * NS + s:dtile * NS + s + 1]
                sc = S.do("vector", SCAN(hs, da_b[js][:, 0:cw], u_b[ju][:, 0:cw], init), (da, uu))
                da_g[js] = sc
                u_g[ju] = sc
                dt_guard[idt] = da
                if not warm:
                    gg = S.do("vector", TT(g_b[js][:, 0:cw], hs, cbc(s)[:, 0:cw], ALU.mult),
                              (sc,) + ((g_g[js],) if g_g[js] else ()))
                    if nmm == 0:
                        mmtok = S.do("tensor", MM(psy[:, 0:cw], ident[:, :], g_b[js][:, 0:cw], True, False),
                                     dps + (gg, t_idl))
                    else:
                        mmtok = S.do("tensor", MM(psy[:, 0:cw], ident[:, :], g_b[js][:, 0:cw], False, False),
                                     (gg, mmtok))
                    g_g[js] = mmtok
                    nmm += 1
                if s % 4 == 3:
                    q = s // 4
                    gath = S.do("vector", CP(states[:, dtile * NS + q * 4:dtile * NS + q * 4 + 4],
                                             hpool[:, cw - 1:cw - 1 + 3 * BW + 1:BW]), (sc,), small=True)
                    scan_last_v = gath
            if not warm:
                js17 = NS % 3
                u17 = S.do("vector", TS(g_b[js17][:, 0:cw], xc(dtile)[:, 0:cw], mD_sb[:, dtile:dtile + 1], ALU.mult),
                           (xc_toks[(c, dtile)], tk_mD) + ((g_g[js17],) if g_g[js17] else ()))
                u17_toks[(c, dtile)] = u17
                mmtok = S.do("tensor", MM(psy[:, 0:cw], ident[:, :], g_b[js17][:, 0:cw], False, True), (u17, mmtok))
                g_g[js17] = mmtok
                pgd = opk_last.get((c - 1, dtile))
                ev = S.do("vector", TT(y2(dtile)[:, 0:cw], psy[:, 0:cw], sz(dtile)[:, 0:cw], ALU.mult),
                          (mmtok, sz_toks[(c, dtile)]) + ((pgd,) if pgd else ()))
                ps_done(psy, ev)
                gate_toks[(c, dtile)] = ev
                scan_last_v = ev
            if zlast_p is not None:
                emit_evz(nxt, pend[0], zlast_p)
            if dtile + 1 < NDT:
                emit_dt_chain(dtile + 1)
            if pend_new is not None:
                pend[0] = pend_new
        if nxt is not None and pend[0] is not None:
            finish_inp_pair(nxt, pend[0])
        scan_end[c] = scan_last_v

    c0_, w_ = CHUNKS[0]
    lnB[0] = layernorm(c0_, w_, ())
    for m in range(16):
        p0g = start_inp_xi(0, m)
        p4g = emit_xi_evac(0, p0g)
        finish_inp_pair(0, p4g, with_z=False)
    for k in range(NDT):
        emit_xproj_step(0, k)
    emit_bcast(0, emit_xproj_tail(0))
    lnB[1] = layernorm(CHUNKS[1][0], CHUNKS[1][1], (inp_last[0],))

    for c in range(NCH):
        emit_scan(c)
        if c + 1 < NCH:
            emit_xproj_step(c + 1, NDT - 1)
            emit_bcast(c + 1, emit_xproj_tail(c + 1))
            if c + 2 < NCH:
                lnB[c + 2] = layernorm(CHUNKS[c + 2][0], CHUNKS[c + 2][1], (inp_last[c + 1],))

    emit_out_proj(NCH - 1)
    if _DBG:
        dep = (opgrp[(NCH - 1, NFT - 1)], scan_end[NCH - 1])
        S.dma("gpsimd", d_dbgxc, xc_t[:, :], deps=dep)
        S.dma("gpsimd", d_dbgsz, sz_t[:, :], deps=dep)
        S.dma("gpsimd", d_dbgy2, y2_t[:, :], deps=dep)
        S.dma("gpsimd", d_dbgrw, rows16[:, :], deps=dep)
        S.dma("gpsimd", d_dbgst, states[:, :], deps=dep)
        S.dma("gpsimd", d_dbga, a_sb[:, :], deps=dep)
        S.dma("gpsimd", d_dbgdt, dtw_sb[:, :], deps=dep)
    dump_resid(2, S.barrier())

    out_toks = ffn_layer(CHUNKS[1:], True)
    S.do("gpsimd", lambda e: e.nop(), tuple(out_toks))

    with nc.Block() as blk:
        S.replay(blk)
    return nc


def kernel(**inputs):
    i = inputs
    x = np.asarray(i["L0_attn_a_x"])[0]
    xT = np.ascontiguousarray(x.T)

    def shard_x(k):
        lo = k * T_MAIN - W_WARM
        if lo < 0:
            pad = np.zeros((D_MODEL, -lo), dtype=np.float32)
            return np.ascontiguousarray(np.concatenate([pad, xT[:, 0:(k + 1) * T_MAIN]], axis=1))
        return np.ascontiguousarray(xT[:, lo:(k + 1) * T_MAIN])

    def cvt(a):
        return np.ascontiguousarray(np.asarray(a), dtype=np.float32)

    def colmaj(v, ntiles):
        return np.ascontiguousarray(cvt(v).reshape(ntiles, P).T)

    common = {
        "log_lam_real": cvt(i["L0_attn_a_log_Lam_real"]).reshape(1, 64),
        "lam_imag": cvt(i["L0_attn_a_Lam_imag"]).reshape(1, 64),
        "b_realT": np.ascontiguousarray(cvt(i["L0_attn_a_B_real"]).T),
        "b_imagT": np.ascontiguousarray(cvt(i["L0_attn_a_B_imag"]).T),
        "c_realT": np.ascontiguousarray(cvt(i["L0_attn_a_C_real"]).T),
        "c_imagT": np.ascontiguousarray(cvt(i["L0_attn_a_C_imag"]).T),
        "a_D": cvt(i["L0_attn_a_D"]).reshape(1, D_MODEL),
        "log_dt": cvt(i["L0_attn_a_log_dt"]).reshape(1, 1),
        "w_outT": np.ascontiguousarray(cvt(i["L0_attn_a_W_out"]).T),
        "b_out": colmaj(i["L0_attn_a_b_out"], NFT),
        "in_proj_w": cvt(i["L0_attn_b_in_proj_w"]),
        "conv_w": np.ascontiguousarray(cvt(i["L0_attn_b_conv_w"]).reshape(NDT, P, 4).transpose(1, 0, 2).reshape(P, 64)),
        "conv_b": colmaj(i["L0_attn_b_conv_b"], NDT),
        "x_proj_w": np.ascontiguousarray(cvt(i["L0_attn_b_x_proj_w"]).reshape(NDT, P, 33).transpose(1, 0, 2).reshape(P, NDT * 33)),
        "dt_w": colmaj(i["L0_attn_b_dt_w"], NDT),
        "dt_b": colmaj(i["L0_attn_b_dt_b"], NDT),
        "a_log": np.ascontiguousarray(cvt(i["L0_attn_b_A_log"]).reshape(NDT, P, NS).transpose(1, 0, 2).reshape(P, NDT * NS)),
        "m_D": colmaj(i["L0_attn_b_D"], NDT),
        "out_proj_w": cvt(i["L0_attn_b_out_proj_w"]),
        "g1r": np.ascontiguousarray(cvt(i["L0_ffn_G1_R"]).reshape(32, 128)),
        "g2r": np.ascontiguousarray(cvt(i["L0_ffn_G2_R"]).reshape(4, 16, 16).transpose(0, 2, 1).reshape(64, 16)),
        "g1i": np.ascontiguousarray(cvt(i["L0_ffn_G1_I"]).reshape(32, 128)),
        "g2i": np.ascontiguousarray(cvt(i["L0_ffn_G2_I"]).reshape(4, 16, 16).transpose(0, 2, 1).reshape(64, 16)),
        "ffn_bias": np.ascontiguousarray(
            np.concatenate([cvt(i["L0_ffn_bias_real"]), cvt(i["L0_ffn_bias_imag"])]).reshape(NFT, P).T),
    }
    in_maps = []
    for k in range(N_CORES):
        m = dict(common)
        m["x_in"] = shard_x(k)
        m["warm"] = np.asarray([[0.0 if k == 0 else 1.0]], dtype=np.float32)
        in_maps.append(m)

    nc = build_nc()
    res = run_bass_kernel_spmd(nc, in_maps, core_ids=list(range(N_CORES)))
    parts = [res.results[k]["out"] for k in range(N_CORES)]
    full = np.concatenate(parts, axis=1)
    return np.ascontiguousarray(full.T).reshape(1, 16384, D_MODEL)


if __name__ == "__main__":
    _nc = build_nc()
    print("build ok; instrs:", {e: len(_nc2) for e, _nc2 in []} if False else "done")


# version 33 (final)
# speedup vs baseline: 1.0982x; 1.0982x over previous
import contextlib
import numpy as np
import concourse.bass as bass
import concourse.mybir as mybir
from concourse.bass_utils import run_bass_kernel_spmd

F32 = mybir.dt.float32
BF16 = mybir.dt.bfloat16
I32 = mybir.dt.int32
AF = mybir.ActivationFunctionType
ALU = mybir.AluOpType

P = 128
W_WARM = 128
T_MAIN = 2048
T_TOT = W_WARM + T_MAIN
CHUNKS = [(0, 128), (128, 352), (480, 352), (832, 352), (1184, 352), (1536, 352), (1888, 288)]
BW = 352
D_MODEL = 1024
D_INNER = 2048
NFT = D_MODEL // P
NDT = D_INNER // P
NS = 16
EPS = 1e-5
N_CORES = 8

_DBG = False
_PARANOID = False
ENGINES = ("sync", "tensor", "vector", "scalar", "gpsimd")
WQ = ("sync", "scalar")


def subchunks(w):
    if w == 128:
        return [128]
    if w == 352:
        return [88, 88, 88, 88]
    if w == 288:
        return [96, 96, 96]
    raise ValueError(w)


class Sched:
    NQL = 8

    def __init__(self, nc, es):
        self.sem = {e: es.enter_context(nc.semaphore("s_" + e)) for e in ENGINES}
        self.qsem = {e: [es.enter_context(nc.semaphore(f"q_{e}{j}")) for j in range(self.NQL)]
                     for e in ("sync", "scalar", "gpsimd")}
        self.n = {e: 0 for e in ENGINES}
        self.q = {e: 0 for e in ENGINES}
        self.ops = {e: [] for e in ENGINES}
        self.last_wait = {e: {} for e in ENGINES}
        self.small_toks = set()
        self._wq_i = 0

    SELF_MARGIN = 8

    def _resolve(self, eng, deps):
        small_toks = self.small_toks
        need = {}
        for d in deps:
            if d is None:
                continue
            kind, e, k = d
            if kind == "c":
                if e == eng:
                    if k <= self.n[eng] - self.SELF_MARGIN or (e not in ("vector", "gpsimd") and (e, k) not in small_toks):
                        continue
                    key, sem, val = ("s", e), self.sem[e], k
                else:
                    key, sem, val = ("s", e), self.sem[e], k
            else:
                lane = (k - 1) % self.NQL
                key = ("q", e, lane)
                sem = self.qsem[e][lane]
                val = 16 * ((k - 1) // self.NQL + 1)
            cur = need.get(key)
            if cur is None or val > cur[1]:
                need[key] = (sem, val)
        waits = []
        for key, (sem, val) in need.items():
            if self.last_wait[eng].get(key, -1) >= val:
                continue
            self.last_wait[eng][key] = val
            waits.append((sem, val))
        return waits

    mark_small = True

    def do(self, eng, fn, deps=(), small=None):
        waits = self._resolve(eng, deps)
        self.ops[eng].append(("c", fn, waits))
        self.n[eng] += 1
        tok = ("c", eng, self.n[eng])
        if small is None:
            small = self.mark_small
        if small:
            self.small_toks.add((eng, self.n[eng]))
        return tok

    def dma(self, eng, out, in_, deps=()):
        waits = self._resolve(eng, deps)
        lane = self.q[eng] % self.NQL
        self.ops[eng].append(("d", (out, in_, lane), waits))
        self.q[eng] += 1
        return ("d", eng, self.q[eng])

    def wdma(self, out, in_, deps=()):
        e = WQ[self._wq_i % len(WQ)]
        self._wq_i += 1
        return self.dma(e, out, in_, deps)

    def barrier(self, engines=("tensor", "vector", "scalar", "gpsimd")):
        snap = tuple(("c", e, self.n[e]) for e in engines) + tuple(
            ("d", e, k) for e in ("sync", "scalar", "gpsimd") for k in range(max(1, self.q[e] - self.NQL + 1), self.q[e] + 1))
        return tuple(self.do(e, lambda x: x.nop(), snap) for e in engines)

    def minibar(self):
        if _PARANOID:
            self.barrier(("tensor", "vector", "scalar", "gpsimd"))

    def replay(self, blk):
        for e in ENGINES:
            mylist, mysem = self.ops[e], self.sem[e]
            myq = self.qsem.get(e)

            def make(section_list, sem, qs):
                def body(engine):
                    for kind, payload, waits in section_list:
                        for s, v in waits:
                            engine.wait_ge(s, v)
                        if kind == "c":
                            payload(engine).then_inc(sem, 1)
                        else:
                            out, in_, lane = payload
                            engine.dma_start(out=out, in_=in_).then_inc(qs[lane], 16)
                return body
            getattr(blk, e)(make(mylist, mysem, myq))


class Rot:
    def __init__(self, bufs):
        self.bufs = bufs
        self.cons = [[] for _ in bufs]
        self.i = 0

    def get(self):
        j = self.i % len(self.bufs)
        self.i += 1
        deps = tuple(self.cons[j])
        self.cons[j] = []
        return j, self.bufs[j], deps

    def used(self, j, tok):
        self.cons[j].append(tok)


def build_nc():
    nc = bass.Bass("TRN2", target_bir_lowering=False)
    es = contextlib.ExitStack()

    DT = nc.dram_tensor
    d_x = DT("x_in", [D_MODEL, T_TOT], F32, kind="ExternalInput").ap()
    d_warm = DT("warm", [1, 1], F32, kind="ExternalInput").ap()
    d_llr = DT("log_lam_real", [1, 64], F32, kind="ExternalInput").ap()
    d_lim = DT("lam_imag", [1, 64], F32, kind="ExternalInput").ap()
    d_brT = DT("b_realT", [D_MODEL, 64], F32, kind="ExternalInput").ap()
    d_biT = DT("b_imagT", [D_MODEL, 64], F32, kind="ExternalInput").ap()
    d_crT = DT("c_realT", [64, D_MODEL], F32, kind="ExternalInput").ap()
    d_ciT = DT("c_imagT", [64, D_MODEL], F32, kind="ExternalInput").ap()
    d_aD = DT("a_D", [1, D_MODEL], F32, kind="ExternalInput").ap()
    d_ldt = DT("log_dt", [1, 1], F32, kind="ExternalInput").ap()
    d_woT = DT("w_outT", [D_MODEL, D_MODEL], F32, kind="ExternalInput").ap()
    d_bout = DT("b_out", [P, NFT], F32, kind="ExternalInput").ap()
    d_inw = DT("in_proj_w", [D_MODEL, 2 * D_INNER], F32, kind="ExternalInput").ap()
    d_cw = DT("conv_w", [P, 64], F32, kind="ExternalInput").ap()
    d_cb = DT("conv_b", [P, NDT], F32, kind="ExternalInput").ap()
    d_xpw = DT("x_proj_w", [P, NDT * 33], F32, kind="ExternalInput").ap()
    d_dtw = DT("dt_w", [P, NDT], F32, kind="ExternalInput").ap()
    d_dtb = DT("dt_b", [P, NDT], F32, kind="ExternalInput").ap()
    d_alog = DT("a_log", [P, NDT * NS], F32, kind="ExternalInput").ap()
    d_mD = DT("m_D", [P, NDT], F32, kind="ExternalInput").ap()
    d_opw = DT("out_proj_w", [D_INNER, D_MODEL], F32, kind="ExternalInput").ap()
    d_g1r = DT("g1r", [32, 128], F32, kind="ExternalInput").ap()
    d_g2r = DT("g2r", [64, 16], F32, kind="ExternalInput").ap()
    d_g1i = DT("g1i", [32, 128], F32, kind="ExternalInput").ap()
    d_g2i = DT("g2i", [64, 16], F32, kind="ExternalInput").ap()
    d_fb = DT("ffn_bias", [P, NFT], F32, kind="ExternalInput").ap()
    d_out = DT("out", [D_MODEL, T_MAIN], F32, kind="ExternalOutput").ap()
    if _DBG:
        d_dbg = [DT(f"dbg{j}", [D_MODEL, T_TOT], F32, kind="ExternalOutput").ap() for j in range(3)]
        d_dbgs = DT("dbgs", [64, 2 * (BW + 1)], F32, kind="ExternalOutput").ap()
        d_dbgb = DT("dbgb", [64, 2 * BW], F32, kind="ExternalOutput").ap()
        d_dbgy = DT("dbgy", [P, NFT * BW], BF16, kind="ExternalOutput").ap()
        d_dbglp = DT("dbglp", [64, 258], F32, kind="ExternalOutput").ap()
        d_dbgln = DT("dbgln", [64, 258], F32, kind="ExternalOutput").ap()
        d_dbglsc = DT("dbglsc", [64, 16], F32, kind="ExternalOutput").ap()
        d_dbglrow = DT("dbglrow", [1, 768], F32, kind="ExternalOutput").ap()
        d_dbgfw = DT("dbgfw", [P, 3 * 2048], BF16, kind="ExternalOutput").ap()
        d_dbgg1t = DT("dbgg1t", [P, 32], F32, kind="ExternalOutput").ap()
        d_dbgxc = DT("dbgxc", [P, NDT * BW], BF16, kind="ExternalOutput").ap()
        d_dbgsz = DT("dbgsz", [P, NDT * BW], BF16, kind="ExternalOutput").ap()
        d_dbgy2 = DT("dbgy2", [P, NDT * BW], BF16, kind="ExternalOutput").ap()
        d_dbgrw = DT("dbgrw", [33, BW], BF16, kind="ExternalOutput").ap()
        d_dbgst = DT("dbgst", [P, NDT * NS], BF16, kind="ExternalOutput").ap()
        d_dbga = DT("dbga", [P, NDT * NS], F32, kind="ExternalOutput").ap()
        d_dbgdt = DT("dbgdt", [P, NDT], F32, kind="ExternalOutput").ap()

    d_winp = DT("scr_inproj", [NFT, 32, P, P], BF16).ap()
    d_wop = DT("scr_outproj", [NDT, NFT, P, P], BF16).ap()
    d_wwo = DT("scr_wout", [NFT, NFT, P, P], BF16).ap()
    d_wcv = DT("scr_conv", [NDT, 4, P, P], BF16).ap()
    d_rows = DT("scr_rows", [1, 512], F32).ap()
    d_rows16 = DT("scr_rows16", [33, 512], BF16).ap()

    A = nc.alloc_sbuf_tensor
    x_sb = [A(f"x{i}", [P, T_TOT], F32) for i in range(NFT)]

    def mk(name, ntiles, dtype=BF16, stride=BW):
        t = A(name, [P, ntiles * stride], dtype)
        return t, stride

    xn_t, _ = mk("xn", NFT)
    xn = lambda i: xn_t[:, i * BW:(i + 1) * BW]
    xsq = A("xsq", [P, BW], BF16)
    m_bc = A("m_bc", [P, BW], BF16)
    r_bc = A("r_bc", [P, BW], BF16)
    dtraw_bc = A("dtraw_bc", [P, BW], BF16)

    bst_t = A("bst", [P, NFT * P], BF16)
    bst = lambda f: bst_t[:, f * P:(f + 1) * P]
    cctR = A("cctR", [64, D_MODEL], BF16)
    cctIN = A("cctIN", [64, D_MODEL], BF16)
    addiag = A("addiag", [P, NFT * P], BF16)
    bu = A("bu", [64, 2 * BW], F32)
    s_sb = A("s_sb", [64, 2 * (BW + 1)], F32)
    sb16 = A("sb16", [64, 2 * BW], BF16)
    ya_t, _ = mk("ya", NFT)
    ya = lambda i: ya_t[:, i * BW:(i + 1) * BW]
    lamP = A("lamP", [64, 258], F32)
    lamN = A("lamN", [64, 258], F32)
    lam_sc = A("lam_sc", [64, 16], F32)
    pow_scr = A("pow_scr", [64, 256], F32)
    onesf = A("onesf", [P, 128], F32)

    fw_t = A("fw", [P, 3 * 2048], BF16)
    fwr = lambda k: fw_t[:, k * 512:(k + 1) * 512]
    fwi = lambda k: fw_t[:, 2048 + k * 512:2048 + (k + 1) * 512]
    fwin = lambda k: fw_t[:, 4096 + k * 512:4096 + (k + 1) * 512]

    xpw_t = A("xpw", [P, NDT * 33], BF16)
    xpw = lambda d: xpw_t[:, d * 33:(d + 1) * 33]
    xi_ext = [A(f"xiext{i}", [P, BW + 3], BF16) for i in range(2)]
    tails = A("tails", [P, 3 * NDT], BF16)
    xc_t, _ = mk("xc", NDT)
    xc = lambda d: xc_t[:, d * BW:(d + 1) * BW]
    sz_t, _ = mk("sz", NDT)
    sz = lambda d: sz_t[:, d * BW:(d + 1) * BW]
    y2_t, _ = mk("y2", NDT)
    y2 = lambda d: y2_t[:, d * BW:(d + 1) * BW]
    bbc_t, _ = mk("bbc", NS)
    bbc = lambda s: bbc_t[:, s * BW:(s + 1) * BW]
    cbc_t, _ = mk("cbc", NS)
    cbc = lambda s: cbc_t[:, s * BW:(s + 1) * BW]
    hpool = A("hpool", [P, 4 * BW], BF16)
    states = A("states", [P, NDT * NS], BF16)
    dt_b2 = [A(f"dtt{i}", [P, BW], BF16) for i in range(2)]
    dtl_t = A("dtl_t", [P, BW], BF16)
    dtx_b2 = [A(f"dtx{i}", [P, BW], BF16) for i in range(2)]
    da_b = [A(f"dab{i}", [P, BW], BF16) for i in range(3)]
    u_b = [A(f"ub{i}", [P, BW], BF16) for i in range(2)]
    g_b = [A(f"gb{i}", [P, BW], BF16) for i in range(3)]
    rows16 = A("rows16", [33, BW], BF16)
    flat_db = [A(f"flatdb{i}", [1, BW], BF16) for i in range(2)]
    dtw_sb = A("dtw_sb", [P, NDT], F32)
    dtb_sb = A("dtb_sb", [P, NDT], F32)
    a_sb = A("a_sb", [P, NDT * NS], F32)
    mD_sb = A("mD_sb", [P, NDT], F32)
    cb_sb = A("cb_sb", [P, NDT], F32)
    bout_sb = A("bout_sb", [P, NFT], F32)
    fb_sb = A("fb_sb", [P, NFT], F32)
    wm_col = A("wm_col", [P, 1], F32)

    cast_st = A("cast_st", [P, 512], F32)
    cast_st2 = A("cast_st2", [P, 512], BF16)
    _csb = cast_st.ap().bitcast(BF16)
    st_inp = Rot([A(f"stinp{i}", [P, P], BF16) for i in range(4)]
                 + [_csb[:, i * P:(i + 1) * P] for i in range(8)])
    st_op = Rot([A(f"stop{i}", [P, P], BF16) for i in range(4)]
                + [cast_st2[:, i * P:(i + 1) * P] for i in range(4)])
    st_wo = Rot([A(f"stwo{i}", [P, P], BF16) for i in range(3)])
    st_cv = Rot([A(f"stcv{i}", [P, P], BF16) for i in range(4)])

    consts = A("consts", [P, 12], F32)
    ones_c = A("ones_c", [1, P], BF16)
    onescol = A("onescol", [P, 1], BF16)
    ident = A("ident", [P, P], BF16)
    diagmask = A("diagmask", [P, P], I32)
    lrow = A("lrow", [1, 768], F32)
    dcol = A("dcol", [P, 2], F32)
    lrow_i = A("lrow_i", [1, 64], I32)
    stat_r = A("stat_r", [1, 2 * BW], F32)
    rowsc = A("rowsc", [1, BW], F32)
    rowb16 = A("rowb16", [1, BW], BF16)
    rowb16b = A("rowb16b", [1, BW], BF16)
    g1_sb = A("g1_sb", [32, 128], F32)
    g2_sb = A("g2_sb", [64, 16], F32)
    g1t_sb = A("g1t_sb", [P, 32], F32)
    selA = A("selA", [8, P], BF16)
    selB = A("selB", [16, P], BF16)
    gsl = A("gsl", [16, 32], F32)
    gsl2 = A("gsl2", [16, 16], F32)
    gslb = A("gslb", [16, 32], BF16)
    gslb2 = A("gslb2", [16, 16], BF16)
    a1ev = A("a1ev", [P, 32], F32)
    a2ev = A("a2ev", [P, 16], F32)
    ftmp = A("ftmp", [P, 512], F32)
    facc = A("facc", [P, 512], F32)
    iota1 = ftmp.ap()[:, 0:P].bitcast(I32)
    iota2 = facc.ap()[:, 0:P].bitcast(I32)

    PT = nc.alloc_psum_tensor
    mmA = PT("mmA", [P, 512], F32)
    mmB = PT("mmB", [P, 512], F32)
    yA = PT("yA", [P, 512], F32)
    yB = PT("yB", [P, 512], F32)
    psS = PT("psS", [P, 512], F32)
    psT = PT("psT", [P, 512], F32)
    psU = PT("psU", [P, 512], F32)
    psV = PT("psV", [P, 512], F32)
    ps_cons = {id(p): [] for p in (mmA, mmB, yA, yB, psS, psT, psU, psV)}

    S = Sched(nc, es)

    def TS(out, in0, s1, op):
        return lambda e: e.tensor_scalar(out, in0, s1, None, op)

    def TS2(out, in0, s1, s2, op0, op1):
        return lambda e: e.tensor_scalar(out, in0, s1, s2, op0, op1)

    def TT(out, a, b, op):
        return lambda e: e.tensor_tensor(out, a, b, op)

    def STT(out, in0, sc, in1, op0, op1):
        return lambda e: e.scalar_tensor_tensor(out, in0, sc, in1, op0, op1)

    def CP(out, in_):
        return lambda e: e.tensor_copy(out, in_)

    def ACT(out, in_, f, bias=0.0, scale=1.0):
        return lambda e: e.activation(out, in_, f, bias=bias, scale=scale)

    def MM(out, lhsT, rhs, start, stop):
        return lambda e: e.matmul(out, lhsT, rhs, start=start, stop=stop)

    def LDW(w):
        return lambda e: e.ldweights(w)

    def MSET(ap, v):
        return lambda e: e.memset(ap, v)

    def SCAN(out, d0, d1, init):
        return lambda e: e.tensor_tensor_scan(out, d0, d1, init, ALU.mult, ALU.add)

    def ps_use(ps, deps=()):
        d = tuple(ps_cons[id(ps)]) + tuple(deps)
        ps_cons[id(ps)] = []
        return d

    def ps_done(ps, tok):
        ps_cons[id(ps)].append(tok)

    cvals = [-1.0, EPS, 1.0 / D_MODEL, np.pi / 2, 2 * np.pi, np.pi, -0.5, -6.0, 6.0, -2.0, 1.0 / (2 * np.pi), 0.0]
    tk_c = None
    for j, v in enumerate(cvals):
        tk_c = S.do("vector", MSET(consts[:, j:j + 1], float(v)))
    C_NEG1, C_EPS, C_INV, C_HPI, C_2PI, C_PI, C_NHALF, C_N6, C_P6, C_N2, C_I2PI = range(11)
    tk_ones = S.do("vector", MSET(ones_c[:, :], 1.0))
    tk_oc = S.do("vector", MSET(onescol[:, :], 1.0))
    tk_of = S.do("vector", MSET(onesf[:, :], 1.0))

    tk_x = tuple(S.dma("gpsimd", x_sb[i][:, :], d_x[i * P:(i + 1) * P, :]) for i in range(NFT))

    t_i1 = S.do("gpsimd", lambda e: e.iota(iota1, [[1, P]], channel_multiplier=0))
    t_i2 = S.do("gpsimd", lambda e: e.iota(iota2, [[0, P]], channel_multiplier=1))
    t_msk = S.do("vector", TT(diagmask[:, :], iota1, iota2, ALU.is_equal), (t_i1, t_i2))
    t_id0 = S.do("vector", MSET(ident[:, :], 0.0))
    t_id = S.do("vector", lambda e: e.copy_predicated(
        ident[:, :], diagmask[:, :], onescol[:, 0:1].broadcast_to([P, P])), (t_msk, t_id0, tk_oc))

    tk_wm = S.dma("sync", wm_col[:, :], d_warm.partition_broadcast(P))
    tk_dtw = S.dma("sync", dtw_sb[:, :], d_dtw)
    tk_dtb = S.dma("sync", dtb_sb[:, :], d_dtb)
    tk_mD = S.dma("sync", mD_sb[:, :], d_mD)
    tk_cb = S.dma("sync", cb_sb[:, :], d_cb)
    tk_bo = S.dma("sync", bout_sb[:, :], d_bout)
    tk_fb = S.dma("sync", fb_sb[:, :], d_fb)
    tk_alog = S.dma("sync", facc[:, 0:256], d_alog)
    t_a = S.do("vector", TS(facc[:, 256:512], facc[:, 0:256], consts[:, C_P6:C_P6 + 1], ALU.min), (tk_alog, tk_c))
    t_a = S.do("vector", TS(a_sb[:, :], facc[:, 256:512], consts[:, C_N6:C_N6 + 1], ALU.max), (t_a,))
    t_a = S.do("scalar", ACT(facc[:, 0:256], a_sb[:, :], AF.Exp), (t_a,))
    tk_a = S.do("vector", TS(a_sb[:, :], facc[:, 0:256], consts[:, C_NEG1:C_NEG1 + 1], ALU.mult), (t_a,))

    tk_xpl = S.dma("sync", cast_st[:, 0:512], d_xpw[:, 0:512])
    tk_xp0 = S.do("vector", CP(xpw_t[:, 0:512], cast_st[:, 0:512]), (tk_xpl,))
    tk_xpl2 = S.dma("sync", cast_st[:, 0:16], d_xpw[:, 512:528], deps=(tk_xp0,))
    tk_xp = S.do("vector", CP(xpw_t[:, 512:528], cast_st[:, 0:16]), (tk_xpl2, tk_xp0))

    tl1 = S.dma("sync", lrow[0:1, 0:64], d_llr)
    tl2 = S.dma("sync", lrow[0:1, 64:128], d_lim)
    tl3 = S.dma("sync", lrow[0:1, 128:129], d_ldt)
    p_dt = S.do("scalar", ACT(lrow[0:1, 129:130], lrow[0:1, 128:129], AF.Exp), (tl3,))
    t_dt = p_dt
    p_e1 = S.do("scalar", ACT(lrow[0:1, 192:256], lrow[0:1, 0:64], AF.Exp), (tl1,))
    p_m1 = S.do("vector", TS(lrow[0:1, 256:320], lrow[0:1, 192:256], lrow[0:1, 129:130], ALU.mult), (p_e1, p_dt))
    p_lrdt = S.do("vector", TS(lrow[0:1, 320:384], lrow[0:1, 256:320], consts[0:1, C_NEG1:C_NEG1 + 1], ALU.mult), (p_m1, tk_c))
    p_mag = S.do("scalar", ACT(lrow[0:1, 384:448], lrow[0:1, 320:384], AF.Exp), (p_lrdt,))
    p_th0 = S.do("vector", TS(lrow[0:1, 448:512], lrow[0:1, 64:128], lrow[0:1, 129:130], ALU.mult), (tl2, p_dt))
    p_q = S.do("vector", TS(lrow[0:1, 512:576], lrow[0:1, 448:512], consts[0:1, C_I2PI:C_I2PI + 1], ALU.mult), (p_th0,))
    p_qi = S.do("vector", CP(lrow_i[0:1, :], lrow[0:1, 512:576]), (p_q,))
    p_qf = S.do("vector", CP(lrow[0:1, 576:640], lrow_i[0:1, :]), (p_qi,))
    p_q2 = S.do("vector", TS(lrow[0:1, 640:704], lrow[0:1, 576:640], consts[0:1, C_2PI:C_2PI + 1], ALU.mult), (p_qf,))
    p_th = S.do("vector", TT(lrow[0:1, 704:768], lrow[0:1, 448:512], lrow[0:1, 640:704], ALU.subtract), (p_q2, p_th0))
    p_gt = S.do("vector", lambda e: e.tensor_scalar(stat_r[0:1, 448:512], lrow[0:1, 704:768],
                                                    consts[0:1, C_PI:C_PI + 1], None, ALU.is_gt), (p_th,))
    p_g2 = S.do("vector", TS(stat_r[0:1, 512:576], stat_r[0:1, 448:512], consts[0:1, C_2PI:C_2PI + 1], ALU.mult), (p_gt,))
    p_thr = S.do("vector", TT(lrow[0:1, 0:64], lrow[0:1, 704:768], stat_r[0:1, 512:576], ALU.subtract), (p_g2, p_th, p_e1))
    p_sn = S.do("scalar", ACT(lrow[0:1, 64:128], lrow[0:1, 0:64], AF.Sin), (p_thr, p_th0))
    p_ab = S.do("scalar", ACT(stat_r[0:1, 0:64], lrow[0:1, 0:64], AF.Abs), (p_thr,))
    p_pm = S.do("vector", TS2(stat_r[0:1, 64:128], stat_r[0:1, 0:64], consts[0:1, C_NEG1:C_NEG1 + 1],
                              consts[0:1, C_HPI:C_HPI + 1], ALU.mult, ALU.add), (p_ab,))
    p_cs = S.do("scalar", ACT(stat_r[0:1, 128:192], stat_r[0:1, 64:128], AF.Sin), (p_pm,))
    p_l1r = S.do("vector", TT(rowsc[0:1, 0:64], stat_r[0:1, 128:192], lrow[0:1, 384:448], ALU.mult), (p_cs, p_mag))
    p_l1i = S.do("vector", TT(rowsc[0:1, 64:128], lrow[0:1, 64:128], lrow[0:1, 384:448], ALU.mult), (p_sn, p_mag))
    p_im2 = S.do("scalar", lambda e: e.activation(stat_r[0:1, 192:256], lrow[0:1, 320:384], AF.Exp,
                                                  scale=consts[0:1, C_N2:C_N2 + 1]), (p_lrdt,))
    p_lm1r = S.do("vector", TT(stat_r[0:1, 256:320], rowsc[0:1, 0:64], stat_r[0:1, 192:256], ALU.mult), (p_im2, p_l1r))
    p_m3 = S.do("vector", TT(stat_r[0:1, 320:384], rowsc[0:1, 64:128], stat_r[0:1, 192:256], ALU.mult), (p_im2, p_l1i))
    p_lm1i = S.do("vector", TS(stat_r[0:1, 384:448], stat_r[0:1, 320:384], consts[0:1, C_NEG1:C_NEG1 + 1], ALU.mult), (p_m3,))
    tkr1 = S.dma("gpsimd", d_rows[0:1, 0:64], rowsc[0:1, 0:64], deps=(p_l1r,))
    tkr2 = S.dma("gpsimd", d_rows[0:1, 64:128], rowsc[0:1, 64:128], deps=(p_l1i,))
    tkr3 = S.dma("gpsimd", d_rows[0:1, 128:192], stat_r[0:1, 256:320], deps=(p_lm1r,))
    tkr4 = S.dma("gpsimd", d_rows[0:1, 192:256], stat_r[0:1, 384:448], deps=(p_lm1i,))
    tc1 = S.dma("sync", lam_sc[0:64, 0:1], d_rows[0:1, 0:64].transpose([1, 0]), deps=(tkr1,))
    tc2 = S.dma("sync", lam_sc[0:64, 1:2], d_rows[0:1, 64:128].transpose([1, 0]), deps=(tkr2,))
    tc3 = S.dma("sync", lam_sc[0:64, 2:3], d_rows[0:1, 128:192].transpose([1, 0]), deps=(tkr3,))
    tc4 = S.dma("sync", lam_sc[0:64, 3:4], d_rows[0:1, 192:256].transpose([1, 0]), deps=(tkr4,))

    def cplx_scale(dst_r, dst_i, src_r, src_i, sc_r, sc_i, deps):
        n = dst_r.shape[-1]
        t1 = S.do("vector", TS(pow_scr[0:64, 0:n], src_i, sc_i, ALU.mult), deps)
        t2 = S.do("vector", TS(dst_r, src_r, sc_r, ALU.mult), (t1,))
        t3 = S.do("vector", TT(dst_r, dst_r, pow_scr[0:64, 0:n], ALU.subtract), (t2, t1))
        t4 = S.do("vector", TS(pow_scr[0:64, 0:n], src_r, sc_i, ALU.mult), (t3,))
        t5 = S.do("vector", TS(dst_i, src_i, sc_r, ALU.mult), (t4,))
        t6 = S.do("vector", TT(dst_i, dst_i, pow_scr[0:64, 0:n], ALU.add), (t5, t4))
        return t6

    IM = 129

    def build_pow(tab, sc0, deps):
        t0 = S.do("vector", MSET(tab[0:64, 0:1], 1.0), deps)
        t1 = S.do("vector", MSET(tab[0:64, IM:IM + 1], 0.0), deps)
        t2 = S.do("vector", CP(lam_sc[0:64, 4:6], lam_sc[0:64, sc0:sc0 + 2]), deps)
        last = S.do("vector", lambda e: e.nop(), (t0, t1, t2))
        width = 1
        while width < 129:
            n = min(width, 129 - width)
            last = cplx_scale(tab[0:64, width:width + n], tab[0:64, IM + width:IM + width + n],
                              tab[0:64, 0:n], tab[0:64, IM:IM + n],
                              lam_sc[0:64, 4:5], lam_sc[0:64, 5:6], (last,))
            if width * 2 < 129:
                last = cplx_scale(lam_sc[0:64, 6:7], lam_sc[0:64, 7:8],
                                  lam_sc[0:64, 4:5], lam_sc[0:64, 5:6],
                                  lam_sc[0:64, 4:5], lam_sc[0:64, 5:6], (last,))
                last = S.do("vector", CP(lam_sc[0:64, 4:6], lam_sc[0:64, 6:8]), (last,))
            width *= 2
        return last

    tk_lp = build_pow(lamP, 0, (tc1, tc2))
    tk_ln = build_pow(lamN, 2, (tc3, tc4, tk_lp))
    if _DBG:
        S.dma("gpsimd", d_dbglp, lamP[0:64, :], deps=(tk_lp, tk_ln))
        S.dma("gpsimd", d_dbgln, lamN[0:64, :], deps=(tk_ln,))
        S.dma("gpsimd", d_dbglsc, lam_sc[0:64, :], deps=(tk_ln,))
        S.dma("gpsimd", d_dbglrow, lrow[0:1, :], deps=(tk_ln,))

    tk_dc0 = S.dma("gpsimd", d_rows[0:1, 256:257], lrow[0:1, 129:130], deps=(t_dt,))
    tk_dc = S.dma("sync", dcol[:, 0:1], d_rows[0:1, 256:257].partition_broadcast(P), deps=(tk_dc0,))
    tk_bst = []
    prev = tk_xp
    for f in range(NFT):
        t1 = S.dma("sync", cast_st[:, 0:64], d_brT[f * P:(f + 1) * P, :], deps=(prev,))
        t2 = S.dma("sync", cast_st[:, 64:128], d_biT[f * P:(f + 1) * P, :], deps=(t1,))
        t3 = S.do("vector", TS(cast_st[:, 0:128], cast_st[:, 0:128], dcol[:, 0:1], ALU.mult), (t1, t2, tk_dc))
        prev = S.do("vector", CP(bst(f), cast_st[:, 0:128]), (t3,))
        tk_bst.append(prev)
    tk_cct = prev
    for half in range(2):
        t1 = S.dma("sync", cast_st[0:64, 0:512], d_crT[:, half * 512:(half + 1) * 512], deps=(tk_cct,))
        tk_cct = S.do("vector", CP(cctR[:, half * 512:(half + 1) * 512], cast_st[0:64, 0:512]), (t1,))
        t2 = S.dma("sync", cast_st[0:64, 0:512], d_ciT[:, half * 512:(half + 1) * 512], deps=(tk_cct,))
        t3 = S.do("vector", TS(cast_st[0:64, 0:512], cast_st[0:64, 0:512], consts[0:64, C_NEG1:C_NEG1 + 1], ALU.mult), (t2,))
        tk_cct = S.do("vector", CP(cctIN[:, half * 512:(half + 1) * 512], cast_st[0:64, 0:512]), (t3,))
    tk_ad = []
    prevz = None
    tk_adl = None
    for f in range(NFT):
        if f % 4 == 0:
            tk_adl = S.dma("sync", cast_st[0:1, 0:512], d_aD[0:1, (f // 4) * 512:(f // 4 + 1) * 512],
                           deps=(tk_cct,) + ((prevz,) if prevz else ()))
        zz = S.wdma(d_rows[0:1, 0:128], cast_st[0:1, (f % 4) * P:(f % 4 + 1) * P],
                   deps=(tk_adl,) + ((prevz,) if prevz else ()))
        zc = S.dma("sync", dcol[:, 1:2], d_rows[0:1, 0:128].transpose([1, 0]), deps=(zz,))
        z1 = S.do("vector", MSET(ftmp[:, 0:128], 0.0), (zc,))
        z2 = S.do("vector", lambda e: e.copy_predicated(
            ftmp[:, 0:128], diagmask[:, :], dcol[:, 1:2].broadcast_to([P, P])), (z1, t_msk))
        tok = S.do("vector", CP(addiag[:, f * P:(f + 1) * P], ftmp[:, 0:128]), (z2,))
        tk_ad.append(tok)
        prevz = tok

    t_sa1 = S.do("gpsimd", lambda e: e.iota(iota1[0:8, :], [[0, P]], channel_multiplier=1), (t_id, tk_ad[-1]))
    t_sa2 = S.do("gpsimd", lambda e: e.iota(iota2[0:8, :], [[1, 8], [0, 16]], channel_multiplier=0), (tk_ad[-1],))
    t_sa3 = S.do("vector", TT(iota1[0:8, :], iota1[0:8, :], iota2[0:8, :], ALU.is_equal), (t_sa1, t_sa2))
    t_z = S.do("vector", MSET(selA[:, :], 0.0))
    t_sa = S.do("vector", lambda e: e.copy_predicated(selA[:, :], iota1[0:8, :],
                                                      onescol[0:8, 0:1].broadcast_to([8, P])), (t_sa3, t_z, tk_oc))
    t_sb1 = S.do("gpsimd", lambda e: e.iota(iota1[0:16, :], [[0, P]], channel_multiplier=1), (t_sa,))
    t_sb2 = S.do("gpsimd", lambda e: e.iota(iota2[0:16, :], [[0, 8], [1, 16]], channel_multiplier=0), (t_sa,))
    t_sb3 = S.do("vector", TT(iota1[0:16, :], iota1[0:16, :], iota2[0:16, :], ALU.is_equal), (t_sb1, t_sb2))
    t_z = S.do("vector", MSET(selB[:, :], 0.0))
    t_sb = S.do("vector", lambda e: e.copy_predicated(selB[:, :], iota1[0:16, :],
                                                      onescol[0:16, 0:1].broadcast_to([16, P])), (t_sb3, t_z, tk_oc))

    def build_ffn_w(dst_tiles, dg1, dg2, deps):
        l1 = S.dma("sync", g1_sb[:, :], dg1, deps=deps)
        l2 = S.dma("sync", g2_sb[:, :], dg2, deps=deps)
        tt = []
        for b in range(4):
            tt.append(S.do("vector", lambda e, b_=b: e.transpose(
                g1t_sb[b_ * 32:(b_ + 1) * 32, 0:32], g1_sb[0:32, b_ * 32:(b_ + 1) * 32]), (l1,)))
        last = None
        w2 = w4 = m1 = m2 = None
        for tile in range(4):
            accum_last = None
            for c in range(4):
                base = c * 32 + tile * 8
                c1 = S.wdma(gsl[0:8, :], g1t_sb[base:base + 8, 0:32], deps=tuple(tt) + ((m1,) if m1 else ()))
                c2 = S.wdma(gsl2[0:16, :], g2_sb[c * 16:(c + 1) * 16, :], deps=(l2,) + ((m2,) if m2 else ()))
                m1 = S.do("vector", CP(gslb[0:8, :], gsl[0:8, :]), (c1,) + ((w2,) if w2 else ()))
                m2 = S.do("vector", CP(gslb2[0:16, :], gsl2[0:16, :]), (c2,) + ((w4,) if w4 else ()))
                dA_ = ps_use(mmA, (t_sa, m1))
                w1 = S.do("tensor", LDW(selA[0:8, :]), dA_)
                w2 = S.do("tensor", MM(mmA[:, 0:32], selA[0:8, :], gslb[0:8, 0:32], True, True), (w1,))
                e1 = S.do("vector", CP(a1ev[:, 0:32], mmA[:, 0:32]), (w2,))
                ps_done(mmA, e1)
                dB_ = ps_use(mmB, (t_sb, m2))
                w3 = S.do("tensor", LDW(selB[0:16, :]), dB_ + (e1,))
                w4 = S.do("tensor", MM(mmB[:, 0:16], selB[0:16, :], gslb2[0:16, 0:16], True, True), (w3,))
                e2 = S.do("vector", CP(a2ev[:, 0:16], mmB[:, 0:16]), (w4,))
                ps_done(mmB, e2)
                a1r = a1ev[:, 0:32].unsqueeze(2).broadcast_to([P, 32, 16])
                a2r = a2ev[:, 0:16].unsqueeze(1).broadcast_to([P, 32, 16])
                if c == 0:
                    accum_last = S.do("vector", TT(facc[:, 0:512].rearrange("p (b e) -> p b e", e=16),
                                                   a1r, a2r, ALU.mult), (e1, e2, tk_a))
                else:
                    t_ = S.do("vector", TT(ftmp[:, 0:512].rearrange("p (b e) -> p b e", e=16),
                                           a1r, a2r, ALU.mult), (e1, e2))
                    accum_last = S.do("vector", TT(facc[:, 0:512], facc[:, 0:512], ftmp[:, 0:512], ALU.add), (t_,))
            last = S.do("vector", CP(dst_tiles[tile], facc[:, 0:512]), (accum_last,))
        return last

    tk_fr = build_ffn_w([fwr(i) for i in range(4)], d_g1r, d_g2r, (t_sb,))
    tk_fi = build_ffn_w([fwi(i) for i in range(4)], d_g1i, d_g2i, (tk_fr,))
    tk_fn = tk_fi
    for i in range(4):
        tk_fn = S.do("vector", TS(fwin(i), fwi(i), consts[:, C_NEG1:C_NEG1 + 1], ALU.mult), (tk_fn,))
    if _DBG:
        S.dma("gpsimd", d_dbgfw, fw_t[:, :], deps=(tk_fn,))
        S.dma("gpsimd", d_dbgg1t, g1t_sb[:, :], deps=(tk_fn,))

    cwinF = bbc_t.ap().bitcast(F32)
    cst_in = [cwinF[:, i * 512:(i + 1) * 512] for i in range(4)]
    cst_out = [cbc_t[:, i * 512:(i + 1) * 512] for i in range(4)]
    cslot = {"cp": [None] * 4, "outs": [()] * 4, "i": 0}

    def cast_weight(dsrc, ddst, n_k, n_m, row_len, deps=()):
        toks = []
        sub = row_len // 512
        for k in range(n_k):
            for s_ in range(sub):
                sl = cslot["i"] % 4
                cslot["i"] += 1
                q = ("sync", "scalar")[sl % 2]
                l1 = S.dma(q, cst_in[sl], dsrc[k * P:(k + 1) * P, s_ * 512:(s_ + 1) * 512],
                           deps=deps + ((cslot["cp"][sl],) if cslot["cp"][sl] else ()))
                c1 = S.do("vector", CP(cst_out[sl], cst_in[sl]), (l1,) + cslot["outs"][sl])
                cslot["cp"][sl] = c1
                cslot["outs"][sl] = ()
                for m in range(4):
                    gm = s_ * 4 + m
                    if gm >= n_m:
                        break
                    t_ = S.wdma(ddst[k, gm], cst_out[sl][:, m * P:(m + 1) * P], deps=(c1,))
                    toks.append(t_)
                    cslot["outs"][sl] += (t_,)
        return toks

    wcast_deps = (tk_cct, tk_bst[-1], tk_xp, tk_adl)
    ct1 = cast_weight(d_inw, d_winp, NFT, 32, 4096)
    ct2 = cast_weight(d_opw, d_wop, NDT, NFT, 1024)
    ct3 = cast_weight(d_woT, d_wwo, NFT, NFT, 1024)
    tk_cwl = S.dma("sync", cast_st[:, 0:64], d_cw, deps=wcast_deps)
    ct4 = []
    prevz = None
    for dtile in range(NDT):
        for tap in range(4):
            z1 = S.do("vector", MSET(ftmp[:, 0:128], 0.0), (tk_cwl,) + ((prevz,) if prevz else ()))
            z2 = S.do("vector", lambda e, d=dtile, tp=tap: e.copy_predicated(
                ftmp[:, 0:128], diagmask[:, :],
                cast_st[:, d * 4 + tp:d * 4 + tp + 1].broadcast_to([P, P])), (z1, t_msk))
            z3 = S.do("vector", CP(cast_st2[:, 0:128], ftmp[:, 0:128]), (z2,) + ((prevz,) if prevz else ()))
            prevz = S.wdma(d_wcv[dtile, tap], cast_st2[:, 0:128], deps=(z3,))
            ct4.append(prevz)
    CASTS = (ct1[-1], ct2[-1], ct3[-1], ct4[-1])

    S.barrier(("tensor", "vector", "scalar", "gpsimd", "sync"))
    S.mark_small = False

    ln_guard = {"xsq": None, "mr": None, "stat": None, "lr": None}

    def ln_stats(c0, w, extra=(), dstf=None):
        dstf = dstf or xn
        cp_toks, mm_toks = [], []
        lrg = (ln_guard["lr"],) if ln_guard["lr"] else ()
        g = tuple(t for t in (ln_guard["xsq"], ln_guard["mr"], ln_guard["stat"]) if t)
        dA_ = ps_use(psS, extra + g)
        dB_ = ps_use(psT, ())
        t_lw = S.do("tensor", LDW(onescol[:, 0:1]), dA_ + dB_ + (tk_oc,))
        for ft in range(NFT):
            cpt = S.do("vector", CP(dstf(ft)[:, 0:w], x_sb[ft][:, c0:c0 + w]), extra + g + tk_x)
            war = (mm_toks[-1],) if mm_toks else ()
            sqt = S.do("vector", TT(xsq[:, 0:w], dstf(ft)[:, 0:w], dstf(ft)[:, 0:w], ALU.mult), (cpt,) + war)
            m1 = S.do("tensor", MM(psS[0:1, 0:w], onescol[:, 0:1], dstf(ft)[:, 0:w], ft == 0, ft == NFT - 1), (cpt, t_lw))
            m2 = S.do("tensor", MM(psT[0:1, 0:w], onescol[:, 0:1], xsq[:, 0:w], ft == 0, ft == NFT - 1), (sqt, t_lw))
            cp_toks.append(cpt)
            mm_toks += [m1, m2]
        ln_guard["xsq"] = mm_toks[-1]
        _ms = S.mark_small
        S.mark_small = True
        r1 = S.do("vector", TS(stat_r[0:1, 0:w], psS[0:1, 0:w], consts[0:1, C_INV:C_INV + 1], ALU.mult), tuple(mm_toks))
        r2 = S.do("vector", TS(stat_r[0:1, BW:BW + w], psT[0:1, 0:w], consts[0:1, C_INV:C_INV + 1], ALU.mult), tuple(mm_toks))
        ps_done(psS, r1)
        ps_done(psT, r2)
        r3 = S.do("vector", TT(rowsc[0:1, 0:w], stat_r[0:1, 0:w], stat_r[0:1, 0:w], ALU.mult), (r1,))
        r4 = S.do("vector", TT(rowsc[0:1, 0:w], stat_r[0:1, BW:BW + w], rowsc[0:1, 0:w], ALU.subtract), (r2, r3))
        r5 = S.do("scalar", lambda e, w_=w: e.activation(lrow[0:1, 0:w_], rowsc[0:1, 0:w_], AF.Ln,
                                                         bias=consts[0:1, C_EPS:C_EPS + 1]), (r4,) + lrg)
        r6 = S.do("scalar", lambda e, w_=w: e.activation(rowsc[0:1, 0:w_], lrow[0:1, 0:w_], AF.Exp,
                                                         scale=consts[0:1, C_NHALF:C_NHALF + 1]), (r5,))
        ln_guard["lr"] = r6
        r7 = S.do("vector", CP(rowb16[0:1, 0:w], rowsc[0:1, 0:w]), (r6,))
        r8 = S.do("vector", CP(rowb16b[0:1, 0:w], stat_r[0:1, 0:w]), (r6, r1))
        ln_guard["stat"] = r8
        S.mark_small = _ms
        return (cp_toks, r7, r8, g)

    def ln_apply(c0, w, st, dstf=None):
        dstf = dstf or xn
        cp_toks, r7, r8, g = st
        dA_ = ps_use(psS, (r7, r8))
        dB_ = ps_use(psT, ())
        b1 = S.do("tensor", LDW(ones_c[0:1, :]), dA_ + dB_ + (tk_ones,))
        b2 = S.do("tensor", MM(psS[:, 0:w], ones_c[0:1, :], rowb16b[0:1, 0:w], True, True), (b1,))
        b3 = S.do("tensor", MM(psT[:, 0:w], ones_c[0:1, :], rowb16[0:1, 0:w], True, True), (b2,))
        e1 = S.do("scalar", ACT(m_bc[:, 0:w], psS[:, 0:w], AF.Copy), (b2,) + g)
        e2 = S.do("scalar", ACT(r_bc[:, 0:w], psT[:, 0:w], AF.Copy), (b3,) + g)
        ps_done(psS, e1)
        ps_done(psT, e2)
        out = []
        for ft in range(NFT):
            s1 = S.do("vector", TT(dstf(ft)[:, 0:w], dstf(ft)[:, 0:w], m_bc[:, 0:w], ALU.subtract), (e1, cp_toks[ft]))
            s2 = S.do("vector", TT(dstf(ft)[:, 0:w], dstf(ft)[:, 0:w], r_bc[:, 0:w], ALU.mult), (s1, e2))
            out.append(s2)
        ln_guard["mr"] = out[-1]
        S.minibar()
        return out

    def layernorm(c0, w, extra=(), dstf=None):
        st = ln_stats(c0, w, extra, dstf)
        return ln_apply(c0, w, st, dstf)

    SI = BW + 1
    t_s0 = S.do("vector", MSET(s_sb[0:64, 0:1], 0.0))
    t_s0b = S.do("vector", MSET(s_sb[0:64, SI:SI + 1], 0.0))
    xcv = lambda i: xc_t[:, i * BW:(i + 1) * BW]
    sza = lambda i: sz_t[:, i * BW:(i + 1) * BW]
    xn_gA = {0: (), 1: ()}
    ya_gA = {0: (), 1: ()}
    sb16_cons = ()
    NCHa = len(CHUNKS)
    lnA = {0: layernorm(CHUNKS[0][0], CHUNKS[0][1], (), dstf=xn)}
    wo_grp = {}

    def yab(cc):
        return ya if cc % 2 == 0 else sza

    def dsta(cc):
        return xn if cc % 2 == 0 else xcv

    def emit_wout(cc):
        cwo = CHUNKS[cc][1]
        yv = yab(cc)
        last = None
        for do_ in range(NFT):
            ps = yA if do_ % 2 == 0 else yB
            dps = ps_use(ps, ())
            ldt_ = None
            mlast = None
            for k in range(NFT):
                j, buf, wdeps = st_wo.get()
                tw = S.wdma(buf[:, :], d_wwo[k, do_], wdeps + (CASTS[2],))
                ldt_ = S.do("tensor", LDW(buf[:, :]), dps + (tw,) + ((ldt_,) if ldt_ else ()))
                mlast = S.do("tensor", MM(ps[:, 0:cwo], buf[:, :], yv(k)[:, 0:cwo], k == 0, k == NFT - 1),
                             (ldt_, ya_tk[(cc, k)]))
                st_wo.used(j, mlast)
            wo_grp[(cc, do_)] = mlast
            last = mlast
        ya_gA[cc % 2] = (last,)

    def emit_evr_a(cc, o):
        cwo = CHUNKS[cc][0:2]
        p0a, pwa = cwo
        ps = yA if o % 2 == 0 else yB
        evr = S.do("vector", STT(x_sb[o][:, p0a:p0a + pwa], ps[:, 0:pwa], bout_sb[:, o:o + 1],
                                 x_sb[o][:, p0a:p0a + pwa], ALU.add, ALU.add), (wo_grp[(cc, o)], tk_bo))
        ps_done(ps, evr)

    ya_tk = {}
    for ci, (c0, w) in enumerate(CHUNKS):
        ln = lnA[ci]
        dstf = dsta(ci)
        dA_ = ps_use(mmA, tuple(ln))
        dB_ = ps_use(mmB, tuple(ln))
        ldt_ = None
        mmr = mmi = None
        for f in range(NFT):
            ldt_ = S.do("tensor", LDW(bst(f)[:, 0:64]), dA_ + dB_ + (tk_bst[f],) + ((ldt_,) if ldt_ else ()))
            mmr = S.do("tensor", MM(mmA[0:64, 0:w], bst(f)[:, 0:64], dstf(f)[:, 0:w], f == 0, f == NFT - 1), (ldt_, ln[f]))
            ldt_ = S.do("tensor", LDW(bst(f)[:, 64:128]), (mmr,))
            mmi = S.do("tensor", MM(mmB[0:64, 0:w], bst(f)[:, 64:128], dstf(f)[:, 0:w], f == 0, f == NFT - 1), (ldt_,))
        if ci >= 1:
            emit_wout(ci - 1)
        evrq = list(range(NFT)) if ci >= 1 else []
        ev_bu = S.do("vector", CP(bu[0:64, 0:w], mmA[0:64, 0:w]), (mmr, mmi, t_s0, t_s0b))
        ev_bu2 = S.do("vector", CP(bu[0:64, BW:BW + w], mmB[0:64, 0:w]), (ev_bu,))
        ps_done(mmA, ev_bu)
        ps_done(mmB, ev_bu2)
        S.minibar()
        off = 0
        prev = ev_bu2
        _ms = S.mark_small
        S.mark_small = True
        for cw in subchunks(w):
            aR, bR = 1 + off, 1 + off + cw
            aI, bI = SI + 1 + off, SI + 1 + off + cw
            buR = bu[0:64, off:off + cw]
            buI = bu[0:64, BW + off:BW + off + cw]
            t1 = S.do("vector", TT(pow_scr[0:64, 0:cw], buI, lamN[0:64, IM:IM + cw], ALU.mult), (prev, tk_ln) + sb16_cons)
            t2 = S.do("vector", TT(s_sb[0:64, aR:bR], buR, lamN[0:64, 0:cw], ALU.mult), (t1,))
            t3 = S.do("vector", TT(s_sb[0:64, aR:bR], s_sb[0:64, aR:bR], pow_scr[0:64, 0:cw], ALU.subtract), (t2,))
            t4 = S.do("vector", TT(pow_scr[0:64, 0:cw], buR, lamN[0:64, IM:IM + cw], ALU.mult), (t3,))
            t5 = S.do("vector", TT(s_sb[0:64, aI:bI], buI, lamN[0:64, 0:cw], ALU.mult), (t4,))
            t6 = S.do("vector", TT(s_sb[0:64, aI:bI], s_sb[0:64, aI:bI], pow_scr[0:64, 0:cw], ALU.add), (t5,))
            csR = S.do("vector", SCAN(s_sb[0:64, aR:bR], onesf[0:64, 0:cw], s_sb[0:64, aR:bR], 0.0), (t6, tk_of))
            csI = S.do("vector", SCAN(s_sb[0:64, aI:bI], onesf[0:64, 0:cw], s_sb[0:64, aI:bI], 0.0), (csR,))
            q1 = S.do("vector", TT(pow_scr[0:64, 0:cw], s_sb[0:64, aR:bR], lamP[0:64, IM:IM + cw], ALU.mult), (csI, tk_lp))
            q2 = S.do("vector", TT(pow_scr[0:64, 128:128 + cw], s_sb[0:64, aI:bI], lamP[0:64, IM:IM + cw], ALU.mult), (q1,))
            q3 = S.do("vector", TT(s_sb[0:64, aR:bR], s_sb[0:64, aR:bR], lamP[0:64, 0:cw], ALU.mult), (q2,))
            q4 = S.do("vector", TT(s_sb[0:64, aR:bR], s_sb[0:64, aR:bR], pow_scr[0:64, 128:128 + cw], ALU.subtract), (q3,))
            q5 = S.do("vector", TT(s_sb[0:64, aI:bI], s_sb[0:64, aI:bI], lamP[0:64, 0:cw], ALU.mult), (q4,))
            q6 = S.do("vector", TT(s_sb[0:64, aI:bI], s_sb[0:64, aI:bI], pow_scr[0:64, 0:cw], ALU.add), (q5,))
            k1 = S.do("vector", TS(pow_scr[0:64, 0:cw], lamP[0:64, IM + 1:IM + 1 + cw], s_sb[0:64, aI - 1:aI], ALU.mult), (q6,))
            k2 = S.do("vector", STT(s_sb[0:64, aR:bR], lamP[0:64, 1:1 + cw], s_sb[0:64, aR - 1:aR], s_sb[0:64, aR:bR], ALU.mult, ALU.add), (k1,))
            k3 = S.do("vector", TT(s_sb[0:64, aR:bR], s_sb[0:64, aR:bR], pow_scr[0:64, 0:cw], ALU.subtract), (k2,))
            k4 = S.do("vector", TS(pow_scr[0:64, 0:cw], lamP[0:64, IM + 1:IM + 1 + cw], s_sb[0:64, aR - 1:aR], ALU.mult), (k3,))
            k5 = S.do("vector", STT(s_sb[0:64, aI:bI], lamP[0:64, 1:1 + cw], s_sb[0:64, aI - 1:aI], s_sb[0:64, aI:bI], ALU.mult, ALU.add), (k4,))
            k6 = S.do("vector", TT(s_sb[0:64, aI:bI], s_sb[0:64, aI:bI], pow_scr[0:64, 0:cw], ALU.add), (k5,))
            prev = k6
            off += cw
            for _ in range(2):
                if evrq:
                    emit_evr_a(ci - 1, evrq.pop(0))
        S.mark_small = _ms
        while evrq:
            emit_evr_a(ci - 1, evrq.pop(0))
        if _DBG and ci == 1:
            S.dma("gpsimd", d_dbgs, s_sb[0:64, :], deps=(prev,))
            S.dma("gpsimd", d_dbgb, bu[0:64, :], deps=(prev,))
        S.minibar()
        t_st = S.do("vector", CP(s_sb[0:64, 0:1], s_sb[0:64, w:w + 1]), (prev,), small=True)
        t_st2 = S.do("vector", CP(s_sb[0:64, SI:SI + 1], s_sb[0:64, SI + w:SI + w + 1]), (prev,), small=True)
        t_sb16 = S.do("vector", CP(sb16[0:64, 0:w], s_sb[0:64, 1:1 + w]), (prev,) + sb16_cons)
        t_sb16b = S.do("vector", CP(sb16[0:64, BW:BW + w], s_sb[0:64, SI + 1:SI + 1 + w]), (t_sb16,))
        if ci + 1 < NCHa:
            lnA[ci + 1] = layernorm(CHUNKS[ci + 1][0], CHUNKS[ci + 1][1],
                                    xn_gA[(ci + 1) % 2], dstf=dsta(ci + 1))
        yv = yab(ci)
        m2l = None
        for f in range(NFT):
            dB_ = ps_use(mmB, (t_sb16b,))
            l1 = S.do("tensor", LDW(cctR[:, f * P:(f + 1) * P]), dB_ + (tk_cct,))
            m1 = S.do("tensor", MM(mmB[:, 0:w], cctR[:, f * P:(f + 1) * P], sb16[0:64, 0:w], True, False), (l1,))
            l2 = S.do("tensor", LDW(cctIN[:, f * P:(f + 1) * P]), (m1,))
            m1b = S.do("tensor", MM(mmB[:, 0:w], cctIN[:, f * P:(f + 1) * P], sb16[0:64, BW:BW + w], False, False), (l2,))
            l3 = S.do("tensor", LDW(addiag[:, f * P:(f + 1) * P]), (m1b, tk_ad[f]))
            m2 = S.do("tensor", MM(mmB[:, 0:w], addiag[:, f * P:(f + 1) * P], dstf(f)[:, 0:w], False, True), (l3, ln[f]))
            ev = S.do("scalar", ACT(yv(f)[:, 0:w], mmB[:, 0:w], AF.Copy), (m2,) + ya_gA[ci % 2])
            ps_done(mmB, ev)
            ya_tk[(ci, f)] = ev
            m2l = m2
        S.minibar()
        xn_gA[ci % 2] = (m2l,)
        sb16_cons = (m2l,)
        if _DBG and ci == 1:
            S.dma("gpsimd", d_dbgy, ya_t[:, :], deps=tuple(ya_tk[(ci, f)] for f in range(NFT)))
    emit_wout(NCHa - 1)
    for o in range(NFT):
        emit_evr_a(NCHa - 1, o)
    S.barrier()

    def dump_resid(j, deps):
        if not _DBG:
            return
        for i_ in range(NFT):
            S.dma("gpsimd", d_dbg[j][i_ * P:(i_ + 1) * P, :], x_sb[i_][:, :], deps=deps)

    dump_resid(0, S.barrier())

    def ffn_layer(chunks, write_out):
        out_toks = []
        guard = {0: (), 1: ()}
        NCF = len(chunks)
        dsts = [xn if i % 2 == 0 else ya for i in range(NCF)]
        lnt = {0: layernorm(chunks[0][0], chunks[0][1], (), dstf=dsts[0])}
        st_nx = {}
        for ci, (c0, w) in enumerate(chunks):
            ln = lnt[ci]
            dstf = dsts[ci]
            nxt = ci + 1 if ci + 1 < NCF else None
            last_mm = None
            for o in range(NFT):
                ps = yA if o % 2 == 0 else yB
                dps = ps_use(ps, tuple(ln))
                real_out = o < 4
                osub = o % 4
                ldt_ = None
                mlast = None
                for k in range(NFT):
                    if real_out:
                        wt = (fwr(k) if k < 4 else fwin(k - 4))
                    else:
                        wt = (fwi(k) if k < 4 else fwr(k - 4))
                    wsl = wt[:, osub * P:(osub + 1) * P]
                    ldt_ = S.do("tensor", LDW(wsl), dps + (tk_fn,) + ((ldt_,) if ldt_ else ()))
                    mlast = S.do("tensor", MM(ps[:, 0:w], wsl, dstf(k)[:, 0:w], k == 0, k == NFT - 1), (ldt_, ln[k]))
                evr = S.do("vector", STT(x_sb[o][:, c0:c0 + w], ps[:, 0:w], fb_sb[:, o:o + 1],
                                         x_sb[o][:, c0:c0 + w], ALU.add, ALU.add), (mlast, tk_fb))
                ps_done(ps, evr)
                last_mm = mlast
                if nxt is not None:
                    if o == 2:
                        st_nx[nxt] = ln_stats(chunks[nxt][0], chunks[nxt][1], guard[nxt % 2], dsts[nxt])
                    elif o == 5:
                        lnt[nxt] = ln_apply(chunks[nxt][0], chunks[nxt][1], st_nx[nxt], dsts[nxt])
                if write_out:
                    out_toks.append(S.dma("gpsimd", d_out[o * P:(o + 1) * P, c0 - W_WARM:c0 - W_WARM + w],
                                          x_sb[o][:, c0:c0 + w], deps=(evr,)))
            guard[ci % 2] = (last_mm,)
        return out_toks

    ffn_layer(CHUNKS, False)
    dump_resid(1, S.barrier())

    t_tl0 = S.do("vector", MSET(tails[:, :], 0.0))
    xie_g = [() for _ in range(2)]
    da_g = [None, None, None]
    g_g = [None, None, None]
    fdb = Rot(flat_db)
    NCH = len(CHUNKS)

    lnB = {}
    xc_toks = {}
    sz_toks = {}
    x1_toks = {}
    u17_toks = {}
    gate_toks = {}
    xproj_mm = {}
    opk_last = {}
    opgrp = {}
    inp_last = {}
    scan_end = {}
    BCx = {}
    PV = {"bo1": None}

    def start_inp_xi(cc, m):
        cw = CHUNKS[cc][1]
        psI = mmA if m % 2 == 0 else mmB
        dps = ps_use(psI, ())
        ldt_ = None
        mlast = None
        for k in range(NFT):
            j, buf, wdeps = st_inp.get()
            tw = S.wdma(buf[:, :], d_winp[k, m], wdeps + (CASTS[0],))
            ldt_ = S.do("tensor", LDW(buf[:, :]), dps + (tw,) + ((ldt_,) if ldt_ else ()))
            mlast = S.do("tensor", MM(psI[:, 0:cw], buf[:, :], xn(k)[:, 0:cw], k == 0, k == NFT - 1),
                         (ldt_, lnB[cc][k]))
            st_inp.used(j, mlast)
        inp_last[cc] = mlast
        return (m, mlast, psI)

    def emit_conv_part(cc, pend):
        m, mlast, psI, ev = pend
        cw = CHUNKS[cc][1]
        jx = m % 2
        xbuf = xi_ext[jx]
        dpc = ps_use(psI, (ev,))
        clast = None
        cl = None
        for tap in range(4):
            jc, cbuf, cdeps = st_cv.get()
            tw = S.wdma(cbuf[:, :], d_wcv[m, tap], cdeps + (CASTS[3],))
            cl = S.do("tensor", LDW(cbuf[:, :]), dpc + (tw,) + ((cl,) if cl else ()))
            clast = S.do("tensor", MM(psI[:, 0:cw], cbuf[:, :], xbuf[:, tap:tap + cw], tap == 0, tap == 3),
                         (cl, ev))
            st_cv.used(jc, clast)
        xie_g[jx] = (clast,)
        return clast

    def emit_evc_z(cc, pend, clast):
        m, mlast, psI, ev = pend
        cw = CHUNKS[cc][1]
        if cc == 0:
            prv_rd = ()
        else:
            prv_rd = (u17_toks.get((cc - 1, m), x1_toks[(cc - 1, m)]), xproj_mm[cc - 1])
        evc = S.do("scalar", lambda e, m_=m, w_=cw, p_=psI: e.activation(
            xc(m_)[:, 0:w_], p_[:, 0:w_], AF.Silu, bias=cb_sb[:, m_:m_ + 1]),
            (clast, tk_cb) + prv_rd)
        ps_done(psI, evc)
        xc_toks[(cc, m)] = evc
        dps = ps_use(psI, ())
        ldt_ = None
        zlast = None
        for k in range(NFT):
            j, buf, wdeps = st_inp.get()
            tw = S.wdma(buf[:, :], d_winp[k, 16 + m], wdeps + (CASTS[0],))
            ldt_ = S.do("tensor", LDW(buf[:, :]), dps + (tw,) + ((ldt_,) if ldt_ else ()))
            zlast = S.do("tensor", MM(psI[:, 0:cw], buf[:, :], xn(k)[:, 0:cw], k == 0, k == NFT - 1),
                         (ldt_, lnB[cc][k]))
            st_inp.used(j, zlast)
        inp_last[cc] = zlast
        return zlast

    def emit_evz(cc, pend, zlast):
        m, mlast, psI, ev = pend
        cw = CHUNKS[cc][1]
        pz = (gate_toks[(cc - 1, m)],) if (cc - 1, m) in gate_toks else ()
        evz = S.do("scalar", lambda e, m_=m, w_=cw, p_=psI: e.activation(
            sz(m_)[:, 0:w_], p_[:, 0:w_], AF.Silu), (zlast,) + pz)
        ps_done(psI, evz)
        sz_toks[(cc, m)] = evz

    def emit_xi_evac(cc, pend0):
        m, mlast, psI = pend0
        cw = CHUNKS[cc][1]
        jx = m % 2
        xbuf = xi_ext[jx]
        tl = S.do("scalar", ACT(xbuf[:, 0:3], tails[:, m * 3:(m + 1) * 3], AF.Copy),
                  (t_tl0, mlast) + xie_g[jx], small=True)
        ev = S.do("scalar", ACT(xbuf[:, 3:3 + cw], psI[:, 0:cw], AF.Copy), (tl,))
        ps_done(psI, ev)
        S.do("scalar", ACT(tails[:, m * 3:(m + 1) * 3], xbuf[:, cw:cw + 3], AF.Copy),
             (ev,), small=True)
        return (m, mlast, psI, ev)

    def finish_inp_pair(cc, pend4, with_z=True):
        clast = emit_conv_part(cc, pend4)
        zlast = emit_evc_z(cc, pend4, clast) if with_z else None
        if not with_z:
            m, mlast, psI, ev = pend4
            cw = CHUNKS[cc][1]
            evc = S.do("scalar", lambda e, m_=m, w_=cw, p_=psI: e.activation(
                xc(m_)[:, 0:w_], p_[:, 0:w_], AF.Silu, bias=cb_sb[:, m_:m_ + 1]),
                (clast, tk_cb))
            ps_done(psI, evc)
            xc_toks[(cc, m)] = evc
            return
        emit_evz(cc, pend4, zlast)

    def emit_xproj(cc):
        cw = CHUNKS[cc][1]
        dps = ps_use(mmA, ())
        ldt_ = None
        mlast = None
        for k in range(NDT):
            ldt_ = S.do("tensor", LDW(xpw(k)), dps + (tk_xp,) + ((ldt_,) if ldt_ else ()))
            mlast = S.do("tensor", MM(mmA[0:33, 0:cw], xpw(k), xc(k)[:, 0:cw], k == 0, k == NDT - 1),
                         (ldt_, xc_toks[(cc, k)]))
        xproj_mm[cc] = mlast
        ev_r16 = S.do("vector", CP(rows16[:, 0:cw], mmA[0:33, 0:cw]),
                      (mlast,) + ((PV["bo1"],) if PV["bo1"] else ()))
        ps_done(mmA, ev_r16)
        PV["bo1"] = S.dma("gpsimd", d_rows16[0:33, 0:cw], rows16[:, 0:cw], deps=(ev_r16,))
        return PV["bo1"]

    def emit_bcast(cc, bo1):
        cw = CHUNKS[cc][1]
        bcc = (scan_end[cc - 1],) if cc >= 1 else ()
        dps = ps_use(mmB, bcc)
        lb = S.do("tensor", LDW(ones_c[0:1, :]), dps + (tk_ones,))
        n_rows = 17 if cc == 0 else 33
        bc_evs = []
        mprev = None
        for ri in range(n_rows):
            jf, fbuf, fdeps = fdb.get()
            td = S.dma("gpsimd", fbuf[0:1, 0:cw], d_rows16[ri:ri + 1, 0:cw], deps=(bo1,) + fdeps)
            mprev = S.do("tensor", MM(mmB[:, 0:cw], ones_c[0:1, :], fbuf[0:1, 0:cw], True, True),
                         ((lb,) if mprev is None else (mprev,)) + (td,) + ((bc_evs[-1],) if bc_evs else ()))
            fdb.used(jf, mprev)
            if ri == 0:
                dst = dtraw_bc[:, 0:cw]
            elif ri <= 16:
                dst = bbc(ri - 1)[:, 0:cw]
            else:
                dst = cbc(ri - 17)[:, 0:cw]
            bc_evs.append(S.do("scalar", ACT(dst, mmB[:, 0:cw], AF.Copy), (mprev,) + bcc))
        ps_done(mmB, bc_evs[-1])
        BCx[cc] = bc_evs[-1]

    def emit_out_proj(cc):
        c0p, cw = CHUNKS[cc]

        def one_group(o):
            psO = psU if o % 2 == 0 else psV
            dps = ps_use(psO, ())
            ldt_ = None
            mlast = None
            for k in range(NDT):
                j, buf, wdeps = st_op.get()
                tw = S.wdma(buf[:, :], d_wop[k, o], wdeps + (CASTS[1],))
                ldt_ = S.do("tensor", LDW(buf[:, :]), dps + (tw,) + ((ldt_,) if ldt_ else ()))
                mlast = S.do("tensor", MM(psO[:, 0:cw], buf[:, :], y2(k)[:, 0:cw], k == 0, k == NDT - 1),
                             (ldt_, gate_toks[(cc, k)]))
                st_op.used(j, mlast)
                if o == NFT - 1:
                    opk_last[(cc, k)] = mlast
            opgrp[(cc, o)] = mlast

        def one_evr(o):
            psO = psU if o % 2 == 0 else psV
            evr = S.do("vector", TT(x_sb[o][:, c0p:c0p + cw], psO[:, 0:cw], x_sb[o][:, c0p:c0p + cw], ALU.add),
                       (opgrp[(cc, o)],))
            ps_done(psO, evr)

        one_group(0)
        one_group(1)
        for o in range(2, NFT):
            one_evr(o - 2)
            one_group(o)
        one_evr(NFT - 2)
        one_evr(NFT - 1)

    def emit_scan(c):
        c0, cw = CHUNKS[c]
        warm = (c == 0)
        nxt = c + 1 if c + 1 < NCH else None
        BC = BCx[c]
        if c - 1 >= 1:
            emit_out_proj(c - 1)
        dt_guard = [None, None]
        d4_prev = [None]
        scan_last_v = None
        pend = [None]
        u_g = [None, None]
        uu_last = [None, None]
        chain = {}

        def emit_dt_chain(dtile):
            idt = dtile % 2
            d1 = S.do("vector", TS2(dt_b2[idt][:, 0:cw], dtraw_bc[:, 0:cw], dtw_sb[:, dtile:dtile + 1],
                                    dtb_sb[:, dtile:dtile + 1], ALU.mult, ALU.add),
                      (BC, tk_dtw, tk_dtb) + ((dt_guard[idt],) if dt_guard[idt] else ()))
            d2 = S.do("scalar", lambda e, i=idt, w_=cw: e.activation(
                dtl_t[:, 0:w_], dt_b2[i][:, 0:w_], AF.Exp, scale=consts[:, C_NEG1:C_NEG1 + 1]),
                (d1,) + ((d4_prev[0],) if d4_prev[0] else ()))
            d3 = S.do("scalar", lambda e, w_=cw: e.activation(
                xsq[:, 0:w_], dtl_t[:, 0:w_], AF.Ln, bias=1.0),
                (d2,) + ((ln_guard["xsq"],) if ln_guard["xsq"] else ()) + ((d4_prev[0],) if d4_prev[0] else ()))
            chain[dtile] = (d1, d2, d3)

        emit_dt_chain(0)
        for dtile in range(NDT):
            idt = dtile % 2
            d1, d2, d3 = chain[dtile]
            d4 = S.do("vector", TT(dt_b2[idt][:, 0:cw], dt_b2[idt][:, 0:cw], xsq[:, 0:cw], ALU.add), (d3, d1))
            d4_prev[0] = d4
            clast_p = emit_conv_part(nxt, pend[0]) if (nxt is not None and pend[0] is not None) else None
            p0_new = start_inp_xi(nxt, dtile) if nxt is not None else None
            x1 = S.do("vector", TT(dtx_b2[idt][:, 0:cw], dt_b2[idt][:, 0:cw], xc(dtile)[:, 0:cw], ALU.mult),
                      (d4, xc_toks[(c, dtile)]) + ((uu_last[idt],) if uu_last[idt] else ()))
            x1_toks[(c, dtile)] = x1
            if warm:
                x1 = S.do("vector", TS(dtx_b2[idt][:, 0:cw], dtx_b2[idt][:, 0:cw], wm_col[:, 0:1], ALU.mult),
                          (x1, tk_wm))
                psy = None
                dps = ()
            else:
                psy = yA if dtile % 2 == 0 else yB
                dps = ps_use(psy, ())
                t_idl = S.do("tensor", LDW(ident[:, :]), (t_id, BC))
            nmm = 0
            mmtok = None
            zlast_p = None
            pend_new = None
            for s in range(NS):
                if s == 2 and clast_p is not None:
                    zlast_p = emit_evc_z(nxt, pend[0], clast_p)
                if s == 8 and p0_new is not None:
                    pend_new = emit_xi_evac(nxt, p0_new)
                js = s % 3
                da = S.do("scalar", lambda e, i=idt, s_=s, d_=dtile, buf=da_b[js], w_=cw: e.activation(
                    buf[:, 0:w_], dt_b2[i][:, 0:w_], AF.Exp,
                    scale=a_sb[:, d_ * NS + s_:d_ * NS + s_ + 1]),
                    (d4, tk_a) + ((da_g[js],) if da_g[js] else ()))
                ju = s % 2
                uu = S.do("vector", TT(u_b[ju][:, 0:cw], dtx_b2[idt][:, 0:cw], bbc(s)[:, 0:cw], ALU.mult),
                          (x1, BC) + ((u_g[ju],) if u_g[ju] else ()))
                uu_last[idt] = uu
                hs = hpool[:, (s % 4) * BW:(s % 4) * BW + cw]
                init = 0.0 if c == 0 else states[:, dtile * NS + s:dtile * NS + s + 1]
                sc = S.do("vector", SCAN(hs, da_b[js][:, 0:cw], u_b[ju][:, 0:cw], init), (da, uu))
                da_g[js] = sc
                u_g[ju] = sc
                dt_guard[idt] = da
                if not warm:
                    gg = S.do("vector", TT(g_b[js][:, 0:cw], hs, cbc(s)[:, 0:cw], ALU.mult),
                              (sc,) + ((g_g[js],) if g_g[js] else ()))
                    if nmm == 0:
                        mmtok = S.do("tensor", MM(psy[:, 0:cw], ident[:, :], g_b[js][:, 0:cw], True, False),
                                     dps + (gg, t_idl))
                    else:
                        mmtok = S.do("tensor", MM(psy[:, 0:cw], ident[:, :], g_b[js][:, 0:cw], False, False),
                                     (gg, mmtok))
                    g_g[js] = mmtok
                    nmm += 1
                if s % 4 == 3:
                    q = s // 4
                    gath = S.do("vector", CP(states[:, dtile * NS + q * 4:dtile * NS + q * 4 + 4],
                                             hpool[:, cw - 1:cw - 1 + 3 * BW + 1:BW]), (sc,), small=True)
                    scan_last_v = gath
            if not warm:
                js17 = NS % 3
                u17 = S.do("vector", TS(g_b[js17][:, 0:cw], xc(dtile)[:, 0:cw], mD_sb[:, dtile:dtile + 1], ALU.mult),
                           (xc_toks[(c, dtile)], tk_mD) + ((g_g[js17],) if g_g[js17] else ()))
                u17_toks[(c, dtile)] = u17
                mmtok = S.do("tensor", MM(psy[:, 0:cw], ident[:, :], g_b[js17][:, 0:cw], False, True), (u17, mmtok))
                g_g[js17] = mmtok
                pgd = opk_last.get((c - 1, dtile))
                ev = S.do("vector", TT(y2(dtile)[:, 0:cw], psy[:, 0:cw], sz(dtile)[:, 0:cw], ALU.mult),
                          (mmtok, sz_toks[(c, dtile)]) + ((pgd,) if pgd else ()))
                ps_done(psy, ev)
                gate_toks[(c, dtile)] = ev
                scan_last_v = ev
            if zlast_p is not None:
                emit_evz(nxt, pend[0], zlast_p)
            if dtile + 1 < NDT:
                emit_dt_chain(dtile + 1)
            if pend_new is not None:
                pend[0] = pend_new
        if nxt is not None and pend[0] is not None:
            finish_inp_pair(nxt, pend[0])
        scan_end[c] = scan_last_v

    c0_, w_ = CHUNKS[0]
    lnB[0] = layernorm(c0_, w_, ())
    for m in range(16):
        p0g = start_inp_xi(0, m)
        p4g = emit_xi_evac(0, p0g)
        finish_inp_pair(0, p4g, with_z=False)
    emit_bcast(0, emit_xproj(0))
    lnB[1] = layernorm(CHUNKS[1][0], CHUNKS[1][1], (inp_last[0],))

    for c in range(NCH):
        emit_scan(c)
        if c + 1 < NCH:
            emit_bcast(c + 1, emit_xproj(c + 1))
            if c + 2 < NCH:
                lnB[c + 2] = layernorm(CHUNKS[c + 2][0], CHUNKS[c + 2][1], (inp_last[c + 1],))

    emit_out_proj(NCH - 1)
    if _DBG:
        dep = (opgrp[(NCH - 1, NFT - 1)], scan_end[NCH - 1])
        S.dma("gpsimd", d_dbgxc, xc_t[:, :], deps=dep)
        S.dma("gpsimd", d_dbgsz, sz_t[:, :], deps=dep)
        S.dma("gpsimd", d_dbgy2, y2_t[:, :], deps=dep)
        S.dma("gpsimd", d_dbgrw, rows16[:, :], deps=dep)
        S.dma("gpsimd", d_dbgst, states[:, :], deps=dep)
        S.dma("gpsimd", d_dbga, a_sb[:, :], deps=dep)
        S.dma("gpsimd", d_dbgdt, dtw_sb[:, :], deps=dep)
    dump_resid(2, S.barrier())

    out_toks = ffn_layer(CHUNKS[1:], True)
    S.do("gpsimd", lambda e: e.nop(), tuple(out_toks))

    with nc.Block() as blk:
        S.replay(blk)
    return nc


def kernel(**inputs):
    i = inputs
    x = np.asarray(i["L0_attn_a_x"])[0]
    xT = np.ascontiguousarray(x.T)

    def shard_x(k):
        lo = k * T_MAIN - W_WARM
        if lo < 0:
            pad = np.zeros((D_MODEL, -lo), dtype=np.float32)
            return np.ascontiguousarray(np.concatenate([pad, xT[:, 0:(k + 1) * T_MAIN]], axis=1))
        return np.ascontiguousarray(xT[:, lo:(k + 1) * T_MAIN])

    def cvt(a):
        return np.ascontiguousarray(np.asarray(a), dtype=np.float32)

    def colmaj(v, ntiles):
        return np.ascontiguousarray(cvt(v).reshape(ntiles, P).T)

    common = {
        "log_lam_real": cvt(i["L0_attn_a_log_Lam_real"]).reshape(1, 64),
        "lam_imag": cvt(i["L0_attn_a_Lam_imag"]).reshape(1, 64),
        "b_realT": np.ascontiguousarray(cvt(i["L0_attn_a_B_real"]).T),
        "b_imagT": np.ascontiguousarray(cvt(i["L0_attn_a_B_imag"]).T),
        "c_realT": np.ascontiguousarray(cvt(i["L0_attn_a_C_real"]).T),
        "c_imagT": np.ascontiguousarray(cvt(i["L0_attn_a_C_imag"]).T),
        "a_D": cvt(i["L0_attn_a_D"]).reshape(1, D_MODEL),
        "log_dt": cvt(i["L0_attn_a_log_dt"]).reshape(1, 1),
        "w_outT": np.ascontiguousarray(cvt(i["L0_attn_a_W_out"]).T),
        "b_out": colmaj(i["L0_attn_a_b_out"], NFT),
        "in_proj_w": cvt(i["L0_attn_b_in_proj_w"]),
        "conv_w": np.ascontiguousarray(cvt(i["L0_attn_b_conv_w"]).reshape(NDT, P, 4).transpose(1, 0, 2).reshape(P, 64)),
        "conv_b": colmaj(i["L0_attn_b_conv_b"], NDT),
        "x_proj_w": np.ascontiguousarray(cvt(i["L0_attn_b_x_proj_w"]).reshape(NDT, P, 33).transpose(1, 0, 2).reshape(P, NDT * 33)),
        "dt_w": colmaj(i["L0_attn_b_dt_w"], NDT),
        "dt_b": colmaj(i["L0_attn_b_dt_b"], NDT),
        "a_log": np.ascontiguousarray(cvt(i["L0_attn_b_A_log"]).reshape(NDT, P, NS).transpose(1, 0, 2).reshape(P, NDT * NS)),
        "m_D": colmaj(i["L0_attn_b_D"], NDT),
        "out_proj_w": cvt(i["L0_attn_b_out_proj_w"]),
        "g1r": np.ascontiguousarray(cvt(i["L0_ffn_G1_R"]).reshape(32, 128)),
        "g2r": np.ascontiguousarray(cvt(i["L0_ffn_G2_R"]).reshape(4, 16, 16).transpose(0, 2, 1).reshape(64, 16)),
        "g1i": np.ascontiguousarray(cvt(i["L0_ffn_G1_I"]).reshape(32, 128)),
        "g2i": np.ascontiguousarray(cvt(i["L0_ffn_G2_I"]).reshape(4, 16, 16).transpose(0, 2, 1).reshape(64, 16)),
        "ffn_bias": np.ascontiguousarray(
            np.concatenate([cvt(i["L0_ffn_bias_real"]), cvt(i["L0_ffn_bias_imag"])]).reshape(NFT, P).T),
    }
    in_maps = []
    for k in range(N_CORES):
        m = dict(common)
        m["x_in"] = shard_x(k)
        m["warm"] = np.asarray([[0.0 if k == 0 else 1.0]], dtype=np.float32)
        in_maps.append(m)

    nc = build_nc()
    res = run_bass_kernel_spmd(nc, in_maps, core_ids=list(range(N_CORES)))
    parts = [res.results[k]["out"] for k in range(N_CORES)]
    full = np.concatenate(parts, axis=1)
    return np.ascontiguousarray(full.T).reshape(1, 16384, D_MODEL)


if __name__ == "__main__":
    _nc = build_nc()
    print("build ok; instrs:", {e: len(_nc2) for e, _nc2 in []} if False else "done")


# version 36
# speedup vs baseline: 1.1199x; 1.0198x over previous
import contextlib
import numpy as np
import concourse.bass as bass
import concourse.mybir as mybir
from concourse.bass_utils import run_bass_kernel_spmd

F32 = mybir.dt.float32
BF16 = mybir.dt.bfloat16
I32 = mybir.dt.int32
AF = mybir.ActivationFunctionType
ALU = mybir.AluOpType

P = 128
W_WARM = 128
T_MAIN = 2048
T_TOT = W_WARM + T_MAIN
CHUNKS = [(0, 128), (128, 352), (480, 352), (832, 352), (1184, 352), (1536, 352), (1888, 288)]
BW = 352
D_MODEL = 1024
D_INNER = 2048
NFT = D_MODEL // P
NDT = D_INNER // P
NS = 16
EPS = 1e-5
N_CORES = 8

_DBG = False
_PARANOID = False
ENGINES = ("sync", "tensor", "vector", "scalar", "gpsimd")
WQ = ("sync", "scalar")


def subchunks(w):
    if w == 128:
        return [128]
    if w == 352:
        return [88, 88, 88, 88]
    if w == 288:
        return [96, 96, 96]
    raise ValueError(w)


class Sched:
    NQL = 8

    def __init__(self, nc, es):
        self.sem = {e: es.enter_context(nc.semaphore("s_" + e)) for e in ENGINES}
        self.qsem = {e: [es.enter_context(nc.semaphore(f"q_{e}{j}")) for j in range(self.NQL)]
                     for e in ("sync", "scalar", "gpsimd")}
        self.n = {e: 0 for e in ENGINES}
        self.q = {e: 0 for e in ENGINES}
        self.ops = {e: [] for e in ENGINES}
        self.last_wait = {e: {} for e in ENGINES}
        self.small_toks = set()
        self._wq_i = 0

    SELF_MARGIN = 8

    def _resolve(self, eng, deps):
        small_toks = self.small_toks
        need = {}
        for d in deps:
            if d is None:
                continue
            kind, e, k = d
            if kind == "c":
                if e == eng:
                    if k <= self.n[eng] - self.SELF_MARGIN or (e not in ("vector", "gpsimd") and (e, k) not in small_toks):
                        continue
                    key, sem, val = ("s", e), self.sem[e], k
                else:
                    key, sem, val = ("s", e), self.sem[e], k
            else:
                lane = (k - 1) % self.NQL
                key = ("q", e, lane)
                sem = self.qsem[e][lane]
                val = 16 * ((k - 1) // self.NQL + 1)
            cur = need.get(key)
            if cur is None or val > cur[1]:
                need[key] = (sem, val)
        waits = []
        for key, (sem, val) in need.items():
            if self.last_wait[eng].get(key, -1) >= val:
                continue
            self.last_wait[eng][key] = val
            waits.append((sem, val))
        return waits

    mark_small = True

    def do(self, eng, fn, deps=(), small=None):
        waits = self._resolve(eng, deps)
        self.ops[eng].append(("c", fn, waits))
        self.n[eng] += 1
        tok = ("c", eng, self.n[eng])
        if small is None:
            small = self.mark_small
        if small:
            self.small_toks.add((eng, self.n[eng]))
        return tok

    def dma(self, eng, out, in_, deps=()):
        waits = self._resolve(eng, deps)
        lane = self.q[eng] % self.NQL
        self.ops[eng].append(("d", (out, in_, lane), waits))
        self.q[eng] += 1
        return ("d", eng, self.q[eng])

    def wdma(self, out, in_, deps=()):
        e = WQ[self._wq_i % len(WQ)]
        self._wq_i += 1
        return self.dma(e, out, in_, deps)

    def barrier(self, engines=("tensor", "vector", "scalar", "gpsimd")):
        snap = tuple(("c", e, self.n[e]) for e in engines) + tuple(
            ("d", e, k) for e in ("sync", "scalar", "gpsimd") for k in range(max(1, self.q[e] - self.NQL + 1), self.q[e] + 1))
        return tuple(self.do(e, lambda x: x.nop(), snap) for e in engines)

    def minibar(self):
        if _PARANOID:
            self.barrier(("tensor", "vector", "scalar", "gpsimd"))

    def replay(self, blk):
        for e in ENGINES:
            mylist, mysem = self.ops[e], self.sem[e]
            myq = self.qsem.get(e)

            def make(section_list, sem, qs):
                def body(engine):
                    for kind, payload, waits in section_list:
                        for s, v in waits:
                            engine.wait_ge(s, v)
                        if kind == "c":
                            payload(engine).then_inc(sem, 1)
                        else:
                            out, in_, lane = payload
                            engine.dma_start(out=out, in_=in_).then_inc(qs[lane], 16)
                return body
            getattr(blk, e)(make(mylist, mysem, myq))


class Rot:
    def __init__(self, bufs):
        self.bufs = bufs
        self.cons = [[] for _ in bufs]
        self.i = 0

    def get(self):
        j = self.i % len(self.bufs)
        self.i += 1
        deps = tuple(self.cons[j])
        self.cons[j] = []
        return j, self.bufs[j], deps

    def used(self, j, tok):
        self.cons[j].append(tok)


def build_nc():
    nc = bass.Bass("TRN2", target_bir_lowering=False)
    es = contextlib.ExitStack()

    DT = nc.dram_tensor
    d_x = DT("x_in", [D_MODEL, T_TOT], F32, kind="ExternalInput").ap()
    d_warm = DT("warm", [1, 1], F32, kind="ExternalInput").ap()
    d_llr = DT("log_lam_real", [1, 64], F32, kind="ExternalInput").ap()
    d_lim = DT("lam_imag", [1, 64], F32, kind="ExternalInput").ap()
    d_brT = DT("b_realT", [D_MODEL, 64], F32, kind="ExternalInput").ap()
    d_biT = DT("b_imagT", [D_MODEL, 64], F32, kind="ExternalInput").ap()
    d_crT = DT("c_realT", [64, D_MODEL], F32, kind="ExternalInput").ap()
    d_ciT = DT("c_imagT", [64, D_MODEL], F32, kind="ExternalInput").ap()
    d_aD = DT("a_D", [1, D_MODEL], F32, kind="ExternalInput").ap()
    d_ldt = DT("log_dt", [1, 1], F32, kind="ExternalInput").ap()
    d_woT = DT("w_outT", [D_MODEL, D_MODEL], F32, kind="ExternalInput").ap()
    d_bout = DT("b_out", [P, NFT], F32, kind="ExternalInput").ap()
    d_inw = DT("in_proj_w", [D_MODEL, 2 * D_INNER], F32, kind="ExternalInput").ap()
    d_cw = DT("conv_w", [P, 64], F32, kind="ExternalInput").ap()
    d_cb = DT("conv_b", [P, NDT], F32, kind="ExternalInput").ap()
    d_xpw = DT("x_proj_w", [P, NDT * 33], F32, kind="ExternalInput").ap()
    d_dtw = DT("dt_w", [P, NDT], F32, kind="ExternalInput").ap()
    d_dtb = DT("dt_b", [P, NDT], F32, kind="ExternalInput").ap()
    d_alog = DT("a_log", [P, NDT * NS], F32, kind="ExternalInput").ap()
    d_mD = DT("m_D", [P, NDT], F32, kind="ExternalInput").ap()
    d_opw = DT("out_proj_w", [D_INNER, D_MODEL], F32, kind="ExternalInput").ap()
    d_g1r = DT("g1r", [32, 128], F32, kind="ExternalInput").ap()
    d_g2r = DT("g2r", [64, 16], F32, kind="ExternalInput").ap()
    d_g1i = DT("g1i", [32, 128], F32, kind="ExternalInput").ap()
    d_g2i = DT("g2i", [64, 16], F32, kind="ExternalInput").ap()
    d_fb = DT("ffn_bias", [P, NFT], F32, kind="ExternalInput").ap()
    d_out = DT("out", [D_MODEL, T_MAIN], F32, kind="ExternalOutput").ap()
    if _DBG:
        d_dbg = [DT(f"dbg{j}", [D_MODEL, T_TOT], F32, kind="ExternalOutput").ap() for j in range(3)]
        d_dbgs = DT("dbgs", [64, 2 * (BW + 1)], F32, kind="ExternalOutput").ap()
        d_dbgb = DT("dbgb", [64, 2 * BW], F32, kind="ExternalOutput").ap()
        d_dbgy = DT("dbgy", [P, NFT * BW], BF16, kind="ExternalOutput").ap()
        d_dbglp = DT("dbglp", [64, 258], F32, kind="ExternalOutput").ap()
        d_dbgln = DT("dbgln", [64, 258], F32, kind="ExternalOutput").ap()
        d_dbglsc = DT("dbglsc", [64, 16], F32, kind="ExternalOutput").ap()
        d_dbglrow = DT("dbglrow", [1, 768], F32, kind="ExternalOutput").ap()
        d_dbgfw = DT("dbgfw", [P, 3 * 2048], BF16, kind="ExternalOutput").ap()
        d_dbgg1t = DT("dbgg1t", [P, 32], F32, kind="ExternalOutput").ap()
        d_dbgxc = DT("dbgxc", [P, NDT * BW], BF16, kind="ExternalOutput").ap()
        d_dbgsz = DT("dbgsz", [P, NDT * BW], BF16, kind="ExternalOutput").ap()
        d_dbgy2 = DT("dbgy2", [P, NDT * BW], BF16, kind="ExternalOutput").ap()
        d_dbgrw = DT("dbgrw", [33, BW], BF16, kind="ExternalOutput").ap()
        d_dbgst = DT("dbgst", [P, NDT * NS], BF16, kind="ExternalOutput").ap()
        d_dbga = DT("dbga", [P, NDT * NS], F32, kind="ExternalOutput").ap()
        d_dbgdt = DT("dbgdt", [P, NDT], F32, kind="ExternalOutput").ap()

    d_winp = DT("scr_inproj", [NFT, 32, P, P], BF16).ap()
    d_wop = DT("scr_outproj", [NDT, NFT, P, P], BF16).ap()
    d_wwo = DT("scr_wout", [NFT, NFT, P, P], BF16).ap()
    d_wcv = DT("scr_conv", [NDT, 4, P, P], BF16).ap()
    d_rows = DT("scr_rows", [1, 512], F32).ap()
    d_rows16 = DT("scr_rows16", [33, 512], BF16).ap()

    A = nc.alloc_sbuf_tensor
    x_sb = [A(f"x{i}", [P, T_TOT], F32) for i in range(NFT)]

    def mk(name, ntiles, dtype=BF16, stride=BW):
        t = A(name, [P, ntiles * stride], dtype)
        return t, stride

    xn_t, _ = mk("xn", NFT)
    xn = lambda i: xn_t[:, i * BW:(i + 1) * BW]
    xsq = A("xsq", [P, BW], BF16)
    m_bc = A("m_bc", [P, BW], BF16)
    r_bc = A("r_bc", [P, BW], BF16)
    dtraw_bc = A("dtraw_bc", [P, BW], BF16)

    bst_t = A("bst", [P, NFT * P], BF16)
    bst = lambda f: bst_t[:, f * P:(f + 1) * P]
    cctR = A("cctR", [64, D_MODEL], BF16)
    cctIN = A("cctIN", [64, D_MODEL], BF16)
    addiag = A("addiag", [P, NFT * P], BF16)
    bu = A("bu", [64, 2 * BW], F32)
    s_sb = A("s_sb", [64, 2 * (BW + 1)], F32)
    sb16 = A("sb16", [64, 2 * BW], BF16)
    ya_t, _ = mk("ya", NFT)
    ya = lambda i: ya_t[:, i * BW:(i + 1) * BW]
    lamP = A("lamP", [64, 258], F32)
    lamN = A("lamN", [64, 258], F32)
    lam_sc = A("lam_sc", [64, 16], F32)
    pow_scr = A("pow_scr", [64, 256], F32)
    onesf = A("onesf", [P, 128], F32)

    fw_t = A("fw", [P, 3 * 2048], BF16)
    fwr = lambda k: fw_t[:, k * 512:(k + 1) * 512]
    fwi = lambda k: fw_t[:, 2048 + k * 512:2048 + (k + 1) * 512]
    fwin = lambda k: fw_t[:, 4096 + k * 512:4096 + (k + 1) * 512]

    xpw_t = A("xpw", [P, NDT * 33], BF16)
    xpw = lambda d: xpw_t[:, d * 33:(d + 1) * 33]
    xi_ext = [A(f"xiext{i}", [P, BW + 3], BF16) for i in range(2)]
    tails = A("tails", [P, 3 * NDT], BF16)
    xc_t, _ = mk("xc", NDT)
    xc = lambda d: xc_t[:, d * BW:(d + 1) * BW]
    sz_t, _ = mk("sz", NDT)
    sz = lambda d: sz_t[:, d * BW:(d + 1) * BW]
    y2_t, _ = mk("y2", NDT)
    y2 = lambda d: y2_t[:, d * BW:(d + 1) * BW]
    bbc_t, _ = mk("bbc", NS)
    bbc = lambda s: bbc_t[:, s * BW:(s + 1) * BW]
    cbc_t, _ = mk("cbc", NS)
    cbc = lambda s: cbc_t[:, s * BW:(s + 1) * BW]
    hpool = A("hpool", [P, 4 * BW], BF16)
    states = A("states", [P, NDT * NS], BF16)
    dt_b2 = [A(f"dtt{i}", [P, BW], BF16) for i in range(2)]
    dtl_t = A("dtl_t", [P, BW], BF16)
    dtx_b2 = [A(f"dtx{i}", [P, BW], BF16) for i in range(2)]
    da_b = [A(f"dab{i}", [P, BW], BF16) for i in range(3)]
    u_b = [A(f"ub{i}", [P, BW], BF16) for i in range(2)]
    g_b = [A(f"gb{i}", [P, BW], BF16) for i in range(3)]
    rows16 = A("rows16", [33, BW], BF16)
    flat_db = [A(f"flatdb{i}", [1, BW], BF16) for i in range(2)]
    dtw_sb = A("dtw_sb", [P, NDT], F32)
    dtb_sb = A("dtb_sb", [P, NDT], F32)
    a_sb = A("a_sb", [P, NDT * NS], F32)
    mD_sb = A("mD_sb", [P, NDT], F32)
    cb_sb = A("cb_sb", [P, NDT], F32)
    bout_sb = A("bout_sb", [P, NFT], F32)
    fb_sb = A("fb_sb", [P, NFT], F32)
    wm_col = A("wm_col", [P, 1], F32)

    cast_st = A("cast_st", [P, 512], F32)
    cast_st2 = A("cast_st2", [P, 512], BF16)
    _csb = cast_st.ap().bitcast(BF16)
    st_inp = Rot([A(f"stinp{i}", [P, P], BF16) for i in range(4)]
                 + [_csb[:, i * P:(i + 1) * P] for i in range(8)])
    st_op = Rot([A(f"stop{i}", [P, P], BF16) for i in range(4)]
                + [cast_st2[:, i * P:(i + 1) * P] for i in range(4)])
    st_wo = Rot([A(f"stwo{i}", [P, P], BF16) for i in range(3)])
    st_cv = Rot([A(f"stcv{i}", [P, P], BF16) for i in range(4)])

    consts = A("consts", [P, 12], F32)
    ones_c = A("ones_c", [1, P], BF16)
    onescol = A("onescol", [P, 1], BF16)
    ident = A("ident", [P, P], BF16)
    diagmask = A("diagmask", [P, P], I32)
    lrow = A("lrow", [1, 768], F32)
    dcol = A("dcol", [P, 2], F32)
    lrow_i = A("lrow_i", [1, 64], I32)
    stat_r = A("stat_r", [1, 2 * BW], F32)
    rowsc = A("rowsc", [1, BW], F32)
    rowb16 = A("rowb16", [1, BW], BF16)
    rowb16b = A("rowb16b", [1, BW], BF16)
    g1_sb = A("g1_sb", [32, 128], F32)
    g2_sb = A("g2_sb", [64, 16], F32)
    g1t_sb = A("g1t_sb", [P, 32], F32)
    selA = A("selA", [8, P], BF16)
    selB = A("selB", [16, P], BF16)
    gsl = A("gsl", [16, 32], F32)
    gsl2 = A("gsl2", [16, 16], F32)
    gslb = A("gslb", [16, 32], BF16)
    gslb2 = A("gslb2", [16, 16], BF16)
    a1ev = A("a1ev", [P, 32], F32)
    a2ev = A("a2ev", [P, 16], F32)
    ftmp = A("ftmp", [P, 512], F32)
    facc = A("facc", [P, 512], F32)
    iota1 = ftmp.ap()[:, 0:P].bitcast(I32)
    iota2 = facc.ap()[:, 0:P].bitcast(I32)

    PT = nc.alloc_psum_tensor
    mmA = PT("mmA", [P, 512], F32)
    mmB = PT("mmB", [P, 512], F32)
    yA = PT("yA", [P, 512], F32)
    yB = PT("yB", [P, 512], F32)
    psS = PT("psS", [P, 512], F32)
    psT = PT("psT", [P, 512], F32)
    psU = PT("psU", [P, 512], F32)
    psV = PT("psV", [P, 512], F32)
    ps_cons = {id(p): [] for p in (mmA, mmB, yA, yB, psS, psT, psU, psV)}

    S = Sched(nc, es)

    def TS(out, in0, s1, op):
        return lambda e: e.tensor_scalar(out, in0, s1, None, op)

    def TS2(out, in0, s1, s2, op0, op1):
        return lambda e: e.tensor_scalar(out, in0, s1, s2, op0, op1)

    def TT(out, a, b, op):
        return lambda e: e.tensor_tensor(out, a, b, op)

    def STT(out, in0, sc, in1, op0, op1):
        return lambda e: e.scalar_tensor_tensor(out, in0, sc, in1, op0, op1)

    def CP(out, in_):
        return lambda e: e.tensor_copy(out, in_)

    def ACT(out, in_, f, bias=0.0, scale=1.0):
        return lambda e: e.activation(out, in_, f, bias=bias, scale=scale)

    def MM(out, lhsT, rhs, start, stop):
        return lambda e: e.matmul(out, lhsT, rhs, start=start, stop=stop)

    def LDW(w):
        return lambda e: e.ldweights(w)

    def MSET(ap, v):
        return lambda e: e.memset(ap, v)

    def SCAN(out, d0, d1, init):
        return lambda e: e.tensor_tensor_scan(out, d0, d1, init, ALU.mult, ALU.add)

    def ps_use(ps, deps=()):
        d = tuple(ps_cons[id(ps)]) + tuple(deps)
        ps_cons[id(ps)] = []
        return d

    def ps_done(ps, tok):
        ps_cons[id(ps)].append(tok)

    cvals = [-1.0, EPS, 1.0 / D_MODEL, np.pi / 2, 2 * np.pi, np.pi, -0.5, -6.0, 6.0, -2.0, 1.0 / (2 * np.pi), 0.0]
    tk_c = None
    for j, v in enumerate(cvals):
        tk_c = S.do("vector", MSET(consts[:, j:j + 1], float(v)))
    C_NEG1, C_EPS, C_INV, C_HPI, C_2PI, C_PI, C_NHALF, C_N6, C_P6, C_N2, C_I2PI = range(11)
    tk_ones = S.do("vector", MSET(ones_c[:, :], 1.0))
    tk_oc = S.do("vector", MSET(onescol[:, :], 1.0))
    tk_of = S.do("vector", MSET(onesf[:, :], 1.0))

    tk_x = tuple(S.dma("gpsimd", x_sb[i][:, :], d_x[i * P:(i + 1) * P, :]) for i in range(NFT))

    t_i1 = S.do("gpsimd", lambda e: e.iota(iota1, [[1, P]], channel_multiplier=0))
    t_i2 = S.do("gpsimd", lambda e: e.iota(iota2, [[0, P]], channel_multiplier=1))
    t_msk = S.do("vector", TT(diagmask[:, :], iota1, iota2, ALU.is_equal), (t_i1, t_i2))
    t_id0 = S.do("vector", MSET(ident[:, :], 0.0))
    t_id = S.do("vector", lambda e: e.copy_predicated(
        ident[:, :], diagmask[:, :], onescol[:, 0:1].broadcast_to([P, P])), (t_msk, t_id0, tk_oc))

    tk_wm = S.dma("sync", wm_col[:, :], d_warm.partition_broadcast(P))
    tk_dtw = S.dma("sync", dtw_sb[:, :], d_dtw)
    tk_dtb = S.dma("sync", dtb_sb[:, :], d_dtb)
    tk_mD = S.dma("sync", mD_sb[:, :], d_mD)
    tk_cb = S.dma("sync", cb_sb[:, :], d_cb)
    tk_bo = S.dma("sync", bout_sb[:, :], d_bout)
    tk_fb = S.dma("sync", fb_sb[:, :], d_fb)
    tk_alog = S.dma("sync", facc[:, 0:256], d_alog)
    t_a = S.do("vector", TS(facc[:, 256:512], facc[:, 0:256], consts[:, C_P6:C_P6 + 1], ALU.min), (tk_alog, tk_c))
    t_a = S.do("vector", TS(a_sb[:, :], facc[:, 256:512], consts[:, C_N6:C_N6 + 1], ALU.max), (t_a,))
    t_a = S.do("scalar", ACT(facc[:, 0:256], a_sb[:, :], AF.Exp), (t_a,))
    tk_a = S.do("vector", TS(a_sb[:, :], facc[:, 0:256], consts[:, C_NEG1:C_NEG1 + 1], ALU.mult), (t_a,))

    tk_xpl = S.dma("sync", cast_st[:, 0:512], d_xpw[:, 0:512])
    tk_xp0 = S.do("vector", CP(xpw_t[:, 0:512], cast_st[:, 0:512]), (tk_xpl,))
    tk_xpl2 = S.dma("sync", cast_st[:, 0:16], d_xpw[:, 512:528], deps=(tk_xp0,))
    tk_xp = S.do("vector", CP(xpw_t[:, 512:528], cast_st[:, 0:16]), (tk_xpl2, tk_xp0))

    tl1 = S.dma("sync", lrow[0:1, 0:64], d_llr)
    tl2 = S.dma("sync", lrow[0:1, 64:128], d_lim)
    tl3 = S.dma("sync", lrow[0:1, 128:129], d_ldt)
    p_dt = S.do("scalar", ACT(lrow[0:1, 129:130], lrow[0:1, 128:129], AF.Exp), (tl3,))
    t_dt = p_dt
    p_e1 = S.do("scalar", ACT(lrow[0:1, 192:256], lrow[0:1, 0:64], AF.Exp), (tl1,))
    p_m1 = S.do("vector", TS(lrow[0:1, 256:320], lrow[0:1, 192:256], lrow[0:1, 129:130], ALU.mult), (p_e1, p_dt))
    p_lrdt = S.do("vector", TS(lrow[0:1, 320:384], lrow[0:1, 256:320], consts[0:1, C_NEG1:C_NEG1 + 1], ALU.mult), (p_m1, tk_c))
    p_mag = S.do("scalar", ACT(lrow[0:1, 384:448], lrow[0:1, 320:384], AF.Exp), (p_lrdt,))
    p_th0 = S.do("vector", TS(lrow[0:1, 448:512], lrow[0:1, 64:128], lrow[0:1, 129:130], ALU.mult), (tl2, p_dt))
    p_q = S.do("vector", TS(lrow[0:1, 512:576], lrow[0:1, 448:512], consts[0:1, C_I2PI:C_I2PI + 1], ALU.mult), (p_th0,))
    p_qi = S.do("vector", CP(lrow_i[0:1, :], lrow[0:1, 512:576]), (p_q,))
    p_qf = S.do("vector", CP(lrow[0:1, 576:640], lrow_i[0:1, :]), (p_qi,))
    p_q2 = S.do("vector", TS(lrow[0:1, 640:704], lrow[0:1, 576:640], consts[0:1, C_2PI:C_2PI + 1], ALU.mult), (p_qf,))
    p_th = S.do("vector", TT(lrow[0:1, 704:768], lrow[0:1, 448:512], lrow[0:1, 640:704], ALU.subtract), (p_q2, p_th0))
    p_gt = S.do("vector", lambda e: e.tensor_scalar(stat_r[0:1, 448:512], lrow[0:1, 704:768],
                                                    consts[0:1, C_PI:C_PI + 1], None, ALU.is_gt), (p_th,))
    p_g2 = S.do("vector", TS(stat_r[0:1, 512:576], stat_r[0:1, 448:512], consts[0:1, C_2PI:C_2PI + 1], ALU.mult), (p_gt,))
    p_thr = S.do("vector", TT(lrow[0:1, 0:64], lrow[0:1, 704:768], stat_r[0:1, 512:576], ALU.subtract), (p_g2, p_th, p_e1))
    p_sn = S.do("scalar", ACT(lrow[0:1, 64:128], lrow[0:1, 0:64], AF.Sin), (p_thr, p_th0))
    p_ab = S.do("scalar", ACT(stat_r[0:1, 0:64], lrow[0:1, 0:64], AF.Abs), (p_thr,))
    p_pm = S.do("vector", TS2(stat_r[0:1, 64:128], stat_r[0:1, 0:64], consts[0:1, C_NEG1:C_NEG1 + 1],
                              consts[0:1, C_HPI:C_HPI + 1], ALU.mult, ALU.add), (p_ab,))
    p_cs = S.do("scalar", ACT(stat_r[0:1, 128:192], stat_r[0:1, 64:128], AF.Sin), (p_pm,))
    p_l1r = S.do("vector", TT(rowsc[0:1, 0:64], stat_r[0:1, 128:192], lrow[0:1, 384:448], ALU.mult), (p_cs, p_mag))
    p_l1i = S.do("vector", TT(rowsc[0:1, 64:128], lrow[0:1, 64:128], lrow[0:1, 384:448], ALU.mult), (p_sn, p_mag))
    p_im2 = S.do("scalar", lambda e: e.activation(stat_r[0:1, 192:256], lrow[0:1, 320:384], AF.Exp,
                                                  scale=consts[0:1, C_N2:C_N2 + 1]), (p_lrdt,))
    p_lm1r = S.do("vector", TT(stat_r[0:1, 256:320], rowsc[0:1, 0:64], stat_r[0:1, 192:256], ALU.mult), (p_im2, p_l1r))
    p_m3 = S.do("vector", TT(stat_r[0:1, 320:384], rowsc[0:1, 64:128], stat_r[0:1, 192:256], ALU.mult), (p_im2, p_l1i))
    p_lm1i = S.do("vector", TS(stat_r[0:1, 384:448], stat_r[0:1, 320:384], consts[0:1, C_NEG1:C_NEG1 + 1], ALU.mult), (p_m3,))
    tkr1 = S.dma("gpsimd", d_rows[0:1, 0:64], rowsc[0:1, 0:64], deps=(p_l1r,))
    tkr2 = S.dma("gpsimd", d_rows[0:1, 64:128], rowsc[0:1, 64:128], deps=(p_l1i,))
    tkr3 = S.dma("gpsimd", d_rows[0:1, 128:192], stat_r[0:1, 256:320], deps=(p_lm1r,))
    tkr4 = S.dma("gpsimd", d_rows[0:1, 192:256], stat_r[0:1, 384:448], deps=(p_lm1i,))
    tc1 = S.dma("sync", lam_sc[0:64, 0:1], d_rows[0:1, 0:64].transpose([1, 0]), deps=(tkr1,))
    tc2 = S.dma("sync", lam_sc[0:64, 1:2], d_rows[0:1, 64:128].transpose([1, 0]), deps=(tkr2,))
    tc3 = S.dma("sync", lam_sc[0:64, 2:3], d_rows[0:1, 128:192].transpose([1, 0]), deps=(tkr3,))
    tc4 = S.dma("sync", lam_sc[0:64, 3:4], d_rows[0:1, 192:256].transpose([1, 0]), deps=(tkr4,))

    def cplx_scale(dst_r, dst_i, src_r, src_i, sc_r, sc_i, deps):
        n = dst_r.shape[-1]
        t1 = S.do("vector", TS(pow_scr[0:64, 0:n], src_i, sc_i, ALU.mult), deps)
        t2 = S.do("vector", TS(dst_r, src_r, sc_r, ALU.mult), (t1,))
        t3 = S.do("vector", TT(dst_r, dst_r, pow_scr[0:64, 0:n], ALU.subtract), (t2, t1))
        t4 = S.do("vector", TS(pow_scr[0:64, 0:n], src_r, sc_i, ALU.mult), (t3,))
        t5 = S.do("vector", TS(dst_i, src_i, sc_r, ALU.mult), (t4,))
        t6 = S.do("vector", TT(dst_i, dst_i, pow_scr[0:64, 0:n], ALU.add), (t5, t4))
        return t6

    IM = 129

    def build_pow(tab, sc0, deps):
        t0 = S.do("vector", MSET(tab[0:64, 0:1], 1.0), deps)
        t1 = S.do("vector", MSET(tab[0:64, IM:IM + 1], 0.0), deps)
        t2 = S.do("vector", CP(lam_sc[0:64, 4:6], lam_sc[0:64, sc0:sc0 + 2]), deps)
        last = S.do("vector", lambda e: e.nop(), (t0, t1, t2))
        width = 1
        while width < 129:
            n = min(width, 129 - width)
            last = cplx_scale(tab[0:64, width:width + n], tab[0:64, IM + width:IM + width + n],
                              tab[0:64, 0:n], tab[0:64, IM:IM + n],
                              lam_sc[0:64, 4:5], lam_sc[0:64, 5:6], (last,))
            if width * 2 < 129:
                last = cplx_scale(lam_sc[0:64, 6:7], lam_sc[0:64, 7:8],
                                  lam_sc[0:64, 4:5], lam_sc[0:64, 5:6],
                                  lam_sc[0:64, 4:5], lam_sc[0:64, 5:6], (last,))
                last = S.do("vector", CP(lam_sc[0:64, 4:6], lam_sc[0:64, 6:8]), (last,))
            width *= 2
        return last

    tk_lp = build_pow(lamP, 0, (tc1, tc2))
    tk_ln = build_pow(lamN, 2, (tc3, tc4, tk_lp))
    if _DBG:
        S.dma("gpsimd", d_dbglp, lamP[0:64, :], deps=(tk_lp, tk_ln))
        S.dma("gpsimd", d_dbgln, lamN[0:64, :], deps=(tk_ln,))
        S.dma("gpsimd", d_dbglsc, lam_sc[0:64, :], deps=(tk_ln,))
        S.dma("gpsimd", d_dbglrow, lrow[0:1, :], deps=(tk_ln,))

    tk_dc0 = S.dma("gpsimd", d_rows[0:1, 256:257], lrow[0:1, 129:130], deps=(t_dt,))
    tk_dc = S.dma("sync", dcol[:, 0:1], d_rows[0:1, 256:257].partition_broadcast(P), deps=(tk_dc0,))
    tk_bst = []
    prev = tk_xp
    for f in range(NFT):
        t1 = S.dma("sync", cast_st[:, 0:64], d_brT[f * P:(f + 1) * P, :], deps=(prev,))
        t2 = S.dma("sync", cast_st[:, 64:128], d_biT[f * P:(f + 1) * P, :], deps=(t1,))
        t3 = S.do("vector", TS(cast_st[:, 0:128], cast_st[:, 0:128], dcol[:, 0:1], ALU.mult), (t1, t2, tk_dc))
        prev = S.do("vector", CP(bst(f), cast_st[:, 0:128]), (t3,))
        tk_bst.append(prev)
    tk_cct = prev
    for half in range(2):
        t1 = S.dma("sync", cast_st[0:64, 0:512], d_crT[:, half * 512:(half + 1) * 512], deps=(tk_cct,))
        tk_cct = S.do("vector", CP(cctR[:, half * 512:(half + 1) * 512], cast_st[0:64, 0:512]), (t1,))
        t2 = S.dma("sync", cast_st[0:64, 0:512], d_ciT[:, half * 512:(half + 1) * 512], deps=(tk_cct,))
        t3 = S.do("vector", TS(cast_st[0:64, 0:512], cast_st[0:64, 0:512], consts[0:64, C_NEG1:C_NEG1 + 1], ALU.mult), (t2,))
        tk_cct = S.do("vector", CP(cctIN[:, half * 512:(half + 1) * 512], cast_st[0:64, 0:512]), (t3,))
    tk_ad = []
    prevz = None
    tk_adl = None
    for f in range(NFT):
        if f % 4 == 0:
            tk_adl = S.dma("sync", cast_st[0:1, 0:512], d_aD[0:1, (f // 4) * 512:(f // 4 + 1) * 512],
                           deps=(tk_cct,) + ((prevz,) if prevz else ()))
        zz = S.wdma(d_rows[0:1, 0:128], cast_st[0:1, (f % 4) * P:(f % 4 + 1) * P],
                   deps=(tk_adl,) + ((prevz,) if prevz else ()))
        zc = S.dma("sync", dcol[:, 1:2], d_rows[0:1, 0:128].transpose([1, 0]), deps=(zz,))
        z1 = S.do("vector", MSET(ftmp[:, 0:128], 0.0), (zc,))
        z2 = S.do("vector", lambda e: e.copy_predicated(
            ftmp[:, 0:128], diagmask[:, :], dcol[:, 1:2].broadcast_to([P, P])), (z1, t_msk))
        tok = S.do("vector", CP(addiag[:, f * P:(f + 1) * P], ftmp[:, 0:128]), (z2,))
        tk_ad.append(tok)
        prevz = tok

    t_sa1 = S.do("gpsimd", lambda e: e.iota(iota1[0:8, :], [[0, P]], channel_multiplier=1), (t_id, tk_ad[-1]))
    t_sa2 = S.do("gpsimd", lambda e: e.iota(iota2[0:8, :], [[1, 8], [0, 16]], channel_multiplier=0), (tk_ad[-1],))
    t_sa3 = S.do("vector", TT(iota1[0:8, :], iota1[0:8, :], iota2[0:8, :], ALU.is_equal), (t_sa1, t_sa2))
    t_z = S.do("vector", MSET(selA[:, :], 0.0))
    t_sa = S.do("vector", lambda e: e.copy_predicated(selA[:, :], iota1[0:8, :],
                                                      onescol[0:8, 0:1].broadcast_to([8, P])), (t_sa3, t_z, tk_oc))
    t_sb1 = S.do("gpsimd", lambda e: e.iota(iota1[0:16, :], [[0, P]], channel_multiplier=1), (t_sa,))
    t_sb2 = S.do("gpsimd", lambda e: e.iota(iota2[0:16, :], [[0, 8], [1, 16]], channel_multiplier=0), (t_sa,))
    t_sb3 = S.do("vector", TT(iota1[0:16, :], iota1[0:16, :], iota2[0:16, :], ALU.is_equal), (t_sb1, t_sb2))
    t_z = S.do("vector", MSET(selB[:, :], 0.0))
    t_sb = S.do("vector", lambda e: e.copy_predicated(selB[:, :], iota1[0:16, :],
                                                      onescol[0:16, 0:1].broadcast_to([16, P])), (t_sb3, t_z, tk_oc))

    def build_ffn_w(dst_tiles, dg1, dg2, deps):
        l1 = S.dma("sync", g1_sb[:, :], dg1, deps=deps)
        l2 = S.dma("sync", g2_sb[:, :], dg2, deps=deps)
        tt = []
        for b in range(4):
            tt.append(S.do("vector", lambda e, b_=b: e.transpose(
                g1t_sb[b_ * 32:(b_ + 1) * 32, 0:32], g1_sb[0:32, b_ * 32:(b_ + 1) * 32]), (l1,)))
        last = None
        w2 = w4 = m1 = m2 = None
        for tile in range(4):
            accum_last = None
            for c in range(4):
                base = c * 32 + tile * 8
                c1 = S.wdma(gsl[0:8, :], g1t_sb[base:base + 8, 0:32], deps=tuple(tt) + ((m1,) if m1 else ()))
                c2 = S.wdma(gsl2[0:16, :], g2_sb[c * 16:(c + 1) * 16, :], deps=(l2,) + ((m2,) if m2 else ()))
                m1 = S.do("vector", CP(gslb[0:8, :], gsl[0:8, :]), (c1,) + ((w2,) if w2 else ()))
                m2 = S.do("vector", CP(gslb2[0:16, :], gsl2[0:16, :]), (c2,) + ((w4,) if w4 else ()))
                dA_ = ps_use(mmA, (t_sa, m1))
                w1 = S.do("tensor", LDW(selA[0:8, :]), dA_)
                w2 = S.do("tensor", MM(mmA[:, 0:32], selA[0:8, :], gslb[0:8, 0:32], True, True), (w1,))
                e1 = S.do("vector", CP(a1ev[:, 0:32], mmA[:, 0:32]), (w2,))
                ps_done(mmA, e1)
                dB_ = ps_use(mmB, (t_sb, m2))
                w3 = S.do("tensor", LDW(selB[0:16, :]), dB_ + (e1,))
                w4 = S.do("tensor", MM(mmB[:, 0:16], selB[0:16, :], gslb2[0:16, 0:16], True, True), (w3,))
                e2 = S.do("vector", CP(a2ev[:, 0:16], mmB[:, 0:16]), (w4,))
                ps_done(mmB, e2)
                a1r = a1ev[:, 0:32].unsqueeze(2).broadcast_to([P, 32, 16])
                a2r = a2ev[:, 0:16].unsqueeze(1).broadcast_to([P, 32, 16])
                if c == 0:
                    accum_last = S.do("vector", TT(facc[:, 0:512].rearrange("p (b e) -> p b e", e=16),
                                                   a1r, a2r, ALU.mult), (e1, e2, tk_a))
                else:
                    t_ = S.do("vector", TT(ftmp[:, 0:512].rearrange("p (b e) -> p b e", e=16),
                                           a1r, a2r, ALU.mult), (e1, e2))
                    accum_last = S.do("vector", TT(facc[:, 0:512], facc[:, 0:512], ftmp[:, 0:512], ALU.add), (t_,))
            last = S.do("vector", CP(dst_tiles[tile], facc[:, 0:512]), (accum_last,))
        return last

    tk_fr = build_ffn_w([fwr(i) for i in range(4)], d_g1r, d_g2r, (t_sb,))
    tk_fi = build_ffn_w([fwi(i) for i in range(4)], d_g1i, d_g2i, (tk_fr,))
    tk_fn = tk_fi
    for i in range(4):
        tk_fn = S.do("vector", TS(fwin(i), fwi(i), consts[:, C_NEG1:C_NEG1 + 1], ALU.mult), (tk_fn,))
    if _DBG:
        S.dma("gpsimd", d_dbgfw, fw_t[:, :], deps=(tk_fn,))
        S.dma("gpsimd", d_dbgg1t, g1t_sb[:, :], deps=(tk_fn,))

    cwinF = bbc_t.ap().bitcast(F32)
    cst_in = [cwinF[:, i * 512:(i + 1) * 512] for i in range(4)]
    cst_out = [cbc_t[:, i * 512:(i + 1) * 512] for i in range(4)]
    cslot = {"cp": [None] * 4, "outs": [()] * 4, "i": 0}

    def cast_weight(dsrc, ddst, n_k, n_m, row_len, deps=()):
        toks = []
        sub = row_len // 512
        for k in range(n_k):
            for s_ in range(sub):
                sl = cslot["i"] % 4
                cslot["i"] += 1
                q = ("sync", "scalar")[sl % 2]
                l1 = S.dma(q, cst_in[sl], dsrc[k * P:(k + 1) * P, s_ * 512:(s_ + 1) * 512],
                           deps=deps + ((cslot["cp"][sl],) if cslot["cp"][sl] else ()))
                c1 = S.do("vector", CP(cst_out[sl], cst_in[sl]), (l1,) + cslot["outs"][sl])
                cslot["cp"][sl] = c1
                cslot["outs"][sl] = ()
                for m in range(4):
                    gm = s_ * 4 + m
                    if gm >= n_m:
                        break
                    t_ = S.wdma(ddst[k, gm], cst_out[sl][:, m * P:(m + 1) * P], deps=(c1,))
                    toks.append(t_)
                    cslot["outs"][sl] += (t_,)
        return toks

    wcast_deps = (tk_cct, tk_bst[-1], tk_xp, tk_adl)
    ct1 = cast_weight(d_inw, d_winp, NFT, 32, 4096)
    ct2 = cast_weight(d_opw, d_wop, NDT, NFT, 1024)
    ct3 = cast_weight(d_woT, d_wwo, NFT, NFT, 1024)
    tk_cwl = S.dma("sync", cast_st[:, 0:64], d_cw, deps=wcast_deps)
    ct4 = []
    prevz = None
    for dtile in range(NDT):
        for tap in range(4):
            z1 = S.do("vector", MSET(ftmp[:, 0:128], 0.0), (tk_cwl,) + ((prevz,) if prevz else ()))
            z2 = S.do("vector", lambda e, d=dtile, tp=tap: e.copy_predicated(
                ftmp[:, 0:128], diagmask[:, :],
                cast_st[:, d * 4 + tp:d * 4 + tp + 1].broadcast_to([P, P])), (z1, t_msk))
            z3 = S.do("vector", CP(cast_st2[:, 0:128], ftmp[:, 0:128]), (z2,) + ((prevz,) if prevz else ()))
            prevz = S.wdma(d_wcv[dtile, tap], cast_st2[:, 0:128], deps=(z3,))
            ct4.append(prevz)
    CASTS = (ct1[-1], ct2[-1], ct3[-1], ct4[-1])

    S.barrier(("tensor", "vector", "scalar", "gpsimd", "sync"))
    S.mark_small = False

    ln_guard = {"xsq": None, "mr": None, "stat": None, "lr": None}

    def ln_stats(c0, w, extra=(), dstf=None):
        dstf = dstf or xn
        cp_toks, mm_toks = [], []
        lrg = (ln_guard["lr"],) if ln_guard["lr"] else ()
        g = tuple(t for t in (ln_guard["xsq"], ln_guard["mr"], ln_guard["stat"]) if t)
        dA_ = ps_use(psS, extra + g)
        dB_ = ps_use(psT, ())
        t_lw = S.do("tensor", LDW(onescol[:, 0:1]), dA_ + dB_ + (tk_oc,))
        for ft in range(NFT):
            cpt = S.do("vector", CP(dstf(ft)[:, 0:w], x_sb[ft][:, c0:c0 + w]), extra + g + tk_x)
            war = (mm_toks[-1],) if mm_toks else ()
            sqt = S.do("vector", TT(xsq[:, 0:w], dstf(ft)[:, 0:w], dstf(ft)[:, 0:w], ALU.mult), (cpt,) + war)
            m1 = S.do("tensor", MM(psS[0:1, 0:w], onescol[:, 0:1], dstf(ft)[:, 0:w], ft == 0, ft == NFT - 1), (cpt, t_lw))
            m2 = S.do("tensor", MM(psT[0:1, 0:w], onescol[:, 0:1], xsq[:, 0:w], ft == 0, ft == NFT - 1), (sqt, t_lw))
            cp_toks.append(cpt)
            mm_toks += [m1, m2]
        ln_guard["xsq"] = mm_toks[-1]
        _ms = S.mark_small
        S.mark_small = True
        r1 = S.do("vector", TS(stat_r[0:1, 0:w], psS[0:1, 0:w], consts[0:1, C_INV:C_INV + 1], ALU.mult), tuple(mm_toks))
        r2 = S.do("vector", TS(stat_r[0:1, BW:BW + w], psT[0:1, 0:w], consts[0:1, C_INV:C_INV + 1], ALU.mult), tuple(mm_toks))
        ps_done(psS, r1)
        ps_done(psT, r2)
        r3 = S.do("vector", TT(rowsc[0:1, 0:w], stat_r[0:1, 0:w], stat_r[0:1, 0:w], ALU.mult), (r1,))
        r4 = S.do("vector", TT(rowsc[0:1, 0:w], stat_r[0:1, BW:BW + w], rowsc[0:1, 0:w], ALU.subtract), (r2, r3))
        r5 = S.do("scalar", lambda e, w_=w: e.activation(lrow[0:1, 0:w_], rowsc[0:1, 0:w_], AF.Ln,
                                                         bias=consts[0:1, C_EPS:C_EPS + 1]), (r4,) + lrg)
        r6 = S.do("scalar", lambda e, w_=w: e.activation(rowsc[0:1, 0:w_], lrow[0:1, 0:w_], AF.Exp,
                                                         scale=consts[0:1, C_NHALF:C_NHALF + 1]), (r5,))
        ln_guard["lr"] = r6
        r7 = S.do("vector", CP(rowb16[0:1, 0:w], rowsc[0:1, 0:w]), (r6,))
        r8 = S.do("vector", CP(rowb16b[0:1, 0:w], stat_r[0:1, 0:w]), (r6, r1))
        ln_guard["stat"] = r8
        S.mark_small = _ms
        return (cp_toks, r7, r8, g)

    def ln_apply(c0, w, st, dstf=None):
        dstf = dstf or xn
        cp_toks, r7, r8, g = st
        dA_ = ps_use(psS, (r7, r8))
        dB_ = ps_use(psT, ())
        b1 = S.do("tensor", LDW(ones_c[0:1, :]), dA_ + dB_ + (tk_ones,))
        b2 = S.do("tensor", MM(psS[:, 0:w], ones_c[0:1, :], rowb16b[0:1, 0:w], True, True), (b1,))
        b3 = S.do("tensor", MM(psT[:, 0:w], ones_c[0:1, :], rowb16[0:1, 0:w], True, True), (b2,))
        e1 = S.do("scalar", ACT(m_bc[:, 0:w], psS[:, 0:w], AF.Copy), (b2,) + g)
        e2 = S.do("scalar", ACT(r_bc[:, 0:w], psT[:, 0:w], AF.Copy), (b3,) + g)
        ps_done(psS, e1)
        ps_done(psT, e2)
        out = []
        for ft in range(NFT):
            s1 = S.do("vector", TT(dstf(ft)[:, 0:w], dstf(ft)[:, 0:w], m_bc[:, 0:w], ALU.subtract), (e1, cp_toks[ft]))
            s2 = S.do("vector", TT(dstf(ft)[:, 0:w], dstf(ft)[:, 0:w], r_bc[:, 0:w], ALU.mult), (s1, e2))
            out.append(s2)
        ln_guard["mr"] = out[-1]
        S.minibar()
        return out

    def layernorm(c0, w, extra=(), dstf=None):
        st = ln_stats(c0, w, extra, dstf)
        return ln_apply(c0, w, st, dstf)

    SI = BW + 1
    t_s0 = S.do("vector", MSET(s_sb[0:64, 0:1], 0.0))
    t_s0b = S.do("vector", MSET(s_sb[0:64, SI:SI + 1], 0.0))
    xcv = lambda i: xc_t[:, i * BW:(i + 1) * BW]
    sza = lambda i: sz_t[:, i * BW:(i + 1) * BW]
    xn_gA = {0: (), 1: ()}
    ya_gA = {0: (), 1: ()}
    sb16_cons = ()
    NCHa = len(CHUNKS)
    lnA = {0: layernorm(CHUNKS[0][0], CHUNKS[0][1], (), dstf=xn)}
    wo_grp = {}

    def yab(cc):
        return ya if cc % 2 == 0 else sza

    def dsta(cc):
        return xn if cc % 2 == 0 else xcv

    def emit_wout(cc):
        cwo = CHUNKS[cc][1]
        yv = yab(cc)
        last = None
        for do_ in range(NFT):
            ps = yA if do_ % 2 == 0 else yB
            dps = ps_use(ps, ())
            ldt_ = None
            mlast = None
            for k in range(NFT):
                j, buf, wdeps = st_wo.get()
                tw = S.wdma(buf[:, :], d_wwo[k, do_], wdeps + (CASTS[2],))
                ldt_ = S.do("tensor", LDW(buf[:, :]), dps + (tw,) + ((ldt_,) if ldt_ else ()))
                mlast = S.do("tensor", MM(ps[:, 0:cwo], buf[:, :], yv(k)[:, 0:cwo], k == 0, k == NFT - 1),
                             (ldt_, ya_tk[(cc, k)]))
                st_wo.used(j, mlast)
            wo_grp[(cc, do_)] = mlast
            last = mlast
        ya_gA[cc % 2] = (last,)

    def emit_evr_a(cc, o):
        cwo = CHUNKS[cc][0:2]
        p0a, pwa = cwo
        ps = yA if o % 2 == 0 else yB
        evr = S.do("vector", STT(x_sb[o][:, p0a:p0a + pwa], ps[:, 0:pwa], bout_sb[:, o:o + 1],
                                 x_sb[o][:, p0a:p0a + pwa], ALU.add, ALU.add), (wo_grp[(cc, o)], tk_bo))
        ps_done(ps, evr)

    ya_tk = {}
    for ci, (c0, w) in enumerate(CHUNKS):
        ln = lnA[ci]
        dstf = dsta(ci)
        dA_ = ps_use(mmA, tuple(ln))
        dB_ = ps_use(mmB, tuple(ln))
        ldt_ = None
        mmr = mmi = None
        for f in range(NFT):
            ldt_ = S.do("tensor", LDW(bst(f)[:, 0:64]), dA_ + dB_ + (tk_bst[f],) + ((ldt_,) if ldt_ else ()))
            mmr = S.do("tensor", MM(mmA[0:64, 0:w], bst(f)[:, 0:64], dstf(f)[:, 0:w], f == 0, f == NFT - 1), (ldt_, ln[f]))
            ldt_ = S.do("tensor", LDW(bst(f)[:, 64:128]), (mmr,))
            mmi = S.do("tensor", MM(mmB[0:64, 0:w], bst(f)[:, 64:128], dstf(f)[:, 0:w], f == 0, f == NFT - 1), (ldt_,))
        if ci >= 1:
            emit_wout(ci - 1)
        evrq = list(range(NFT)) if ci >= 1 else []
        ev_bu = S.do("vector", CP(bu[0:64, 0:w], mmA[0:64, 0:w]), (mmr, mmi, t_s0, t_s0b))
        ev_bu2 = S.do("vector", CP(bu[0:64, BW:BW + w], mmB[0:64, 0:w]), (ev_bu,))
        ps_done(mmA, ev_bu)
        ps_done(mmB, ev_bu2)
        S.minibar()
        off = 0
        prev = ev_bu2
        _ms = S.mark_small
        S.mark_small = True
        for cw in subchunks(w):
            aR, bR = 1 + off, 1 + off + cw
            aI, bI = SI + 1 + off, SI + 1 + off + cw
            buR = bu[0:64, off:off + cw]
            buI = bu[0:64, BW + off:BW + off + cw]
            t1 = S.do("vector", TT(pow_scr[0:64, 0:cw], buI, lamN[0:64, IM:IM + cw], ALU.mult), (prev, tk_ln) + sb16_cons)
            t2 = S.do("vector", TT(s_sb[0:64, aR:bR], buR, lamN[0:64, 0:cw], ALU.mult), (t1,))
            t3 = S.do("vector", TT(s_sb[0:64, aR:bR], s_sb[0:64, aR:bR], pow_scr[0:64, 0:cw], ALU.subtract), (t2,))
            t4 = S.do("vector", TT(pow_scr[0:64, 0:cw], buR, lamN[0:64, IM:IM + cw], ALU.mult), (t3,))
            t5 = S.do("vector", TT(s_sb[0:64, aI:bI], buI, lamN[0:64, 0:cw], ALU.mult), (t4,))
            t6 = S.do("vector", TT(s_sb[0:64, aI:bI], s_sb[0:64, aI:bI], pow_scr[0:64, 0:cw], ALU.add), (t5,))
            csR = S.do("vector", SCAN(s_sb[0:64, aR:bR], onesf[0:64, 0:cw], s_sb[0:64, aR:bR], 0.0), (t6, tk_of))
            csI = S.do("vector", SCAN(s_sb[0:64, aI:bI], onesf[0:64, 0:cw], s_sb[0:64, aI:bI], 0.0), (csR,))
            q1 = S.do("vector", TT(pow_scr[0:64, 0:cw], s_sb[0:64, aR:bR], lamP[0:64, IM:IM + cw], ALU.mult), (csI, tk_lp))
            q2 = S.do("vector", TT(pow_scr[0:64, 128:128 + cw], s_sb[0:64, aI:bI], lamP[0:64, IM:IM + cw], ALU.mult), (q1,))
            q3 = S.do("vector", TT(s_sb[0:64, aR:bR], s_sb[0:64, aR:bR], lamP[0:64, 0:cw], ALU.mult), (q2,))
            q4 = S.do("vector", TT(s_sb[0:64, aR:bR], s_sb[0:64, aR:bR], pow_scr[0:64, 128:128 + cw], ALU.subtract), (q3,))
            q5 = S.do("vector", TT(s_sb[0:64, aI:bI], s_sb[0:64, aI:bI], lamP[0:64, 0:cw], ALU.mult), (q4,))
            q6 = S.do("vector", TT(s_sb[0:64, aI:bI], s_sb[0:64, aI:bI], pow_scr[0:64, 0:cw], ALU.add), (q5,))
            k1 = S.do("vector", TS(pow_scr[0:64, 0:cw], lamP[0:64, IM + 1:IM + 1 + cw], s_sb[0:64, aI - 1:aI], ALU.mult), (q6,))
            k2 = S.do("vector", STT(s_sb[0:64, aR:bR], lamP[0:64, 1:1 + cw], s_sb[0:64, aR - 1:aR], s_sb[0:64, aR:bR], ALU.mult, ALU.add), (k1,))
            k3 = S.do("vector", TT(s_sb[0:64, aR:bR], s_sb[0:64, aR:bR], pow_scr[0:64, 0:cw], ALU.subtract), (k2,))
            k4 = S.do("vector", TS(pow_scr[0:64, 0:cw], lamP[0:64, IM + 1:IM + 1 + cw], s_sb[0:64, aR - 1:aR], ALU.mult), (k3,))
            k5 = S.do("vector", STT(s_sb[0:64, aI:bI], lamP[0:64, 1:1 + cw], s_sb[0:64, aI - 1:aI], s_sb[0:64, aI:bI], ALU.mult, ALU.add), (k4,))
            k6 = S.do("vector", TT(s_sb[0:64, aI:bI], s_sb[0:64, aI:bI], pow_scr[0:64, 0:cw], ALU.add), (k5,))
            prev = k6
            off += cw
            for _ in range(2):
                if evrq:
                    emit_evr_a(ci - 1, evrq.pop(0))
        S.mark_small = _ms
        while evrq:
            emit_evr_a(ci - 1, evrq.pop(0))
        if _DBG and ci == 1:
            S.dma("gpsimd", d_dbgs, s_sb[0:64, :], deps=(prev,))
            S.dma("gpsimd", d_dbgb, bu[0:64, :], deps=(prev,))
        S.minibar()
        t_st = S.do("vector", CP(s_sb[0:64, 0:1], s_sb[0:64, w:w + 1]), (prev,), small=True)
        t_st2 = S.do("vector", CP(s_sb[0:64, SI:SI + 1], s_sb[0:64, SI + w:SI + w + 1]), (prev,), small=True)
        t_sb16 = S.do("vector", CP(sb16[0:64, 0:w], s_sb[0:64, 1:1 + w]), (prev,) + sb16_cons)
        t_sb16b = S.do("vector", CP(sb16[0:64, BW:BW + w], s_sb[0:64, SI + 1:SI + 1 + w]), (t_sb16,))
        if ci + 1 < NCHa:
            lnA[ci + 1] = layernorm(CHUNKS[ci + 1][0], CHUNKS[ci + 1][1],
                                    xn_gA[(ci + 1) % 2], dstf=dsta(ci + 1))
        yv = yab(ci)
        m2l = None
        for f in range(NFT):
            dB_ = ps_use(mmB, (t_sb16b,))
            l1 = S.do("tensor", LDW(cctR[:, f * P:(f + 1) * P]), dB_ + (tk_cct,))
            m1 = S.do("tensor", MM(mmB[:, 0:w], cctR[:, f * P:(f + 1) * P], sb16[0:64, 0:w], True, False), (l1,))
            l2 = S.do("tensor", LDW(cctIN[:, f * P:(f + 1) * P]), (m1,))
            m1b = S.do("tensor", MM(mmB[:, 0:w], cctIN[:, f * P:(f + 1) * P], sb16[0:64, BW:BW + w], False, False), (l2,))
            l3 = S.do("tensor", LDW(addiag[:, f * P:(f + 1) * P]), (m1b, tk_ad[f]))
            m2 = S.do("tensor", MM(mmB[:, 0:w], addiag[:, f * P:(f + 1) * P], dstf(f)[:, 0:w], False, True), (l3, ln[f]))
            ev = S.do("scalar", ACT(yv(f)[:, 0:w], mmB[:, 0:w], AF.Copy), (m2,) + ya_gA[ci % 2])
            ps_done(mmB, ev)
            ya_tk[(ci, f)] = ev
            m2l = m2
        S.minibar()
        xn_gA[ci % 2] = (m2l,)
        sb16_cons = (m2l,)
        if _DBG and ci == 1:
            S.dma("gpsimd", d_dbgy, ya_t[:, :], deps=tuple(ya_tk[(ci, f)] for f in range(NFT)))
    emit_wout(NCHa - 1)
    for o in range(NFT):
        emit_evr_a(NCHa - 1, o)
    S.barrier()

    def dump_resid(j, deps):
        if not _DBG:
            return
        for i_ in range(NFT):
            S.dma("gpsimd", d_dbg[j][i_ * P:(i_ + 1) * P, :], x_sb[i_][:, :], deps=deps)

    dump_resid(0, S.barrier())

    def ffn_layer(chunks, write_out):
        out_toks = []
        guard = {0: (), 1: ()}
        NCF = len(chunks)
        dsts = [xn if i % 2 == 0 else ya for i in range(NCF)]
        lnt = {0: layernorm(chunks[0][0], chunks[0][1], (), dstf=dsts[0])}
        st_nx = {}
        for ci, (c0, w) in enumerate(chunks):
            ln = lnt[ci]
            dstf = dsts[ci]
            nxt = ci + 1 if ci + 1 < NCF else None
            last_mm = None
            for o in range(NFT):
                ps = yA if o % 2 == 0 else yB
                dps = ps_use(ps, tuple(ln))
                real_out = o < 4
                osub = o % 4
                ldt_ = None
                mlast = None
                for k in range(NFT):
                    if real_out:
                        wt = (fwr(k) if k < 4 else fwin(k - 4))
                    else:
                        wt = (fwi(k) if k < 4 else fwr(k - 4))
                    wsl = wt[:, osub * P:(osub + 1) * P]
                    ldt_ = S.do("tensor", LDW(wsl), dps + (tk_fn,) + ((ldt_,) if ldt_ else ()))
                    mlast = S.do("tensor", MM(ps[:, 0:w], wsl, dstf(k)[:, 0:w], k == 0, k == NFT - 1), (ldt_, ln[k]))
                evr = S.do("vector", STT(x_sb[o][:, c0:c0 + w], ps[:, 0:w], fb_sb[:, o:o + 1],
                                         x_sb[o][:, c0:c0 + w], ALU.add, ALU.add), (mlast, tk_fb))
                ps_done(ps, evr)
                last_mm = mlast
                if nxt is not None:
                    if o == 2:
                        st_nx[nxt] = ln_stats(chunks[nxt][0], chunks[nxt][1], guard[nxt % 2], dsts[nxt])
                    elif o == 5:
                        lnt[nxt] = ln_apply(chunks[nxt][0], chunks[nxt][1], st_nx[nxt], dsts[nxt])
                if write_out:
                    out_toks.append(S.dma("gpsimd", d_out[o * P:(o + 1) * P, c0 - W_WARM:c0 - W_WARM + w],
                                          x_sb[o][:, c0:c0 + w], deps=(evr,)))
            guard[ci % 2] = (last_mm,)
        return out_toks

    ffn_layer(CHUNKS, False)
    dump_resid(1, S.barrier())

    t_tl0 = S.do("vector", MSET(tails[:, :], 0.0))
    xie_g = [() for _ in range(2)]
    da_g = [None, None, None]
    g_g = [None, None, None]
    fdb = Rot(flat_db)
    NCH = len(CHUNKS)

    lnB = {}
    xc_toks = {}
    sz_toks = {}
    x1_toks = {}
    u17_toks = {}
    gate_toks = {}
    xproj_mm = {}
    opk_last = {}
    opgrp = {}
    inp_last = {}
    scan_end = {}
    BCx = {}
    PV = {"bo1": None}

    def start_inp_xi(cc, m):
        cw = CHUNKS[cc][1]
        psI = mmA if m % 2 == 0 else mmB
        dps = ps_use(psI, ())
        ldt_ = None
        mlast = None
        for k in range(NFT):
            j, buf, wdeps = st_inp.get()
            tw = S.wdma(buf[:, :], d_winp[k, m], wdeps + (CASTS[0],))
            ldt_ = S.do("tensor", LDW(buf[:, :]), dps + (tw,) + ((ldt_,) if ldt_ else ()))
            mlast = S.do("tensor", MM(psI[:, 0:cw], buf[:, :], xn(k)[:, 0:cw], k == 0, k == NFT - 1),
                         (ldt_, lnB[cc][k]))
            st_inp.used(j, mlast)
        inp_last[cc] = mlast
        return (m, mlast, psI)

    def emit_conv_part(cc, pend):
        m, mlast, psI, ev = pend
        cw = CHUNKS[cc][1]
        jx = m % 2
        xbuf = xi_ext[jx]
        dpc = ps_use(psI, (ev,))
        clast = None
        cl = None
        for tap in range(4):
            jc, cbuf, cdeps = st_cv.get()
            tw = S.wdma(cbuf[:, :], d_wcv[m, tap], cdeps + (CASTS[3],))
            cl = S.do("tensor", LDW(cbuf[:, :]), dpc + (tw,) + ((cl,) if cl else ()))
            clast = S.do("tensor", MM(psI[:, 0:cw], cbuf[:, :], xbuf[:, tap:tap + cw], tap == 0, tap == 3),
                         (cl, ev))
            st_cv.used(jc, clast)
        xie_g[jx] = (clast,)
        return clast

    def emit_evc_z(cc, pend, clast):
        m, mlast, psI, ev = pend
        cw = CHUNKS[cc][1]
        if cc == 0:
            prv_rd = ()
        else:
            prv_rd = (u17_toks.get((cc - 1, m), x1_toks[(cc - 1, m)]), xproj_mm[cc - 1])
        evc = S.do("scalar", lambda e, m_=m, w_=cw, p_=psI: e.activation(
            xc(m_)[:, 0:w_], p_[:, 0:w_], AF.Silu, bias=cb_sb[:, m_:m_ + 1]),
            (clast, tk_cb) + prv_rd)
        ps_done(psI, evc)
        xc_toks[(cc, m)] = evc
        dps = ps_use(psI, ())
        ldt_ = None
        zlast = None
        for k in range(NFT):
            j, buf, wdeps = st_inp.get()
            tw = S.wdma(buf[:, :], d_winp[k, 16 + m], wdeps + (CASTS[0],))
            ldt_ = S.do("tensor", LDW(buf[:, :]), dps + (tw,) + ((ldt_,) if ldt_ else ()))
            zlast = S.do("tensor", MM(psI[:, 0:cw], buf[:, :], xn(k)[:, 0:cw], k == 0, k == NFT - 1),
                         (ldt_, lnB[cc][k]))
            st_inp.used(j, zlast)
        inp_last[cc] = zlast
        return zlast

    def emit_evz(cc, pend, zlast):
        m, mlast, psI, ev = pend
        cw = CHUNKS[cc][1]
        pz = (gate_toks[(cc - 1, m)],) if (cc - 1, m) in gate_toks else ()
        evz = S.do("scalar", lambda e, m_=m, w_=cw, p_=psI: e.activation(
            sz(m_)[:, 0:w_], p_[:, 0:w_], AF.Silu), (zlast,) + pz)
        ps_done(psI, evz)
        sz_toks[(cc, m)] = evz

    def emit_xi_evac(cc, pend0):
        m, mlast, psI = pend0
        cw = CHUNKS[cc][1]
        jx = m % 2
        xbuf = xi_ext[jx]
        tl = S.do("scalar", ACT(xbuf[:, 0:3], tails[:, m * 3:(m + 1) * 3], AF.Copy),
                  (t_tl0, mlast) + xie_g[jx], small=True)
        ev = S.do("scalar", ACT(xbuf[:, 3:3 + cw], psI[:, 0:cw], AF.Copy), (tl,))
        ps_done(psI, ev)
        S.do("scalar", ACT(tails[:, m * 3:(m + 1) * 3], xbuf[:, cw:cw + 3], AF.Copy),
             (ev,), small=True)
        return (m, mlast, psI, ev)

    def finish_inp_pair(cc, pend4, with_z=True):
        clast = emit_conv_part(cc, pend4)
        zlast = emit_evc_z(cc, pend4, clast) if with_z else None
        if not with_z:
            m, mlast, psI, ev = pend4
            cw = CHUNKS[cc][1]
            evc = S.do("scalar", lambda e, m_=m, w_=cw, p_=psI: e.activation(
                xc(m_)[:, 0:w_], p_[:, 0:w_], AF.Silu, bias=cb_sb[:, m_:m_ + 1]),
                (clast, tk_cb))
            ps_done(psI, evc)
            xc_toks[(cc, m)] = evc
            return
        emit_evz(cc, pend4, zlast)

    def emit_xproj(cc):
        cw = CHUNKS[cc][1]
        dps = ps_use(mmA, ())
        ldt_ = None
        mlast = None
        for k in range(NDT):
            ldt_ = S.do("tensor", LDW(xpw(k)), dps + (tk_xp,) + ((ldt_,) if ldt_ else ()))
            mlast = S.do("tensor", MM(mmA[0:33, 0:cw], xpw(k), xc(k)[:, 0:cw], k == 0, k == NDT - 1),
                         (ldt_, xc_toks[(cc, k)]))
        xproj_mm[cc] = mlast
        ev_r16 = S.do("vector", CP(rows16[:, 0:cw], mmA[0:33, 0:cw]),
                      (mlast,) + ((PV["bo1"],) if PV["bo1"] else ()))
        ps_done(mmA, ev_r16)
        PV["bo1"] = S.dma("gpsimd", d_rows16[0:33, 0:cw], rows16[:, 0:cw], deps=(ev_r16,))
        return PV["bo1"]

    def emit_bcast(cc, bo1):
        cw = CHUNKS[cc][1]
        bcc = (scan_end[cc - 1],) if cc >= 1 else ()
        dps = ps_use(mmB, bcc)
        lb = S.do("tensor", LDW(ones_c[0:1, :]), dps + (tk_ones,))
        n_rows = 17 if cc == 0 else 33
        bc_evs = []
        mprev = None
        for ri in range(n_rows):
            jf, fbuf, fdeps = fdb.get()
            td = S.dma("gpsimd", fbuf[0:1, 0:cw], d_rows16[ri:ri + 1, 0:cw], deps=(bo1,) + fdeps)
            mprev = S.do("tensor", MM(mmB[:, 0:cw], ones_c[0:1, :], fbuf[0:1, 0:cw], True, True),
                         ((lb,) if mprev is None else (mprev,)) + (td,) + ((bc_evs[-1],) if bc_evs else ()))
            fdb.used(jf, mprev)
            if ri == 0:
                dst = dtraw_bc[:, 0:cw]
            elif ri <= 16:
                dst = bbc(ri - 1)[:, 0:cw]
            else:
                dst = cbc(ri - 17)[:, 0:cw]
            bc_evs.append(S.do("scalar", ACT(dst, mmB[:, 0:cw], AF.Copy), (mprev,) + bcc))
        ps_done(mmB, bc_evs[-1])
        BCx[cc] = bc_evs[-1]

    def emit_out_proj(cc):
        c0p, cw = CHUNKS[cc]

        def one_group(o):
            psO = psU if o % 2 == 0 else psV
            dps = ps_use(psO, ())
            ldt_ = None
            mlast = None
            for k in range(NDT):
                j, buf, wdeps = st_op.get()
                tw = S.wdma(buf[:, :], d_wop[k, o], wdeps + (CASTS[1],))
                ldt_ = S.do("tensor", LDW(buf[:, :]), dps + (tw,) + ((ldt_,) if ldt_ else ()))
                mlast = S.do("tensor", MM(psO[:, 0:cw], buf[:, :], y2(k)[:, 0:cw], k == 0, k == NDT - 1),
                             (ldt_, gate_toks[(cc, k)]))
                st_op.used(j, mlast)
                if o == NFT - 1:
                    opk_last[(cc, k)] = mlast
            opgrp[(cc, o)] = mlast

        def one_evr(o):
            psO = psU if o % 2 == 0 else psV
            evr = S.do("vector", TT(x_sb[o][:, c0p:c0p + cw], psO[:, 0:cw], x_sb[o][:, c0p:c0p + cw], ALU.add),
                       (opgrp[(cc, o)],))
            ps_done(psO, evr)

        one_group(0)
        one_group(1)
        for o in range(2, NFT):
            one_evr(o - 2)
            one_group(o)
        one_evr(NFT - 2)
        one_evr(NFT - 1)

    def emit_scan(c):
        c0, cw = CHUNKS[c]
        warm = (c == 0)
        nxt = c + 1 if c + 1 < NCH else None
        BC = BCx[c]
        if c - 1 >= 1:
            emit_out_proj(c - 1)
        dt_guard = [None, None]
        d4_prev = [None]
        scan_last_v = None
        pend = [None]
        u_g = [None, None]
        uu_last = [None, None]
        x1_g = [None, None]
        chain = {}

        def emit_dt_chain(dtile):
            idt = dtile % 2
            d1 = S.do("vector", TS2(dt_b2[idt][:, 0:cw], dtraw_bc[:, 0:cw], dtw_sb[:, dtile:dtile + 1],
                                    dtb_sb[:, dtile:dtile + 1], ALU.mult, ALU.add),
                      (BC, tk_dtw, tk_dtb) + ((dt_guard[idt],) if dt_guard[idt] else ())
                      + ((x1_g[idt],) if x1_g[idt] else ()))
            d2 = S.do("scalar", lambda e, i=idt, w_=cw: e.activation(
                dtl_t[:, 0:w_], dt_b2[i][:, 0:w_], AF.Exp, scale=consts[:, C_NEG1:C_NEG1 + 1]),
                (d1,) + ((d4_prev[0],) if d4_prev[0] else ()))
            d3 = S.do("scalar", lambda e, w_=cw: e.activation(
                xsq[:, 0:w_], dtl_t[:, 0:w_], AF.Ln, bias=1.0),
                (d2,) + ((ln_guard["xsq"],) if ln_guard["xsq"] else ()) + ((d4_prev[0],) if d4_prev[0] else ()))
            chain[dtile] = (d1, d2, d3)

        emit_dt_chain(0)
        for dtile in range(NDT):
            idt = dtile % 2
            d1, d2, d3 = chain[dtile]
            d4 = S.do("gpsimd", TT(dt_b2[idt][:, 0:cw], dt_b2[idt][:, 0:cw], xsq[:, 0:cw], ALU.add), (d3, d1))
            d4_prev[0] = d4
            clast_p = emit_conv_part(nxt, pend[0]) if (nxt is not None and pend[0] is not None) else None
            p0_new = start_inp_xi(nxt, dtile) if nxt is not None else None
            x1 = S.do("gpsimd", TT(dtx_b2[idt][:, 0:cw], dt_b2[idt][:, 0:cw], xc(dtile)[:, 0:cw], ALU.mult),
                      (d4, xc_toks[(c, dtile)]) + ((uu_last[idt],) if uu_last[idt] else ()))
            x1_toks[(c, dtile)] = x1
            x1_g[idt] = x1
            if warm:
                x1 = S.do("gpsimd", TS(dtx_b2[idt][:, 0:cw], dtx_b2[idt][:, 0:cw], wm_col[:, 0:1], ALU.mult),
                          (x1, tk_wm))
                x1_g[idt] = x1
                psy = None
                dps = ()
            else:
                psy = yA if dtile % 2 == 0 else yB
                dps = ps_use(psy, ())
                t_idl = S.do("tensor", LDW(ident[:, :]), (t_id, BC))
            nmm = 0
            mmtok = None
            zlast_p = None
            pend_new = None
            for s in range(NS):
                if s == 2 and clast_p is not None:
                    zlast_p = emit_evc_z(nxt, pend[0], clast_p)
                if s == 8 and p0_new is not None:
                    pend_new = emit_xi_evac(nxt, p0_new)
                js = s % 3
                da = S.do("scalar", lambda e, i=idt, s_=s, d_=dtile, buf=da_b[js], w_=cw: e.activation(
                    buf[:, 0:w_], dt_b2[i][:, 0:w_], AF.Exp,
                    scale=a_sb[:, d_ * NS + s_:d_ * NS + s_ + 1]),
                    (d4, tk_a) + ((da_g[js],) if da_g[js] else ()))
                ju = s % 2
                uu = S.do("vector", TT(u_b[ju][:, 0:cw], dtx_b2[idt][:, 0:cw], bbc(s)[:, 0:cw], ALU.mult),
                          (x1, BC) + ((u_g[ju],) if u_g[ju] else ()))
                uu_last[idt] = uu
                hs = hpool[:, (s % 4) * BW:(s % 4) * BW + cw]
                init = 0.0 if c == 0 else states[:, dtile * NS + s:dtile * NS + s + 1]
                sc = S.do("vector", SCAN(hs, da_b[js][:, 0:cw], u_b[ju][:, 0:cw], init), (da, uu))
                da_g[js] = sc
                u_g[ju] = sc
                dt_guard[idt] = da
                if not warm:
                    gg = S.do("vector", TT(g_b[js][:, 0:cw], hs, cbc(s)[:, 0:cw], ALU.mult),
                              (sc,) + ((g_g[js],) if g_g[js] else ()))
                    if nmm == 0:
                        mmtok = S.do("tensor", MM(psy[:, 0:cw], ident[:, :], g_b[js][:, 0:cw], True, False),
                                     dps + (gg, t_idl))
                    else:
                        mmtok = S.do("tensor", MM(psy[:, 0:cw], ident[:, :], g_b[js][:, 0:cw], False, False),
                                     (gg, mmtok))
                    g_g[js] = mmtok
                    nmm += 1
                if s % 4 == 3:
                    q = s // 4
                    gath = S.do("vector", CP(states[:, dtile * NS + q * 4:dtile * NS + q * 4 + 4],
                                             hpool[:, cw - 1:cw - 1 + 3 * BW + 1:BW]), (sc,), small=True)
                    scan_last_v = gath
            if not warm:
                js17 = NS % 3
                u17 = S.do("vector", TS(g_b[js17][:, 0:cw], xc(dtile)[:, 0:cw], mD_sb[:, dtile:dtile + 1], ALU.mult),
                           (xc_toks[(c, dtile)], tk_mD) + ((g_g[js17],) if g_g[js17] else ()))
                u17_toks[(c, dtile)] = u17
                mmtok = S.do("tensor", MM(psy[:, 0:cw], ident[:, :], g_b[js17][:, 0:cw], False, True), (u17, mmtok))
                g_g[js17] = mmtok
                pgd = opk_last.get((c - 1, dtile))
                ev = S.do("vector", TT(y2(dtile)[:, 0:cw], psy[:, 0:cw], sz(dtile)[:, 0:cw], ALU.mult),
                          (mmtok, sz_toks[(c, dtile)]) + ((pgd,) if pgd else ()))
                ps_done(psy, ev)
                gate_toks[(c, dtile)] = ev
                scan_last_v = ev
            if zlast_p is not None:
                emit_evz(nxt, pend[0], zlast_p)
            if dtile + 1 < NDT:
                emit_dt_chain(dtile + 1)
            if pend_new is not None:
                pend[0] = pend_new
        if nxt is not None and pend[0] is not None:
            finish_inp_pair(nxt, pend[0])
        scan_end[c] = scan_last_v

    c0_, w_ = CHUNKS[0]
    lnB[0] = layernorm(c0_, w_, ())
    for m in range(16):
        p0g = start_inp_xi(0, m)
        p4g = emit_xi_evac(0, p0g)
        finish_inp_pair(0, p4g, with_z=False)
    emit_bcast(0, emit_xproj(0))
    lnB[1] = layernorm(CHUNKS[1][0], CHUNKS[1][1], (inp_last[0],))

    for c in range(NCH):
        emit_scan(c)
        if c + 1 < NCH:
            emit_bcast(c + 1, emit_xproj(c + 1))
            if c + 2 < NCH:
                lnB[c + 2] = layernorm(CHUNKS[c + 2][0], CHUNKS[c + 2][1], (inp_last[c + 1],))

    emit_out_proj(NCH - 1)
    if _DBG:
        dep = (opgrp[(NCH - 1, NFT - 1)], scan_end[NCH - 1])
        S.dma("gpsimd", d_dbgxc, xc_t[:, :], deps=dep)
        S.dma("gpsimd", d_dbgsz, sz_t[:, :], deps=dep)
        S.dma("gpsimd", d_dbgy2, y2_t[:, :], deps=dep)
        S.dma("gpsimd", d_dbgrw, rows16[:, :], deps=dep)
        S.dma("gpsimd", d_dbgst, states[:, :], deps=dep)
        S.dma("gpsimd", d_dbga, a_sb[:, :], deps=dep)
        S.dma("gpsimd", d_dbgdt, dtw_sb[:, :], deps=dep)
    dump_resid(2, S.barrier())

    out_toks = ffn_layer(CHUNKS[1:], True)
    S.do("gpsimd", lambda e: e.nop(), tuple(out_toks))

    with nc.Block() as blk:
        S.replay(blk)
    return nc


def kernel(**inputs):
    i = inputs
    x = np.asarray(i["L0_attn_a_x"])[0]
    xT = np.ascontiguousarray(x.T)

    def shard_x(k):
        lo = k * T_MAIN - W_WARM
        if lo < 0:
            pad = np.zeros((D_MODEL, -lo), dtype=np.float32)
            return np.ascontiguousarray(np.concatenate([pad, xT[:, 0:(k + 1) * T_MAIN]], axis=1))
        return np.ascontiguousarray(xT[:, lo:(k + 1) * T_MAIN])

    def cvt(a):
        return np.ascontiguousarray(np.asarray(a), dtype=np.float32)

    def colmaj(v, ntiles):
        return np.ascontiguousarray(cvt(v).reshape(ntiles, P).T)

    common = {
        "log_lam_real": cvt(i["L0_attn_a_log_Lam_real"]).reshape(1, 64),
        "lam_imag": cvt(i["L0_attn_a_Lam_imag"]).reshape(1, 64),
        "b_realT": np.ascontiguousarray(cvt(i["L0_attn_a_B_real"]).T),
        "b_imagT": np.ascontiguousarray(cvt(i["L0_attn_a_B_imag"]).T),
        "c_realT": np.ascontiguousarray(cvt(i["L0_attn_a_C_real"]).T),
        "c_imagT": np.ascontiguousarray(cvt(i["L0_attn_a_C_imag"]).T),
        "a_D": cvt(i["L0_attn_a_D"]).reshape(1, D_MODEL),
        "log_dt": cvt(i["L0_attn_a_log_dt"]).reshape(1, 1),
        "w_outT": np.ascontiguousarray(cvt(i["L0_attn_a_W_out"]).T),
        "b_out": colmaj(i["L0_attn_a_b_out"], NFT),
        "in_proj_w": cvt(i["L0_attn_b_in_proj_w"]),
        "conv_w": np.ascontiguousarray(cvt(i["L0_attn_b_conv_w"]).reshape(NDT, P, 4).transpose(1, 0, 2).reshape(P, 64)),
        "conv_b": colmaj(i["L0_attn_b_conv_b"], NDT),
        "x_proj_w": np.ascontiguousarray(cvt(i["L0_attn_b_x_proj_w"]).reshape(NDT, P, 33).transpose(1, 0, 2).reshape(P, NDT * 33)),
        "dt_w": colmaj(i["L0_attn_b_dt_w"], NDT),
        "dt_b": colmaj(i["L0_attn_b_dt_b"], NDT),
        "a_log": np.ascontiguousarray(cvt(i["L0_attn_b_A_log"]).reshape(NDT, P, NS).transpose(1, 0, 2).reshape(P, NDT * NS)),
        "m_D": colmaj(i["L0_attn_b_D"], NDT),
        "out_proj_w": cvt(i["L0_attn_b_out_proj_w"]),
        "g1r": np.ascontiguousarray(cvt(i["L0_ffn_G1_R"]).reshape(32, 128)),
        "g2r": np.ascontiguousarray(cvt(i["L0_ffn_G2_R"]).reshape(4, 16, 16).transpose(0, 2, 1).reshape(64, 16)),
        "g1i": np.ascontiguousarray(cvt(i["L0_ffn_G1_I"]).reshape(32, 128)),
        "g2i": np.ascontiguousarray(cvt(i["L0_ffn_G2_I"]).reshape(4, 16, 16).transpose(0, 2, 1).reshape(64, 16)),
        "ffn_bias": np.ascontiguousarray(
            np.concatenate([cvt(i["L0_ffn_bias_real"]), cvt(i["L0_ffn_bias_imag"])]).reshape(NFT, P).T),
    }
    in_maps = []
    for k in range(N_CORES):
        m = dict(common)
        m["x_in"] = shard_x(k)
        m["warm"] = np.asarray([[0.0 if k == 0 else 1.0]], dtype=np.float32)
        in_maps.append(m)

    nc = build_nc()
    res = run_bass_kernel_spmd(nc, in_maps, core_ids=list(range(N_CORES)))
    parts = [res.results[k]["out"] for k in range(N_CORES)]
    full = np.concatenate(parts, axis=1)
    return np.ascontiguousarray(full.T).reshape(1, 16384, D_MODEL)


if __name__ == "__main__":
    _nc = build_nc()
    print("build ok; instrs:", {e: len(_nc2) for e, _nc2 in []} if False else "done")
